# version 01
# baseline (speedup 1.0000x reference)
import contextlib
import numpy as np
import ml_dtypes
import concourse.bass as bass
import concourse.mybir as mybir
from concourse.bass_utils import run_bass_kernel_spmd

F32 = mybir.dt.float32
BF16 = mybir.dt.bfloat16
I32 = mybir.dt.int32
AF = mybir.ActivationFunctionType
ALU = mybir.AluOpType
AX = mybir.AxisListType

NCORES = 8
T = 2048
SEQ_PER_CORE = 2
NT = T * SEQ_PER_CORE
D = 1024
NTILE = NT // 128
DCH = D // 128

ENGINES = ("tensor", "vector", "scalar", "gpsimd", "sync")
DMA_SEM = {"sync": "sd", "scalar": "ad", "gpsimd": "gd", "vector": "vd"}


class Sched:

    def __init__(self, nc):
        self.nc = nc
        self.sem_names = ["pe", "dv", "ac", "po"]
        self.sems = {}
        self.cnt = {n: 0 for n in self.sem_names}
        self.ops = {e: [] for e in ENGINES}
        self.chan_eng = {}
        self.ctx = None

    def make_sems(self, ctx):
        self.ctx = ctx
        for n in self.sem_names:
            self.sems[n] = ctx.enter_context(self.nc.semaphore(n))

    def _chan(self, name):
        if name not in self.sems:
            self.sems[name] = self.ctx.enter_context(self.nc.semaphore(name))
            self.cnt[name] = 0
        return name

    def op(self, eng, fn, waits=(), inc=None, n=1):
        for s, v in waits:
            assert v <= self.cnt[s], f"wait on future: {s} {v} > {self.cnt[s]}"
        out = None
        if inc is not None:
            self.cnt[inc] += n
            out = self.cnt[inc]
        self.ops[eng].append((fn, [(s, v) for s, v in waits if v > 0], inc, n))
        return out

    def dma(self, qeng, dst, src, waits=(), ch="misc"):
        ch = self._chan("D_" + ch)
        self.chan_eng[ch] = qeng
        v = self.op(qeng, lambda e, d=dst, s=src: e.dma_start(d, s),
                    waits, inc=ch, n=16)
        return (ch, v)

    def v(self, sem):
        return self.cnt[sem]

    def flush(self, drain_dma=True):
        if drain_dma:
            per_eng = {}
            for ch, eng in self.chan_eng.items():
                if self.cnt.get(ch, 0) > 0:
                    per_eng.setdefault(eng, []).append((ch, self.cnt[ch]))
            for eng, waits in per_eng.items():
                self.op(eng, lambda e: e.nop(), waits=waits)
            self.chan_eng = {}
        ops = self.ops
        self.ops = {e: [] for e in ENGINES}
        if not any(ops.values()):
            return
        sems = self.sems
        with self.nc.Block() as blk:
            for eng_name in ENGINES:
                lst = ops[eng_name]
                if not lst:
                    continue

                def section(eng, lst=lst):
                    for fn, waits, inc, n in lst:
                        for s, v in waits:
                            eng.wait_ge(sems[s], v)
                        ins = fn(eng)
                        if inc is not None:
                            ins.then_inc(sems[inc], n)
                getattr(blk, eng_name)(section)


def host_prep(inputs, need):
    f32 = np.float32

    def T_(a):
        return np.ascontiguousarray(np.asarray(a, f32).T)

    d = {}
    d["c_ident16"] = np.eye(128, dtype=ml_dtypes.bfloat16)
    if "L0f" in need:
        d["w_l0_g"] = np.ascontiguousarray(np.asarray(inputs["L0_ffn_w_gate"], f32))
        d["w_l0_u"] = np.ascontiguousarray(np.asarray(inputs["L0_ffn_w_up"], f32))
        d["w_l0_d"] = np.ascontiguousarray(np.asarray(inputs["L0_ffn_w_down"], f32))
    if "L0a" in need or "L1a" in need:
        d["c_causal_mask"] = np.triu(np.full((128, 128), -1e30, f32), 1)
        d["c_ones_f32"] = np.ones((1, 128), f32)
    if "L0a" in need:
        d["w_a0_qkv"] = T_(inputs["L0_attn_wqkv"])
        d["w_a0_wo"] = T_(inputs["L0_attn_wo"])
        t = np.arange(2048, dtype=np.float64)
        inv = 1.0 / (10000.0 ** (np.arange(0, 64, 2, dtype=np.float64) / 64))
        fr = t[:, None] * inv[None, :]
        cos = np.cos(fr).astype(f32)
        sin = np.sin(fr).astype(f32)
        d["c_rope_cos"] = np.ascontiguousarray(
            cos.reshape(16, 128, 32).transpose(1, 0, 2).reshape(128, 512))
        d["c_rope_sin"] = np.ascontiguousarray(
            sin.reshape(16, 128, 32).transpose(1, 0, 2).reshape(128, 512))
        qn = np.tile(np.asarray(inputs["L0_attn_q_norm_w"], f32), 16)
        kn = np.tile(np.asarray(inputs["L0_attn_k_norm_w"], f32), 16)
        d["c_qknw"] = np.ascontiguousarray(np.broadcast_to(
            np.concatenate([qn, kn]), (128, 2048)))
    if "L1a" in need:
        d["w_a1_qkv"] = np.ascontiguousarray(np.asarray(inputs["L1_attn_w_qkv"], f32))
        d["w_a1_wo"] = np.ascontiguousarray(np.asarray(inputs["L1_attn_w_out"], f32))
        from math import log2
        start = 2 ** (-2 ** (-(log2(16) - 3)))
        slopes = np.array([start * start ** i for i in range(16)], f32)
        k = np.arange(2048, dtype=f32)
        d["c_alibi"] = np.ascontiguousarray(8.0 * slopes[:, None] * k[None, :])
        d["c_l1_bob"] = np.ascontiguousarray(np.broadcast_to(
            np.asarray(inputs["L1_attn_b_out"], f32), (128, 1024)))
    if "L3a" in need:
        d["c_ones_f32"] = np.ones((1, 128), f32)
        d["w_a3_w1"] = np.ascontiguousarray(np.asarray(inputs["L3_attn_W1"], f32))
        d["w_a3_w2"] = np.ascontiguousarray(np.asarray(inputs["L3_attn_W2"], f32))
        d["c_a3_b1"] = np.asarray(inputs["L3_attn_b1"], f32).reshape(1, 1024)
        d["c_a3_b2"] = np.asarray(inputs["L3_attn_b2"], f32).reshape(1, 1024)
        pos = np.arange(2048, dtype=np.float64)[:, None]
        div = np.exp(np.arange(0, 64, 2, dtype=np.float64) * (-np.log(10000.0) / 64))
        ang = pos * div
        pe_ = np.zeros((2048, 64), np.float32)
        pe_[:, 0::2] = np.sin(ang)
        pe_[:, 1::2] = np.cos(ang)
        d["c_posT"] = np.ascontiguousarray(pe_.T)
        N_ = 4096
        k_ = np.arange(2176, dtype=np.float64)
        t_ = np.arange(2048, dtype=np.float64)
        th = 2.0 * np.pi * np.outer(t_, k_) / N_
        fre = np.cos(th); fim = -np.sin(th)
        fre[:, 2049:] = 0.0; fim[:, 2049:] = 0.0
        import ml_dtypes as _mld
        d["t_fre"] = fre.astype(_mld.bfloat16)
        d["t_fim"] = fim.astype(_mld.bfloat16)
        alp = np.full(2176, 2.0); alp[0] = 1.0; alp[2048] = 1.0; alp[2049:] = 0.0
        thg = 2.0 * np.pi * np.outer(k_, t_) / N_
        gre = (alp[:, None] * np.cos(thg)) / N_
        gim = (-alp[:, None] * np.sin(thg)) / N_
        d["t_gre"] = gre.astype(_mld.bfloat16)
        d["t_gim"] = gim.astype(_mld.bfloat16)
    if "L2a" in need:
        d["c_l2_mask"] = 8.0 * np.triu(np.ones((128, 128), f32), 1)
        d["c_ones_f32"] = np.ones((1, 128), f32)
        d["w_a2_plT"] = T_(inputs["L2_attn_proj_local"])
        d["w_a2_pgT"] = T_(inputs["L2_attn_proj_global"])
        d["w_a2_wlT"] = T_(inputs["L2_attn_Wl_in"])
        d["w_a2_wgT"] = T_(inputs["L2_attn_Wg_in"])
        d["w_a2_wlo"] = T_(inputs["L2_attn_Wl_out"])
        d["w_a2_wgo"] = T_(inputs["L2_attn_Wg_out"])
        d["c_l2_blin"] = np.asarray(inputs["L2_attn_bl_in"], f32).reshape(1, 3072)
        d["c_l2_bgin"] = np.asarray(inputs["L2_attn_bg_in"], f32).reshape(1, 3072)
        d["c_l2_blo"] = np.ascontiguousarray(np.broadcast_to(
            np.asarray(inputs["L2_attn_bl_out"], f32), (128, 1024)))
        d["c_l2_bgo"] = np.ascontiguousarray(np.broadcast_to(
            np.asarray(inputs["L2_attn_bg_out"], f32), (128, 1024)))
        d["c_l2a_gam"] = np.ascontiguousarray(np.broadcast_to(
            np.asarray(inputs["L2_attn_ln_gamma"], f32), (128, 1024)))
        d["c_l2a_bet"] = np.ascontiguousarray(np.broadcast_to(
            np.asarray(inputs["L2_attn_ln_beta"], f32), (128, 1024)))
        d["L2_attn_fusion_w"] = np.asarray(inputs["L2_attn_fusion_w"],
                                           f32).reshape(1, 2)
    if "L3f" in need:
        q = np.asarray(inputs["L3_ffn_qweights"], np.int32)
        d["w_l3_q"] = np.ascontiguousarray(q.transpose(0, 2, 1)).reshape(6144, 1024)
        sc = np.asarray(inputs["L3_ffn_scales"], f32)
        scr = np.repeat(sc, 16, axis=2)
        d["w_l3_sc"] = np.ascontiguousarray(scr.transpose(0, 2, 1)).reshape(6144, 1024)
        la = np.asarray(inputs["L3_ffn_lora_a"], f32)
        laT = la.transpose(0, 2, 1)
        d["w_l3_la"] = np.ascontiguousarray(
            laT.reshape(6, 8, 128, 32).transpose(0, 2, 1, 3).reshape(6 * 128, 256))
        lb = np.asarray(inputs["L3_ffn_lora_b"], f32)
        d["w_l3_lb"] = np.ascontiguousarray(lb.transpose(0, 2, 1)).reshape(6 * 32, 1024)
        d["w_l3_bias"] = np.ascontiguousarray(np.asarray(inputs["L3_ffn_biases"], f32))
        d["c_l3_gam"] = np.ascontiguousarray(np.broadcast_to(
            np.asarray(inputs["L3_ffn_ln_gamma"], f32)[:, None, :],
            (3, 128, 1024))).reshape(384, 1024)
        d["c_l3_bet"] = np.ascontiguousarray(np.broadcast_to(
            np.asarray(inputs["L3_ffn_ln_beta"], f32)[:, None, :],
            (3, 128, 1024))).reshape(384, 1024)
        d["c_ones_f32"] = np.ones((1, 128), f32)
    if "L2f" in need:
        gu = np.asarray(inputs["L2_ffn_gate_up_weight"], f32)
        guT = np.ascontiguousarray(gu.transpose(0, 2, 1))
        d["w_l2_g"] = np.ascontiguousarray(guT[:, :, :2816]).reshape(8 * 1024, 2816)
        d["w_l2_u"] = np.ascontiguousarray(guT[:, :, 2816:]).reshape(8 * 1024, 2816)
        dw = np.asarray(inputs["L2_ffn_down_weight"], f32)
        d["w_l2_d"] = np.ascontiguousarray(dw.transpose(0, 2, 1)).reshape(8 * 2816, 1024)
        rw = np.asarray(inputs["L2_ffn_router_weight"], f32).T
        d["w_l2_router"] = np.ascontiguousarray(
            rw.reshape(8, 128, 8).transpose(1, 0, 2).reshape(128, 64))
        d["c_l2_normw"] = np.ascontiguousarray(np.broadcast_to(
            np.asarray(inputs["L2_ffn_norm_weight"], f32), (128, 1024)))
        d["s_l2_alpha"] = np.asarray(inputs["L2_ffn_residual_alpha"],
                                     f32).reshape(1, 1)
        d["c_ones_f32"] = np.ones((1, 128), f32)
    if "L1f" in need:
        d["w_l1_g"] = np.asarray(inputs["L1_ffn_Wg"], f32).reshape(8 * 1024, 4096)
        d["w_l1_u"] = np.asarray(inputs["L1_ffn_Wu"], f32).reshape(8 * 1024, 4096)
        d["w_l1_d"] = np.asarray(inputs["L1_ffn_Wd"], f32).reshape(8 * 4096, 1024)
    return d


def shard_x(inputs):
    x = np.asarray(inputs["L0_attn_x"], np.float32)
    return [np.ascontiguousarray(x[2 * c:2 * c + 2].reshape(NT, D))
            for c in range(NCORES)]


class Stage:
    def __init__(self, subblocks):
        self.subblocks = subblocks
        self.nc = bass.Bass("TRN2", target_bir_lowering=False)
        self.S = Sched(self.nc)
        self.dram = {}
        self.ext_inputs = set()
        self.sb = {}
        self._names = {}
        self.x_ready = []

    def din(self, name, shape, dtype=F32):
        if name not in self.dram:
            self.dram[name] = self.nc.dram_tensor(name, list(shape), dtype,
                                                  kind="ExternalInput")
            self.ext_inputs.add(name)
        return self.dram[name]

    def dint(self, name, shape, dtype=F32):
        if name not in self.dram:
            self.dram[name] = self.nc.dram_tensor(name, list(shape), dtype,
                                                  kind="Internal")
        return self.dram[name]

    def dout(self, name, shape, dtype=F32):
        if name not in self.dram:
            self.dram[name] = self.nc.dram_tensor(name, list(shape), dtype,
                                                  kind="ExternalOutput")
        return self.dram[name]

    def _uniq(self, name):
        n = self._names.get(name, 0)
        self._names[name] = n + 1
        return name if n == 0 else f"{name}_{n}"

    def alloc(self, ctx, name, shape, dtype):
        t = ctx.enter_context(self.nc.sbuf_tensor(self._uniq(name),
                                                  list(shape), dtype))
        self.sb[name] = t
        return t

    def palloc(self, ctx, name, shape, dtype=F32):
        t = ctx.enter_context(self.nc.psum_tensor(self._uniq(name),
                                                  list(shape), dtype))
        self.sb[name] = t
        return t


def emit_ln_to_xT(st, x_src, mode="plain", wb=None):
    S, sb = st.S, st.sb
    xT, ident = sb["xT"], sb["ident"]
    with contextlib.ExitStack() as ctx:
        xt = st.alloc(ctx, "ln_xt", [128, 2 * 1024], F32)
        ln16 = st.alloc(ctx, "ln_16", [128, 2 * 1024], BF16)
        stat = st.alloc(ctx, "ln_stat", [128, 12], F32)
        mv = st.alloc(ctx, "ln_mv", [128, 2], F32)
        vec = st.alloc(ctx, "ln_vec", [128, 8], F32)
        pst = st.palloc(ctx, "ln_ptr", [128, 2048], BF16)
        if mode == "l0":
            lnf = st.alloc(ctx, "ln_f32", [128, 2 * 1024], F32)
            qs = alloc_quant(st, ctx, "lnq")
        if mode == "l3":
            lnf = st.alloc(ctx, "ln_f32", [128, 2 * 1024], F32)
        dump16 = (mode == "dump")

        stats_dv, ln_ac, tr_pe, copy_dv = [], [], [], []
        for i in range(NTILE):
            b = i % 2
            xt_b = xt[:, b * 1024:(b + 1) * 1024]
            ln_b = ln16[:, b * 1024:(b + 1) * 1024]
            waits = list(st.x_ready)
            if i >= 2:
                waits += [("dv", stats_dv[i - 2])]
                if mode == "l2":
                    waits.append((f"D_lnwb{i % 2}", S.v(f"D_lnwb{i % 2}")))
                elif mode == "l0":
                    waits.append(("ac", S.v("ac")))
                elif mode == "l3":
                    waits.append((f"D_lnwb{i % 2}", S.v(f"D_lnwb{i % 2}")))
                else:
                    waits.append(("ac", ln_ac[i - 2]))
            ld = S.dma("sync", xt_b, x_src[i * 128:(i + 1) * 128, :], waits,
                       ch=f"ln{i % 2}")
            S.op("vector", lambda v, a=xt_b: v.bn_stats(stat[:, 0:6], a[:, 0:512]),
                 waits=[ld])
            stats_dv.append(S.op("vector", lambda v, a=xt_b:
                                 v.bn_stats(stat[:, 6:12], a[:, 512:1024]),
                                 inc="dv"))
            S.op("vector", lambda v: v.bn_aggr(mv[:, 0:2], stat[:, 0:12]),
                 waits=[("dv", S.v("dv"))], inc="dv")
            S.op("vector", lambda v: v.tensor_scalar(vec[:, 2:3], mv[:, 1:2],
                                                     1e-5, None, ALU.add),
                 waits=[("dv", S.v("dv"))], inc="dv")
            S.op("scalar", lambda a: a.activation(vec[:, 3:4], vec[:, 2:3], AF.Sqrt),
                 waits=[("dv", S.v("dv"))], inc="ac")
            S.op("vector", lambda v: v.reciprocal(vec[:, 0:1], vec[:, 3:4]),
                 waits=[("ac", S.v("ac"))], inc="dv")
            S.op("vector", lambda v: v.tensor_scalar(vec[:, 1:2], mv[:, 0:1],
                                                     vec[:, 0:1], -1.0,
                                                     ALU.mult, ALU.mult),
                 waits=[("dv", S.v("dv"))], inc="dv")
            if mode == "l2":
                S.op("vector", lambda v: v.tensor_tensor(vec[:, 4:5], vec[:, 0:1],
                                                         vec[:, 0:1], ALU.mult),
                     waits=[("dv", S.v("dv"))], inc="dv")
                S.op("vector", lambda v: v.scalar_tensor_tensor(
                    vec[:, 4:5], vec[:, 4:5], 1e-6, mv[:, 1:2],
                    ALU.bypass, ALU.mult), waits=[("dv", S.v("dv"))], inc="dv")
                S.op("vector", lambda v: v.tensor_scalar(vec[:, 4:5], vec[:, 4:5],
                                                         1e-6, None, ALU.add),
                     waits=[("dv", S.v("dv"))], inc="dv")
                S.op("scalar", lambda a: a.activation(vec[:, 5:6], vec[:, 4:5],
                                                      AF.Sqrt),
                     waits=[("dv", S.v("dv"))], inc="ac")
                S.op("vector", lambda v: v.reciprocal(vec[:, 6:7], vec[:, 5:6]),
                     waits=[("ac", S.v("ac"))], inc="dv")
                S.op("vector", lambda v: v.tensor_tensor(vec[:, 6:7], vec[:, 6:7],
                                                         vec[:, 0:1], ALU.mult),
                     waits=[("dv", S.v("dv"))], inc="dv")
                S.op("vector", lambda v: v.tensor_scalar(vec[:, 7:8], mv[:, 0:1],
                                                         vec[:, 6:7], -1.0,
                                                         ALU.mult, ALU.mult),
                     waits=[("dv", S.v("dv"))], inc="dv")
                scol, bcol = 6, 7
            else:
                scol, bcol = 0, 1
            wl = [("dv", S.v("dv"))]
            if i >= 2:
                wl.append(("pe", tr_pe[i - 2]))
            if mode == "l0":
                lnf_b = lnf[:, b * 1024:(b + 1) * 1024]
                av = S.op("scalar", lambda a, xi=xt_b, o=lnf_b,
                          s_=vec[:, scol:scol + 1], b_=vec[:, bcol:bcol + 1]:
                          a.activation(o, xi, AF.Identity, bias=b_, scale=s_),
                          waits=wl, inc="ac")
                ln_ac.append(av)
                qdv = emit_quant(st, lnf_b, ln_b, 1024, qs,
                                 waits_in=[("ac", av)])
                ln_ac[-1] = qdv
            elif mode == "l3":
                lnf_b = lnf[:, b * 1024:(b + 1) * 1024]
                wl2 = wl + ([(f"D_lnwb{i % 2}", S.v(f"D_lnwb{i % 2}"))]
                            if i >= 2 else [])
                av = S.op("scalar", lambda a, xi=xt_b, o=lnf_b,
                          s_=vec[:, scol:scol + 1], b_=vec[:, bcol:bcol + 1]:
                          a.activation(o, xi, AF.Identity, bias=b_, scale=s_),
                          waits=wl2, inc="ac")
                cpv = S.op("vector", lambda v, o=ln_b, i_=lnf_b:
                           v.tensor_copy(o, i_), waits=[("ac", av)], inc="dv")
                ln_ac.append(cpv)
                S.dma("gpsimd", wb[i * 128:(i + 1) * 128, :], lnf_b,
                      [("dv", cpv)], ch=f"lnwb{i % 2}")
            else:
                wl2 = wl
                if dump16 and i >= 2:
                    wl2 = wl + [(f"D_lnwb{i % 2}", S.v(f"D_lnwb{i % 2}"))]
                ln_ac.append(S.op("scalar", lambda a, xi=xt_b, o=ln_b,
                                  s_=vec[:, scol:scol + 1],
                                  b_=vec[:, bcol:bcol + 1]:
                                  a.activation(o, xi, AF.Identity, bias=b_,
                                               scale=s_),
                                  waits=wl2, inc="ac"))
                if dump16:
                    S.dma("gpsimd", wb[i * 128:(i + 1) * 128, :], ln_b,
                          [("ac", ln_ac[-1])], ch=f"lnwb{i % 2}")
            if mode == "l2":
                ln_ac[-1] = S.op("vector", lambda v, o=ln_b:
                                 v.tensor_tensor(o, o, sb["w_bcast"][:, :],
                                                 ALU.mult),
                                 waits=[("ac", ln_ac[-1])], inc="dv")
                S.op("vector", lambda v: v.tensor_scalar(vec[:, 4:5], vec[:, 0:1],
                                                         0.5, 1.0, ALU.mult,
                                                         ALU.add),
                     waits=[("dv", S.v("dv"))], inc="dv")
                S.op("vector", lambda v: v.tensor_scalar(vec[:, 5:6], vec[:, 1:2],
                                                         0.5, None, ALU.mult),
                     waits=[("dv", S.v("dv"))], inc="dv")
                wb_dv = S.op("vector", lambda v, o=xt_b:
                             v.tensor_scalar(o, o, vec[:, 4:5], vec[:, 5:6],
                                             ALU.mult, ALU.add),
                             waits=[("dv", S.v("dv"))], inc="dv")
                S.dma("gpsimd", wb[i * 128:(i + 1) * 128, :], xt_b,
                      [("dv", wb_dv)], ch=f"lnwb{i % 2}")
            lnsem = "dv" if mode in ("l2", "l0", "l3") else "ac"
            if mode == "dump":
                tr_pe.append(S.v("pe"))
                continue
            for c in range(DCH):
                w = [(lnsem, ln_ac[-1])] if c == 0 else []
                if len(copy_dv) >= 2:
                    w.append(("dv", copy_dv[-2]))
                slot = (c % 2) * 1024
                pv = S.op("tensor", lambda t, src=ln_b[:, c * 128:(c + 1) * 128],
                          dst=pst[:, slot:slot + 128]:
                          t.transpose(dst, src, ident[:, :]), waits=w, inc="pe")
                copy_dv.append(S.op(
                    "vector",
                    lambda v, src=pst[:, slot:slot + 128],
                    dst=xT[:, c * 4096 + i * 128: c * 4096 + (i + 1) * 128]:
                    v.tensor_copy(dst, src), waits=[("pe", pv)], inc="dv"))
            tr_pe.append(S.v("pe"))
        ln_loads = [("D_ln0", S.v("D_ln0")), ("D_ln1", S.v("D_ln1"))]
        S.flush()
    return ln_loads


def _w3(ap, G, blk):
    return bass.AP(ap.tensor, ap.offset, [list(ap.ap[0]), [blk, G], [1, blk]])


def _b3(ap, G, blk):
    return bass.AP(ap.tensor, ap.offset, [list(ap.ap[0]), [1, G], [0, blk]])


def emit_quant(st, src_ap, dst_ap, W, qs, waits_in=()):
    S = st.S
    G = W // 16
    am, xn, aa, q2, sg = qs["am"], qs["xn"], qs["aa"], qs["q2"], qs["sg"]
    amv = am[:, 0:G]
    s6v = am[:, 176:176 + G]
    inv = am[:, 352:352 + G]
    xnv, aav, q2v = xn[:, 0:W], aa[:, 0:W], q2[:, 0:W]
    sgv = sg[:, 0:W]
    src3 = _w3(src_ap, G, 16)
    d = S.op("vector", lambda v: v.tensor_reduce(amv, src3, AX.X, ALU.max,
                                                 apply_absolute_value=True),
             waits=list(waits_in), inc="dv")
    d = S.op("vector", lambda v: v.tensor_scalar(s6v, amv, 1.0 / 6.0, 1e-12,
                                                 ALU.mult, ALU.max),
             waits=[("dv", d)], inc="dv")
    d = S.op("vector", lambda v: v.reciprocal(inv, s6v), waits=[("dv", d)],
             inc="dv")
    d = S.op("vector", lambda v: v.tensor_tensor(_w3(xnv, G, 16), src3,
                                                 _b3(inv, G, 16), ALU.mult),
             waits=[("dv", d)], inc="dv")
    a1 = S.op("scalar", lambda a: a.activation(sgv, xnv, AF.Sign, bias=0.0),
              waits=[("dv", d)], inc="ac")
    a2 = S.op("scalar", lambda a: a.activation(aav, xnv, AF.Abs),
              waits=[("ac", a1)], inc="ac")
    d = S.op("vector", lambda v: v.tensor_scalar(xnv, aav, 0.25, None,
                                                 ALU.is_gt),
             waits=[("ac", a2)], inc="dv")
    for thr in (0.75, 1.25, 1.75):
        d = S.op("vector", lambda v, t=thr: v.scalar_tensor_tensor(
            xnv, aav, t, xnv, ALU.is_gt, ALU.add), waits=[("dv", d)], inc="dv")
    d = S.op("vector", lambda v: v.tensor_scalar(q2v, aav, 2.5, None,
                                                 ALU.is_gt),
             waits=[("dv", d)], inc="dv")
    d = S.op("vector", lambda v: v.scalar_tensor_tensor(
        q2v, aav, 3.5, q2v, ALU.is_gt, ALU.add), waits=[("dv", d)], inc="dv")
    d = S.op("vector", lambda v: v.tensor_scalar(xnv, xnv, 0.5, None,
                                                 ALU.mult),
             waits=[("dv", d)], inc="dv")
    d = S.op("vector", lambda v: v.tensor_tensor(xnv, xnv, q2v, ALU.add),
             waits=[("dv", d)], inc="dv")
    d = S.op("vector", lambda v: v.tensor_scalar(q2v, aav, 5.0, None,
                                                 ALU.is_gt),
             waits=[("dv", d)], inc="dv")
    d = S.op("vector", lambda v: v.scalar_tensor_tensor(
        xnv, q2v, 2.0, xnv, ALU.mult, ALU.add), waits=[("dv", d)], inc="dv")
    d = S.op("vector", lambda v: v.tensor_tensor(xnv, xnv, sgv, ALU.mult),
             waits=[("dv", d)], inc="dv")
    d = S.op("vector", lambda v: v.tensor_tensor(_w3(dst_ap, G, 16),
                                                 _w3(xnv, G, 16),
                                                 _b3(s6v, G, 16), ALU.mult),
             waits=[("dv", d)], inc="dv")
    return d


def alloc_quant(st, ctx, tag="q"):
    return {
        "am": st.alloc(ctx, tag + "_am", [128, 3 * 176], F32),
        "xn": st.alloc(ctx, tag + "_xn", [128, 2816], F32),
        "aa": st.alloc(ctx, tag + "_aa", [128, 2816], F32),
        "q2": st.alloc(ctx, tag + "_q2", [128, 2816], F32),
        "sg": st.alloc(ctx, tag + "_sg", [128, 2816], BF16),
    }


L0_PIECES = [6, 6, 5, 5]


def emit_l0_cache(st):
    S, sb = st.S, st.sb
    ident = sb["ident"]
    wg = st.din("w_l0_g", (2816, 1024))
    wu = st.din("w_l0_u", (2816, 1024))
    wd = st.din("w_l0_d", (1024, 2816))
    cg = st.dint("c_l0_g", (4 * 128, 6 * 1024), BF16)
    cu = st.dint("c_l0_u", (4 * 128, 6 * 1024), BF16)
    cd = st.dint("c_l0_d", (4 * 128, 6 * 1024), BF16)
    offs = [sum(L0_PIECES[:p]) for p in range(4)]
    with contextlib.ExitStack() as ctx:
        qs = alloc_quant(st, ctx, "wq")
        stg = st.alloc(ctx, "l0c_stage", [128, 2 * 2816], F32)
        q16 = st.alloc(ctx, "l0c_q16", [128, 2 * 2816], BF16)
        img = st.alloc(ctx, "l0c_img", [128, 2 * 6144], BF16)
        pst = st.palloc(ctx, "l0c_pt", [128, 2048], BF16)
        copy_hist = []
        store_hist = []
        qd_hist = []
        for kind, wsrc, cdst in ((0, wg, cg), (1, wu, cu)):
            for p in range(4):
                NF = L0_PIECES[p]
                off = offs[p]
                ib = (kind * 4 + p) % 2
                imgv = img[:, ib * 6144:(ib + 1) * 6144]
                for f in range(NF):
                    rc = off + f
                    cb = (len(qd_hist)) % 2
                    stgv = stg[:, cb * 2816:cb * 2816 + 1024]
                    q16v = q16[:, cb * 2816:cb * 2816 + 1024]
                    w = [("dv", qd_hist[-2])] if len(qd_hist) >= 2 else []
                    ld = S.dma("sync", stgv, wsrc.ap()[rc * 128:(rc + 1) * 128, :],
                               w, ch=f"l0c{cb}")
                    qd = emit_quant(st, stgv, q16v, 1024, qs, waits_in=[ld])
                    qd_hist.append(qd)
                    for c in range(DCH):
                        wt = [("dv", qd)] if c == 0 else []
                        if len(copy_hist) >= 2:
                            wt.append(("dv", copy_hist[-2]))
                        slot = (len(copy_hist) % 2) * 1024
                        pv = S.op("tensor", lambda t, s=q16v[:, c * 128:(c + 1) * 128],
                                  o=pst[:, slot:slot + 128]:
                                  t.transpose(o, s, ident[:, :]), waits=wt,
                                  inc="pe")
                        dst = imgv[:, c * NF * 128 + f * 128: c * NF * 128 + (f + 1) * 128]
                        wcp = [("pe", pv)]
                        if len(store_hist) >= 2 and f == 0 and c == 0:
                            wcp.append(store_hist[-2])
                        copy_hist.append(S.op(
                            "vector", lambda v, s=pst[:, slot:slot + 128], o=dst:
                            v.tensor_copy(o, s), waits=wcp, inc="dv"))
                store_hist.append(S.dma(
                    "gpsimd", cdst.ap()[p * 128:(p + 1) * 128, :NF * 1024],
                    imgv[:, :NF * 1024], [("dv", copy_hist[-1])],
                    ch=f"l0cs{(kind * 4 + p) % 2}"))
        dstrip = st.alloc(ctx, "l0c_dstrip", [128, 2 * 128], BF16)
        dsc_hist = []
        for hc in range(8):
            cb = (len(qd_hist)) % 2
            stgv = stg[:, cb * 2816:cb * 2816 + 2816]
            q16v = q16[:, cb * 2816:cb * 2816 + 2816]
            w = [("dv", qd_hist[-2])] if len(qd_hist) >= 2 else []
            ld = S.dma("sync", stgv, wd.ap()[hc * 128:(hc + 1) * 128, :],
                       w, ch=f"l0c{cb}")
            qd = emit_quant(st, stgv, q16v, 2816, qs, waits_in=[ld])
            qd_hist.append(qd)
            for ic in range(22):
                p = next(pp for pp in range(4)
                         if offs[pp] <= ic < offs[pp] + L0_PIECES[pp])
                c_local = ic - offs[p]
                k = len(dsc_hist)
                wt = [("dv", qd)] if ic == 0 else []
                if k >= 2:
                    wt.append(("ac", dsc_hist[-2]))
                slot = (k % 2) * 1024
                pv = S.op("tensor", lambda t, s=q16v[:, ic * 128:(ic + 1) * 128],
                          o=pst[:, slot:slot + 128]:
                          t.transpose(o, s, ident[:, :]), waits=wt, inc="pe")
                sv = dstrip[:, (k % 2) * 128:(k % 2 + 1) * 128]
                wcp = [("pe", pv)]
                if k >= 2:
                    wcp.append(store_hist[-2])
                cpd = S.op("scalar", lambda a, s=pst[:, slot:slot + 128], o=sv:
                           a.activation(o, s, AF.Copy, bias=0.0, scale=0.5),
                           waits=wcp, inc="ac")
                dsc_hist.append(cpd)
                store_hist.append(S.dma(
                    "gpsimd",
                    cd.ap()[p * 128:(p + 1) * 128,
                            c_local * 1024 + hc * 128: c_local * 1024 + (hc + 1) * 128],
                    sv, [("ac", cpd)], ch=f"l0cs{k % 2}"))
        done = [store_hist[-1], store_hist[-2]]
        S.flush()
    return done


def emit_l0_main(st, x_ap, ln_loads, cache_gd=0):
    S, sb = st.S, st.sb
    xT = sb["xT"]
    ident = sb["ident"]
    cg, cu, cd = (st.dram[n] for n in ("c_l0_g", "c_l0_u", "c_l0_d"))
    offs = [sum(L0_PIECES[:p]) for p in range(4)]
    with contextlib.ExitStack() as ctx:
        qs = alloc_quant(st, ctx, "hq")
        wgs = st.alloc(ctx, "l0_wg", [128, 2 * 6144], BF16)
        wus = st.alloc(ctx, "l0_wu", [128, 2 * 6144], BF16)
        wds = st.alloc(ctx, "l0_wd", [128, 2 * 6144], BF16)
        hge = st.alloc(ctx, "l0_h", [128, 768], F32)
        hq = st.alloc(ctx, "l0_hq", [128, 768], BF16)
        hqT = st.alloc(ctx, "l0_hqT", [128, 2 * 768], BF16)
        xe = st.alloc(ctx, "l0_xe", [128, 2 * 1024], F32)
        psg = st.palloc(ctx, "l0_pg", [128, 1024])
        psu = st.palloc(ctx, "l0_pu", [128, 1024])
        psy = st.palloc(ctx, "l0_py", [128, 2 * 512])
        pst = st.palloc(ctx, "l0_pt", [128, 2048], BF16)

        pe_done = {}
        gd_out = {}
        last_buf_store = {0: None, 1: None}
        sem_hist = {"hmul": [], "yev": [], "tcp": [], "hqd": []}
        for p in range(4):
            NF = L0_PIECES[p]
            W = NF * 128
            jb = p % 2
            wgv = wgs[:, jb * 6144:jb * 6144 + NF * 1024]
            wuv = wus[:, jb * 6144:jb * 6144 + NF * 1024]
            wdv = wds[:, jb * 6144:jb * 6144 + NF * 1024]
            w = list(cache_gd) if (p == 0 and cache_gd) else []
            if p >= 2:
                w.append(("pe", pe_done[p - 2]))
            lg = S.dma("sync", wgv, cg.ap()[p * 128:(p + 1) * 128, :NF * 1024],
                       w, ch=f"l0wg{jb}")
            lu = S.dma("sync", wuv, cu.ap()[p * 128:(p + 1) * 128, :NF * 1024],
                       ch=f"l0wu{jb}")
            ld = S.dma("sync", wdv, cd.ap()[p * 128:(p + 1) * 128, :NF * 1024],
                       ch=f"l0wd{jb}")
            for i in range(NTILE):
                rows = slice(i * 128, (i + 1) * 128)
                pegs = []
                for kind, ps, wv, lw in ((0, psg, wgv, lg), (1, psu, wuv, lu)):
                    for sess in range((W + 511) // 512):
                        n0 = sess * 512
                        nn = min(512, W - n0)
                        wfirst = []
                        if i == 0 and sess == 0:
                            wfirst.append(lw)
                        if kind == 0 and sess == 0:
                            if len(sem_hist["hmul"]) >= 1:
                                wfirst.append(("dv", sem_hist["hmul"][-1]))
                        pv = None
                        for c in range(DCH):
                            pv = S.op(
                                "tensor", lambda t, o=ps[:, n0:n0 + nn],
                                l=xT[:, c * 4096 + i * 128: c * 4096 + (i + 1) * 128],
                                r=wv[:, c * W + n0: c * W + n0 + nn],
                                st_=(c == 0), sp=(c == DCH - 1):
                                t.matmul(o, l, r, start=st_, stop=sp),
                                waits=(wfirst if c == 0 else []),
                                inc=("pe" if c == DCH - 1 else None))
                        pegs.append(pv)
                a1 = S.op("scalar", lambda a, o=hge[:, :W], i_=psg[:, :W]:
                          a.activation(o, i_, AF.Silu),
                          waits=[("pe", pegs[1])], inc="ac")
                hm = S.op("vector", lambda v, o=hge[:, :W], b=psu[:, :W]:
                          v.tensor_tensor(o, o, b, ALU.mult),
                          waits=[("ac", a1), ("pe", pegs[-1])], inc="dv")
                sem_hist["hmul"].append(hm)
                qd = emit_quant(st, hge[:, :W], hq[:, :W], W, qs,
                                waits_in=[("dv", hm)])
                sem_hist["hqd"].append(qd)
                hb = (i % 2) * 768
                for f in range(NF):
                    wt = [("dv", qd)] if f == 0 else []
                    if len(sem_hist["tcp"]) >= 2:
                        wt.append(("dv", sem_hist["tcp"][-2]))
                    slot = (len(sem_hist["tcp"]) % 2) * 1024
                    pv = S.op("tensor", lambda t, s=hq[:, f * 128:(f + 1) * 128],
                              o=pst[:, slot:slot + 128]:
                              t.transpose(o, s, ident[:, :]), waits=wt,
                              inc="pe")
                    sem_hist["tcp"].append(S.op(
                        "vector", lambda v, s=pst[:, slot:slot + 128],
                        o=hqT[:, hb + f * 128: hb + (f + 1) * 128]:
                        v.tensor_copy(o, s), waits=[("pe", pv)], inc="dv"))
                tcp_done = sem_hist["tcp"][-1]
                xb = i % 2
                xev = xe[:, xb * 1024:(xb + 1) * 1024]
                wx = []
                if (p - 1, i) in gd_out:
                    wx.append(gd_out[(p - 1, i)])
                if last_buf_store[xb] is not None:
                    wx.append(last_buf_store[xb])
                lx = S.dma("scalar", xev, x_ap[rows, :], wx, ch=f"l0xl{i % 2}")
                evs = []
                for n in range(2):
                    yps = psy[:, n * 512:(n + 1) * 512]
                    wy = [("dv", tcp_done)]
                    if i == 0 and n == 0:
                        wy.append(ld)
                    if len(sem_hist["yev"]) >= 2:
                        wy.append(("dv", sem_hist["yev"][-2]))
                    pv = None
                    for f in range(NF):
                        pv = S.op("tensor", lambda t, o=yps,
                                  l=hqT[:, hb + f * 128: hb + (f + 1) * 128],
                                  r=wdv[:, f * 1024 + n * 512: f * 1024 + n * 512 + 512],
                                  st_=(f == 0), sp=(f == NF - 1):
                                  t.matmul(o, l, r, start=st_, stop=sp),
                                  waits=(wy if f == 0 else []),
                                  inc=("pe" if f == NF - 1 else None))
                    ev = S.op("vector", lambda v, o=xev[:, n * 512:(n + 1) * 512],
                              p_=yps: v.tensor_tensor(o, o, p_, ALU.add),
                              waits=[("pe", pv), lx], inc="dv")
                    sem_hist["yev"].append(ev)
                    evs.append(ev)
                wout = [("dv", evs[-1])]
                if p == 0:
                    wout += ln_loads
                gd_out[(p, i)] = S.dma("gpsimd", x_ap[rows, :], xev, wout,
                                       ch=f"l0xs{i % 2}")
                last_buf_store[xb] = gd_out[(p, i)]
            pe_done[p] = S.v("pe")
        S.flush()
    return gd_out[(3, NTILE - 1)]


def emit_transpose_only(st, src_dram, tag="tp"):
    S, sb = st.S, st.sb
    xT, ident = sb["xT"], sb["ident"]
    with contextlib.ExitStack() as ctx:
        xt = st.alloc(ctx, tag + "_xt", [128, 2 * 1024], F32)
        b16 = st.alloc(ctx, tag + "_16", [128, 2 * 1024], BF16)
        pst = st.palloc(ctx, tag + "_pt", [128, 2048], BF16)
        cast_ac, tr_pe, copy_dv = [], [], []
        for i in range(NTILE):
            b = i % 2
            xt_b = xt[:, b * 1024:(b + 1) * 1024]
            b_b = b16[:, b * 1024:(b + 1) * 1024]
            w = []
            if i >= 2:
                w.append(("ac", cast_ac[i - 2]))
            ld = S.dma("sync", xt_b, src_dram[i * 128:(i + 1) * 128, :], w,
                       ch=tag + f"{i % 2}")
            wl = [ld]
            if i >= 2:
                wl.append(("pe", tr_pe[i - 2]))
            cast_ac.append(S.op("scalar", lambda a, o=b_b, i_=xt_b:
                                a.activation(o, i_, AF.Copy, bias=0.0,
                                             scale=1.0),
                                waits=wl, inc="ac"))
            for c in range(DCH):
                w2 = [("ac", cast_ac[-1])] if c == 0 else []
                if len(copy_dv) >= 2:
                    w2.append(("dv", copy_dv[-2]))
                slot = (c % 2) * 1024
                pv = S.op("tensor", lambda t, s=b_b[:, c * 128:(c + 1) * 128],
                          o=pst[:, slot:slot + 128]:
                          t.transpose(o, s, ident[:, :]), waits=w2, inc="pe")
                copy_dv.append(S.op(
                    "vector", lambda v, s=pst[:, slot:slot + 128],
                    o=xT[:, c * 4096 + i * 128: c * 4096 + (i + 1) * 128]:
                    v.tensor_copy(o, s), waits=[("pe", pv)], inc="dv"))
            tr_pe.append(S.v("pe"))
        S.flush()


def emit_l3_cache(st):
    S, sb = st.S, st.sb
    qT = st.din("w_l3_q", (6 * 1024, 1024), I32)
    scT = st.din("w_l3_sc", (6 * 1024, 1024))
    cw = st.dint("c_l3_w", (6 * 128, 8192), BF16)
    with contextlib.ExitStack() as ctx:
        qs_ = st.alloc(ctx, "l3c_q", [128, 2 * 1024], I32)
        sc_ = st.alloc(ctx, "l3c_s", [128, 2 * 1024], F32)
        t_ = st.alloc(ctx, "l3c_t", [128, 2 * 1024], F32)
        o_ = st.alloc(ctx, "l3c_o", [128, 2 * 1024], BF16)
        hist, st_hist = [], []
        for k in range(48):
            mi, c = k // 8, k % 8
            b = k % 2
            qv = qs_[:, b * 1024:(b + 1) * 1024]
            sv = sc_[:, b * 1024:(b + 1) * 1024]
            tv = t_[:, b * 1024:(b + 1) * 1024]
            ov = o_[:, b * 1024:(b + 1) * 1024]
            w = [("dv", hist[-2])] if len(hist) >= 2 else []
            l1 = S.dma("sync", qv, qT.ap()[k * 128:(k + 1) * 128, :], w,
                       ch=f"l3cq{b}")
            l2 = S.dma("sync", sv, scT.ap()[k * 128:(k + 1) * 128, :],
                       ch=f"l3cs{b}")
            d = S.op("vector", lambda v, o=tv, i_=qv: v.tensor_copy(o, i_),
                     waits=[l1], inc="dv")
            a = S.op("scalar", lambda a_, o=tv, i_=tv:
                     a_.activation(o, i_, AF.Copy, bias=-1.0, scale=1.0 / 7.5),
                     waits=[("dv", d)], inc="ac")
            w2 = [("ac", a), l2]
            if len(st_hist) >= 2:
                w2.append(st_hist[-2])
            d = S.op("vector", lambda v, o=ov, a_=tv, b_=sv:
                     v.tensor_tensor(o, a_, b_, ALU.mult), waits=w2, inc="dv")
            hist.append(d)
            st_hist.append(S.dma(
                "gpsimd",
                cw.ap()[mi * 128:(mi + 1) * 128, c * 1024:(c + 1) * 1024],
                ov, [("dv", d)], ch=f"l3cw{k % 2}"))
        done = [st_hist[-1], st_hist[-2]]
        S.flush()
    return done


def emit_l3_ffn(st, x_ap, h_ap, cache_gd=0):
    S, sb = st.S, st.sb
    nc = st.nc
    cw = st.dram["c_l3_w"]
    la_d = st.din("w_l3_la", (6 * 128, 256))
    lb_d = st.din("w_l3_lb", (6 * 32, 1024))
    bi_d = st.din("w_l3_bias", (6, 1024))
    gm_d = st.din("c_l3_gam", (3 * 128, 1024))
    bt_d = st.din("c_l3_bet", (3 * 128, 1024))
    on_d = st.din("c_ones_f32", (1, 128))
    ident = sb["ident"]
    xT = sb["xT"]
    with contextlib.ExitStack() as ctx:
        xT2 = st.alloc(ctx, "l3_xT2", [128, 8 * 4096], BF16)
        wsb = st.alloc(ctx, "l3_w", [128, 8192], BF16)
        laf = st.alloc(ctx, "l3_laf", [128, 256], F32)
        la16 = st.alloc(ctx, "l3_la", [128, 256], BF16)
        lbf = st.alloc(ctx, "l3_lbf", [32, 1024], F32)
        lb16 = st.alloc(ctx, "l3_lb", [32, 1024], BF16)
        bia = st.alloc(ctx, "l3_bia", [1, 1024], F32)
        ones = st.alloc(ctx, "l3_ones", [1, 128], F32)
        mid = st.alloc(ctx, "l3_mid", [32, 4096], BF16)
        gam = st.alloc(ctx, "l3_gam", [128, 1024], F32)
        bet = st.alloc(ctx, "l3_bet", [128, 1024], F32)
        ysb = st.alloc(ctx, "l3_ysb", [128, 2 * 1024], F32)
        g16 = st.alloc(ctx, "l3_g16", [128, 2 * 1024], BF16)
        stat = st.alloc(ctx, "l3_stat", [128, 12], F32)
        mv = st.alloc(ctx, "l3_mv", [128, 2], F32)
        vec = st.alloc(ctx, "l3_vec", [128, 8], F32)
        he = st.alloc(ctx, "l3_he", [128, 2 * 1024], F32)
        psm = st.palloc(ctx, "l3_pm", [32, 2 * 512])
        psy = st.palloc(ctx, "l3_py", [128, 2 * 512])
        pst = st.palloc(ctx, "l3_pt", [128, 2048], BF16)

        lon = S.dma("sync", ones[:, :], on_d.ap(), ch="l3on")

        prev_c = [0]

        def load_linear(mi, prev_pe, extra=()):
            w0 = [("pe", prev_pe)] + list(extra)
            if prev_c[0]:
                w0.append(("dv", prev_c[0]))
            lw = S.dma("sync", wsb[:, :], cw.ap()[mi * 128:(mi + 1) * 128, :],
                       w0, ch="l3w")
            l1 = S.dma("sync", laf[:, :], la_d.ap()[mi * 128:(mi + 1) * 128, :],
                       ch="l3la")
            l2 = S.dma("sync", lbf[:, :], lb_d.ap()[mi * 32:(mi + 1) * 32, :],
                       ch="l3lb")
            l3_ = S.dma("sync", bia[:, :], bi_d.ap()[mi:mi + 1, :], ch="l3bi")
            c1 = S.op("vector", lambda v: v.tensor_copy(la16[:, :], laf[:, :]),
                      waits=[l1, ("pe", prev_pe)], inc="dv")
            c2 = S.op("vector", lambda v: v.tensor_copy(lb16[:, :], lbf[:, :]),
                      waits=[l2], inc="dv")
            prev_c[0] = c2
            return lw, c1, c2, l3_

        def compute_mid(srcT, c1):
            cps = []
            for sess in range(8):
                pb = sess % 2
                ps = psm[:, pb * 512:(pb + 1) * 512]
                w = [("dv", c1)] if sess == 0 else []
                if len(cps) >= 2:
                    w.append(("dv", cps[-2]))
                pv = None
                for c in range(DCH):
                    pv = S.op("tensor", lambda t, o=ps,
                              l=la16[:, c * 32:(c + 1) * 32],
                              r=srcT[:, c * 4096 + sess * 512:
                                     c * 4096 + (sess + 1) * 512],
                              st_=(c == 0), sp=(c == DCH - 1):
                              t.matmul(o, l, r, start=st_, stop=sp),
                              waits=(w if c == 0 else []),
                              inc=("pe" if c == DCH - 1 else None))
                cps.append(S.op("vector", lambda v,
                                o=mid[:, sess * 512:(sess + 1) * 512], s=ps:
                                v.tensor_copy(o, s), waits=[("pe", pv)],
                                inc="dv"))
            return cps[-1]

        prev_pe = 0
        cg = cache_gd
        for blk in range(3):
            i0, i1 = 2 * blk, 2 * blk + 1
            lw, c1, c2, lb_ = load_linear(i0, prev_pe,
                                          extra=(list(cg) if cg else ()))
            cg = 0
            lgm = S.dma("sync", gam[:, :],
                        gm_d.ap()[blk * 128:(blk + 1) * 128, :],
                        [("dv", S.v("dv"))], ch="l3gm")
            lbt = S.dma("sync", bet[:, :],
                        bt_d.ap()[blk * 128:(blk + 1) * 128, :], ch="l3bt")
            mid_dv = compute_mid(xT, c1)
            ev_hist, tr_pe, cast_hist, tcp_list = [], [], [], []
            for i in range(NTILE):
                b = i % 2
                yv = ysb[:, b * 1024:(b + 1) * 1024]
                t1v = yv
                g16v = g16[:, b * 1024:(b + 1) * 1024]
                for n in range(2):
                    ps = psy[:, n * 512:(n + 1) * 512]
                    w = []
                    if i == 0 and n == 0:
                        w += [lw, ("dv", mid_dv), lon, lb_]
                        w.append(("dv", c2))
                    if len(ev_hist) >= 2:
                        w.append(("dv", ev_hist[-2]))
                    pv = None
                    for c in range(DCH):
                        pv = S.op("tensor", lambda t, o=ps,
                                  l=xT[:, c * 4096 + i * 128:
                                       c * 4096 + (i + 1) * 128],
                                  r=wsb[:, c * 1024 + n * 512:
                                        c * 1024 + n * 512 + 512],
                                  st_=(c == 0), sp=False:
                                  t.matmul(o, l, r, start=st_, stop=sp),
                                  waits=(w if c == 0 else []),
                                  inc=None)
                    pv = S.op("tensor", lambda t, o=ps,
                              l=mid[:, i * 128:(i + 1) * 128],
                              r=lb16[:, n * 512:n * 512 + 512]:
                              t.matmul(o, l, r, start=False, stop=False),
                              inc=None)
                    pv = S.op("tensor", lambda t, o=ps,
                              l=ones[:, :],
                              r=bia[:, n * 512:n * 512 + 512]:
                              t.matmul(o, l, r, start=False, stop=True),
                              inc="pe")
                    ev_hist.append(S.op(
                        "vector", lambda v, o=yv[:, n * 512:(n + 1) * 512],
                        s=ps: v.tensor_copy(o, s), waits=[("pe", pv)],
                        inc="dv"))
                S.op("vector", lambda v, a=yv: v.bn_stats(stat[:, 0:6],
                                                          a[:, 0:512]),
                     waits=[("dv", ev_hist[-1])], inc="dv")
                S.op("vector", lambda v, a=yv: v.bn_stats(stat[:, 6:12],
                                                          a[:, 512:1024]),
                     waits=[("dv", S.v("dv"))], inc="dv")
                S.op("vector", lambda v: v.bn_aggr(mv[:, 0:2], stat[:, 0:12]),
                     waits=[("dv", S.v("dv"))], inc="dv")
                S.op("vector", lambda v: v.tensor_scalar(vec[:, 2:3],
                                                         mv[:, 1:2], 1e-5,
                                                         None, ALU.add),
                     waits=[("dv", S.v("dv"))], inc="dv")
                aa = S.op("scalar", lambda a: a.activation(vec[:, 3:4],
                                                           vec[:, 2:3],
                                                           AF.Sqrt),
                          waits=[("dv", S.v("dv"))], inc="ac")
                S.op("vector", lambda v: v.reciprocal(vec[:, 0:1],
                                                      vec[:, 3:4]),
                     waits=[("ac", aa)], inc="dv")
                S.op("vector", lambda v: v.tensor_scalar(vec[:, 1:2],
                                                         mv[:, 0:1],
                                                         vec[:, 0:1], -1.0,
                                                         ALU.mult, ALU.mult),
                     waits=[("dv", S.v("dv"))], inc="dv")
                w3 = [("dv", S.v("dv")), lgm, lbt]
                a2 = S.op("scalar", lambda a, o=t1v, i_=yv:
                          a.activation(o, i_, AF.Identity, bias=vec[:, 1:2],
                                       scale=vec[:, 0:1]), waits=w3, inc="ac")
                d = S.op("vector", lambda v, o=t1v: v.tensor_tensor(
                    o, o, gam[:, :], ALU.mult), waits=[("ac", a2)], inc="dv")
                d = S.op("vector", lambda v, o=t1v: v.tensor_tensor(
                    o, o, bet[:, :], ALU.add), waits=[("dv", d)], inc="dv")
                wg = [("dv", d)]
                if len(tr_pe) >= 2:
                    wg.append(("pe", tr_pe[-2]))
                a3 = S.op("scalar", lambda a, o=g16v, i_=t1v:
                          a.activation(o, i_, AF.Gelu), waits=wg, inc="ac")
                cast_hist.append(a3)
                for c in range(DCH):
                    w2 = [("ac", a3)] if c == 0 else []
                    if len(tcp_list) >= 2:
                        w2.append(("dv", tcp_list[-2]))
                    slot = (c % 2) * 1024
                    pv2 = S.op("tensor", lambda t,
                               s=g16v[:, c * 128:(c + 1) * 128],
                               o=pst[:, slot:slot + 128]:
                               t.transpose(o, s, ident[:, :]), waits=w2,
                               inc="pe")
                    tcp_list.append(S.op(
                        "vector", lambda v, s=pst[:, slot:slot + 128],
                        o=xT2[:, c * 4096 + i * 128: c * 4096 + (i + 1) * 128]:
                        v.tensor_copy(o, s), waits=[("pe", pv2)], inc="dv"))
                tr_pe.append(S.v("pe"))
            prev_pe = S.v("pe")
            lw, c1, c2, lb_ = load_linear(i1, prev_pe)
            mid_dv = compute_mid(xT2, c1)
            ev_hist = []
            hst = {0: None, 1: None}
            prev_st = {}
            for i in range(NTILE):
                b = i % 2
                hv = he[:, b * 1024:(b + 1) * 1024]
                wx = []
                if hst[b] is not None:
                    wx.append(hst[b])
                if (blk, i) in prev_st:
                    wx.append(prev_st[(blk, i)])
                lx = S.dma("scalar", hv, h_ap[i * 128:(i + 1) * 128, :], wx,
                           ch=f"l3hl{b}")
                evs = []
                for n in range(2):
                    ps = psy[:, n * 512:(n + 1) * 512]
                    w = []
                    if i == 0 and n == 0:
                        w += [lw, ("dv", mid_dv), lb_, ("dv", c2)]
                    if len(ev_hist) >= 2:
                        w.append(("dv", ev_hist[-2]))
                    pv = None
                    for c in range(DCH):
                        pv = S.op("tensor", lambda t, o=ps,
                                  l=xT2[:, c * 4096 + i * 128:
                                        c * 4096 + (i + 1) * 128],
                                  r=wsb[:, c * 1024 + n * 512:
                                        c * 1024 + n * 512 + 512],
                                  st_=(c == 0), sp=False:
                                  t.matmul(o, l, r, start=st_, stop=sp),
                                  waits=(w if c == 0 else []), inc=None)
                    pv = S.op("tensor", lambda t, o=ps,
                              l=mid[:, i * 128:(i + 1) * 128],
                              r=lb16[:, n * 512:n * 512 + 512]:
                              t.matmul(o, l, r, start=False, stop=False),
                              inc=None)
                    pv = S.op("tensor", lambda t, o=ps,
                              l=ones[:, :], r=bia[:, n * 512:n * 512 + 512]:
                              t.matmul(o, l, r, start=False, stop=True),
                              inc="pe")
                    ev = S.op("vector", lambda v,
                              o=hv[:, n * 512:(n + 1) * 512], s=ps:
                              v.tensor_tensor(o, o, s, ALU.add),
                              waits=[("pe", pv), lx], inc="dv")
                    ev_hist.append(ev)
                    evs.append(ev)
                hst[b] = S.dma("gpsimd", h_ap[i * 128:(i + 1) * 128, :], hv,
                               [("dv", evs[-1])], ch=f"l3hs{b}")
                prev_st[(blk + 1, i)] = hst[b]
            prev_pe = S.v("pe")
            if blk < 2:
                S.flush()
                emit_transpose_only(st, h_ap, tag=f"l3tp{blk}")
        S.flush()
        fin_dv, fin_st = [], {0: None, 1: None}
        for i in range(NTILE):
            b = i % 2
            xv = ysb[:, b * 1024:(b + 1) * 1024]
            hv = he[:, b * 1024:(b + 1) * 1024]
            wx = []
            if fin_st[b] is not None:
                wx.append(fin_st[b])
            l1 = S.dma("sync", hv, h_ap[i * 128:(i + 1) * 128, :], wx,
                       ch=f"l3fh{b}")
            l2 = S.dma("scalar", xv, x_ap[i * 128:(i + 1) * 128, :],
                       list(wx), ch=f"l3fx{b}")
            d = S.op("vector", lambda v, o=xv, h_=hv:
                     v.scalar_tensor_tensor(o, h_, 0.5, o, ALU.mult, ALU.add),
                     waits=[l1, l2], inc="dv")
            fin_dv.append(d)
            fin_st[b] = S.dma("gpsimd", x_ap[i * 128:(i + 1) * 128, :], xv,
                              [("dv", d)], ch=f"l3fs{b}")
        S.flush()
    return fin_st


def emit_plain_cache(st, src_name, cache_name, K, N, tag):
    S, sb = st.S, st.sb
    wsrc = st.din(src_name, (K, N))
    cds = st.dint(cache_name, (128, (K // 128) * N), BF16)
    KC = K // 128
    with contextlib.ExitStack() as ctx:
        stg = st.alloc(ctx, tag + "_stage", [128, 2 * N], F32)
        img = st.alloc(ctx, tag + "_img", [128, KC * N], BF16)
        cast_ac = []
        for c in range(KC):
            cb = c % 2
            stgv = stg[:, cb * N:(cb + 1) * N]
            w = [("ac", cast_ac[-2])] if len(cast_ac) >= 2 else []
            ld = S.dma("sync", stgv, wsrc.ap()[c * 128:(c + 1) * 128, :], w,
                       ch=tag + f"l{cb}")
            cast_ac.append(S.op(
                "scalar", lambda a, i_=stgv, o=img[:, c * N:(c + 1) * N]:
                a.activation(o, i_, AF.Copy, bias=0.0, scale=1.0),
                waits=[ld], inc="ac"))
        done = S.dma("gpsimd", cds.ap(), img[:, :], [("ac", cast_ac[-1])],
                     ch=tag + "s")
        S.flush()
    return [done]


def _sv(t_ap, off, pairs):
    return bass.AP(t_ap.tensor, t_ap.offset + off, pairs)


def emit_attn01_qkv(st, variant, qkv_dram, cache_name, cache_gd=0,
                    bias_name=None, tag="aq"):
    S, sb = st.S, st.sb
    xT = sb["xT"]
    cw = st.dram[cache_name]
    with contextlib.ExitStack() as ctx:
        wq = st.alloc(ctx, tag + "_w", [128, 8 * 3072], BF16)
        qkvt = st.alloc(ctx, tag + "_t", [128, 2 * 3072], BF16)
        ps = st.palloc(ctx, tag + "_ps", [128, 2 * 512])
        msq = S.op("vector", lambda v: v.memset(qkvt[:, :], 0.0), inc="dv")
        lw = S.dma("sync", wq[:, :], cw.ap(), list(cache_gd) if cache_gd else [],
                   ch=tag + "w")
        lbi = lon2 = None
        if bias_name is not None:
            bi = st.alloc(ctx, tag + "_bi", [1, 3072], F32)
            on2 = st.alloc(ctx, tag + "_on", [1, 128], F32)
            bi_d = st.din(bias_name, (1, 3072))
            on2_d = st.din("c_ones_f32", (1, 128))
            lbi = S.dma("sync", bi[:, :], bi_d.ap(), ch=tag + "bi")
            lon2 = S.dma("sync", on2[:, :], on2_d.ap(), ch=tag + "on")
        lnw = None
        if variant == "l0":
            qf32 = st.alloc(ctx, tag + "_f32", [128, 2 * 1024], F32)
            sq = st.alloc(ctx, tag + "_sq", [128, 1024], F32)
            rb = st.alloc(ctx, tag + "_rb", [128, 1024], F32)
            ss = st.alloc(ctx, tag + "_ss", [128, 64], F32)
            ms0 = S.op("vector", lambda v: v.memset(qf32[:, :], 0.0), inc="dv")
            ms0 = S.op("vector", lambda v: v.memset(ss[:, :], 0.0),
                       waits=[("dv", ms0)], inc="dv")
            ms0 = S.op("vector", lambda v: v.memset(sq[:, :], 0.0),
                       waits=[("dv", ms0)], inc="dv")
            ms0 = S.op("vector", lambda v: v.memset(rb[:, :], 0.0),
                       waits=[("dv", ms0)], inc="dv")
            cs = st.alloc(ctx, tag + "_cs", [128, 2 * 512], F32)
            qknw = st.alloc(ctx, tag + "_nw", [128, 2048], F32)
            cos_d = st.din("c_rope_cos", (128, 16 * 32))
            sin_d = st.din("c_rope_sin", (128, 16 * 32))
            qnw_d = st.din("c_qknw", (128, 2048))
            lnw = S.dma("sync", qknw[:, :], qnw_d.ap(), ch=tag + "c3")
        ev_hist = []
        epi_hist = []
        st_hist = {0: None, 1: None}
        for i in range(NTILE):
            b = i % 2
            tq = qkvt[:, b * 3072:(b + 1) * 3072]
            if variant == "l0":
                tc = i % 16
                lc = S.dma("sync", cs[:, 0:32],
                           cos_d.ap()[:, tc * 32:(tc + 1) * 32],
                           [("dv", epi_hist[-1])] if epi_hist else [],
                           ch=tag + "cs")
                ls2 = S.dma("sync", cs[:, 512:544],
                            sin_d.ap()[:, tc * 32:(tc + 1) * 32], ch=tag + "sn")
            for sess in range(6):
                pb = sess % 2
                psv = ps[:, pb * 512:(pb + 1) * 512]
                w = [lw] if (i == 0 and sess == 0) else []
                if i == 0 and sess == 0:
                    w.append(("dv", msq))
                if len(ev_hist) >= 2:
                    w.append(ev_hist[-2])
                if sess == 0 and st_hist[b] is not None:
                    w.append(st_hist[b])
                pv = None
                haslast = bias_name is None
                for c in range(DCH):
                    pv = S.op("tensor", lambda t, o=psv,
                              l=xT[:, c * 4096 + i * 128: c * 4096 + (i + 1) * 128],
                              r=wq[:, c * 3072 + sess * 512: c * 3072 + (sess + 1) * 512],
                              st_=(c == 0), sp=(haslast and c == DCH - 1):
                              t.matmul(o, l, r, start=st_, stop=sp),
                              waits=(w if c == 0 else []),
                              inc=("pe" if haslast and c == DCH - 1 else None))
                if bias_name is not None:
                    wb2 = [lbi, lon2] if (i == 0 and sess == 0) else []
                    pv = S.op("tensor", lambda t, o=psv,
                              l=on2[:, :],
                              r=bi[:, sess * 512:(sess + 1) * 512]:
                              t.matmul(o, l, r, start=False, stop=True),
                              waits=wb2, inc="pe")
                if variant in ("l1", "l2") or sess >= 4:
                    ev_hist.append(("ac", S.op(
                        "scalar", lambda a, o=tq[:, sess * 512:(sess + 1) * 512],
                        i_=psv: a.activation(o, i_, AF.Copy, bias=0.0,
                                             scale=1.0),
                        waits=[("pe", pv)], inc="ac")))
                else:
                    qb = sess // 2
                    dst = qf32[:, qb * 1024 + (sess % 2) * 512:
                               qb * 1024 + (sess % 2 + 1) * 512]
                    wcp = [("pe", pv)]
                    if i == 0:
                        wcp.append(("dv", ms0))
                    ev_hist.append(("dv", S.op(
                        "vector", lambda v, o=dst, i_=psv: v.tensor_copy(o, i_),
                        waits=wcp, inc="dv")))
                if variant == "l0" and sess in (1, 3):
                    qb = sess // 2
                    src = qf32[:, qb * 1024:(qb + 1) * 1024]
                    ssv = ss[:, qb * 32:qb * 32 + 16]
                    rsv = ss[:, qb * 32 + 16:qb * 32 + 32]
                    wv = [ev_hist[-1]]
                    if i == 0 and sess == 1:
                        wv.append(lnw)
                    a1 = S.op("scalar", lambda a, o=sq[:, :], i_=src:
                              a.activation(o, i_, AF.Square), waits=wv,
                              inc="ac")
                    d = S.op("vector", lambda v, o=ssv:
                             v.tensor_reduce(o, _sv(sq[:, :], 0,
                                                    [[1024, 128], [64, 16],
                                                     [1, 64]]),
                                             AX.X, ALU.add),
                             waits=[("ac", a1)], inc="dv")
                    d = S.op("vector", lambda v, o=ssv:
                             v.tensor_scalar(o, o, 1.0 / 64, 1e-6, ALU.mult,
                                             ALU.add), waits=[("dv", d)],
                             inc="dv")
                    a2 = S.op("scalar", lambda a, o=rsv, i_=ssv:
                              a.activation(o, i_, AF.Sqrt),
                              waits=[("dv", d)], inc="ac")
                    d = S.op("vector", lambda v, o=rsv: v.reciprocal(o, o),
                             waits=[("ac", a2)], inc="dv")
                    sv3 = _sv(src, 0, [[2048, 128], [64, 16], [1, 64]])
                    rb3 = _sv(rsv, 0, [[64, 128], [1, 16], [0, 64]])
                    d = S.op("vector", lambda v, a_=sv3, b_=rb3: v.tensor_tensor(
                        a_, a_, b_, ALU.mult), waits=[("dv", d)], inc="dv")
                    d = S.op("vector", lambda v, o=src, nw=qknw[:, qb * 1024:
                                                                (qb + 1) * 1024]:
                             v.tensor_tensor(o, o, nw, ALU.mult),
                             waits=[("dv", d)], inc="dv")
                    x1 = _sv(src, 0, [[2048, 128], [64, 16], [1, 32]])
                    x2 = _sv(src, 32, [[2048, 128], [64, 16], [1, 32]])
                    r1 = _sv(rb[:, :], 0, [[1024, 128], [64, 16], [1, 32]])
                    r2 = _sv(rb[:, :], 32, [[1024, 128], [64, 16], [1, 32]])
                    u1 = _sv(sq[:, :], 0, [[1024, 128], [64, 16], [1, 32]])
                    cosb = _sv(cs[:, :], 0, [[1024, 128], [0, 16], [1, 32]])
                    sinb = _sv(cs[:, :], 512, [[1024, 128], [0, 16], [1, 32]])
                    o1 = _sv(tq[:, :], qb * 1024, [[3072 * 2, 128], [64, 16], [1, 32]])
                    o2 = _sv(tq[:, :], qb * 1024 + 32, [[3072 * 2, 128], [64, 16], [1, 32]])
                    wv2 = [("dv", d), lc, ls2] if sess == 1 else [("dv", d)]
                    d = S.op("vector", lambda v, a_=r1, b_=x1, c_=cosb:
                             v.tensor_tensor(a_, b_, c_, ALU.mult),
                             waits=wv2, inc="dv")
                    d = S.op("vector", lambda v, a_=r2, b_=x1, c_=sinb:
                             v.tensor_tensor(a_, b_, c_, ALU.mult),
                             waits=[("dv", d)], inc="dv")
                    d = S.op("vector", lambda v, a_=u1, b_=x2, c_=sinb:
                             v.tensor_tensor(a_, b_, c_, ALU.mult),
                             waits=[("dv", d)], inc="dv")
                    w_o1 = [("dv", d)]
                    if st_hist[b] is not None:
                        w_o1.append(st_hist[b])
                    d = S.op("vector", lambda v, a_=o1, b_=r1, c_=u1:
                             v.tensor_tensor(a_, b_, c_, ALU.subtract),
                             waits=w_o1, inc="dv")
                    d = S.op("vector", lambda v, a_=u1, b_=x2, c_=cosb:
                             v.tensor_tensor(a_, b_, c_, ALU.mult),
                             waits=[("dv", d)], inc="dv")
                    w_o2 = [("dv", d)]
                    if st_hist[b] is not None:
                        w_o2.append(st_hist[b])
                    d = S.op("vector", lambda v, a_=o2, b_=r2, c_=u1:
                             v.tensor_tensor(a_, b_, c_, ALU.add),
                             waits=w_o2, inc="dv")
                    epi_hist.append(d)
                    ev_hist.append(("dv", d))
            wv = [ev_hist[-1]]
            if variant == "l0":
                wv.append(("dv", epi_hist[-1]))
            st_hist[b] = S.dma("gpsimd", qkv_dram[i * 128:(i + 1) * 128, :],
                               tq, wv, ch=tag + f"s{b}")
        S.flush()


def emit_attn01_core(st, variant, qkv_dram, x_ap, wo_cache, bout_name=None):
    S, sb = st.S, st.sb
    ident = sb["ident"]
    cwo = st.dram[wo_cache]
    with contextlib.ExitStack() as ctx:
        qst = st.alloc(ctx, "ac_qst", [128, 1024], BF16)
        kst = st.alloc(ctx, "ac_kst", [128, 1024], BF16)
        vst = st.alloc(ctx, "ac_vst", [128, 1024], BF16)
        qf = st.alloc(ctx, "ac_qf", [64, 2048], BF16)
        kf = st.alloc(ctx, "ac_kf", [64, 2048], BF16)
        asb = st.alloc(ctx, "ac_a", [128, 2048], BF16)
        at = st.alloc(ctx, "ac_at", [128, 2048], BF16)
        ao = st.alloc(ctx, "ac_ao", [128, 8 * 2048], BF16)
        rl = st.alloc(ctx, "ac_rl", [128, 32], F32)
        nm = st.alloc(ctx, "ac_nm", [128, 4], F32)
        msk = st.alloc(ctx, "ac_msk", [128, 128], F32)
        alb = st.alloc(ctx, "ac_alb", [1, 2048], F32)
        wo = st.alloc(ctx, "ac_wo", [128, 8 * 1024], BF16)
        bo = st.alloc(ctx, "ac_bo", [1, 1024], F32)
        ones = st.alloc(ctx, "ac_on", [1, 128], F32)
        xe = st.alloc(ctx, "ac_xe", [128, 2 * 1024], F32)
        psS = st.palloc(ctx, "ac_pS", [128, 2048])
        pst = st.palloc(ctx, "ac_pt", [128, 2048], BF16)
        pav = st.palloc(ctx, "ac_pav", [128, 512])

        msk_d = st.din("c_causal_mask", (128, 128))
        on_d = st.din("c_ones_f32", (1, 128))
        lm = S.dma("sync", msk[:, :], msk_d.ap(), ch="acm")
        lo = S.dma("sync", wo[:, :], cwo.ap(), ch="acwo")
        lon = S.dma("sync", ones[:, :], on_d.ap(), ch="acon")
        if variant == "l1":
            alb_d = st.din("c_alibi", (16, 2048))
            bob = st.alloc(ctx, "ac_bob", [128, 1024], F32)
            bo_d = st.din("c_l1_bob", (128, 1024))
            lb = S.dma("sync", bob[:, :], bo_d.ap(), ch="acbo")
        SCALE = 0.125

        tr_hist = []
        sm_hist = []
        ao_pe = {}
        ld_hist = {}
        evl = []
        stl = {0: None, 1: None}
        for s in range(SEQ_PER_CORE):
            for h in range(16):
                wv = []
                if (s, h) in ld_hist:
                    pass
                prev = ld_hist.get(s * 16 + h - 1)
                if prev is not None:
                    wv = [("pe", prev)]
                qsrc = _sv(qkv_dram, (s * 2048) * 3072 + h * 64,
                           [[3072 * 128, 16], [3072, 128], [1, 64]])
                ksrc = _sv(qkv_dram, (s * 2048) * 3072 + 1024 + h * 64,
                           [[3072 * 128, 16], [3072, 128], [1, 64]])
                vsrc = _sv(qkv_dram, (s * 2048) * 3072 + 2048 + h * 64,
                           [[3072 * 128, 16], [3072, 128], [1, 64]])
                lq = S.dma("sync",
                           bass.AP(qst[:, :].tensor, 0, [[1024, 128], [64, 16], [1, 64]]),
                           bass.AP(qkv_dram.tensor,
                                   (s * 2048) * 3072 + h * 64,
                                   [[3072, 128], [3072 * 128, 16], [1, 64]]),
                           wv, ch="acq")
                lk = S.dma("sync",
                           bass.AP(kst[:, :].tensor, 0, [[1024, 128], [64, 16], [1, 64]]),
                           bass.AP(qkv_dram.tensor,
                                   (s * 2048) * 3072 + 1024 + h * 64,
                                   [[3072, 128], [3072 * 128, 16], [1, 64]]),
                           list(wv), ch="ack")
                lv = S.dma("sync",
                           bass.AP(vst[:, :].tensor, 0, [[1024, 128], [64, 16], [1, 64]]),
                           bass.AP(qkv_dram.tensor,
                                   (s * 2048) * 3072 + 2048 + h * 64,
                                   [[3072, 128], [3072 * 128, 16], [1, 64]]),
                           list(wv), ch="acv")
                if variant == "l1":
                    la = S.dma("sync", alb[:, :],
                               st.dram["c_alibi"].ap()[h:h + 1, :],
                               list(wv), ch="acal")
                tq_pe = []
                for tens, stg_, fm, lds in ((0, qst, qf, lq), (1, kst, kf, lk)):
                    for tc2 in range(16):
                        k2 = len(tr_hist)
                        w2 = [lds] if tc2 == 0 else []
                        if len(tr_hist) >= 2:
                            w2.append(("dv", tr_hist[-2]))
                        slot = (k2 % 2) * 1024
                        pv = S.op("tensor", lambda t,
                                  s_=stg_[:, tc2 * 64:(tc2 + 1) * 64],
                                  o=pst[0:64, slot:slot + 128]:
                                  t.transpose(o, s_, ident[:, :]),
                                  waits=w2, inc="pe")
                        tr_hist.append(S.op(
                            "vector", lambda v, s_=pst[0:64, slot:slot + 128],
                            o=fm[:, tc2 * 128:(tc2 + 1) * 128]:
                            v.tensor_copy(o, s_),
                            waits=[("pe", pv)], inc="dv"))
                qk_dv = tr_hist[-1]
                for qt in range(16):
                    kext = (qt + 1) * 128
                    nses = (kext + 511) // 512
                    w0 = [("dv", qk_dv)] if qt == 0 else []
                    if sm_hist:
                        w0.append(("ac", sm_hist[-1][0]))
                    pv = None
                    for sess2 in range(nses):
                        n0 = sess2 * 512
                        nn = min(512, kext - n0)
                        pv = S.op("tensor", lambda t, o=psS[:, n0:n0 + nn],
                                  l=qf[:, qt * 128:(qt + 1) * 128],
                                  r=kf[:, n0:n0 + nn]:
                                  t.matmul(o, l, r, start=True, stop=(variant == "l0")),
                                  waits=(w0 if sess2 == 0 else []),
                                  inc=("pe" if variant == "l0" and
                                       sess2 == nses - 1 else None))
                        if variant == "l1":
                            wal = [la] if sess2 == 0 and qt == 0 else []
                            if sess2 == 0 and qt == 0 and h == 0 and s == 0:
                                wal.append(lon)
                            pv = S.op("tensor", lambda t, o=psS[:, n0:n0 + nn],
                                      l=ones[:, :], r=alb[:, n0:n0 + nn]:
                                      t.matmul(o, l, r, start=False, stop=True),
                                      waits=wal,
                                      inc=("pe" if sess2 == nses - 1 else None))
                    dmv = S.op("vector", lambda v,
                               o=psS[:, qt * 128:qt * 128 + 128]:
                               v.tensor_tensor(o, o, msk[:, :], ALU.add),
                               waits=[("pe", pv)] + ([lm] if qt == 0 and h == 0
                                                     and s == 0 else []),
                               inc="dv")
                    d1 = S.op("vector", lambda v, o=nm[:, 0:1],
                              i_=psS[:, 0:kext]:
                              v.reduce_max(o, i_, AX.X, negate=True),
                              waits=[("dv", dmv)], inc="dv")
                    d2 = S.op("vector", lambda v, o=nm[:, 1:2], i_=nm[:, 0:1]:
                              v.tensor_scalar(o, i_, SCALE, None, ALU.mult),
                              waits=[("dv", d1)], inc="dv")
                    a1 = S.op("scalar", lambda a, o=asb[:, 0:kext],
                              i_=psS[:, 0:kext], b_=nm[:, 1:2],
                              l_=nm[:, 2:3]:
                              a.activation(o, i_, AF.Exp, bias=b_, scale=SCALE,
                                           accum_out=l_),
                              waits=[("dv", d2)] +
                              ([("dv", sm_hist[-1][1])] if sm_hist else []),
                              inc="ac")
                    dr = S.op("vector", lambda v, o=nm[:, 3:4],
                              i_=nm[:, 2:3]: v.reciprocal(o, i_),
                              waits=[("ac", a1)], inc="dv")
                    dn = S.op("vector", lambda v, o=asb[:, 0:kext],
                              s_=nm[:, 3:4]:
                              v.tensor_scalar(o, o, s_, None, ALU.mult),
                              waits=[("dv", dr)], inc="dv")
                    last_at = None
                    for kc in range(qt + 1):
                        k2 = len(tr_hist)
                        w2 = [("dv", dn)] if kc == 0 else []
                        if len(tr_hist) >= 2:
                            w2.append(("dv", tr_hist[-2]))
                        slot = (k2 % 2) * 1024
                        pv2 = S.op("tensor", lambda t,
                                   s_=asb[:, kc * 128:(kc + 1) * 128],
                                   o=pst[:, slot:slot + 128]:
                                   t.transpose(o, s_, ident[:, :]),
                                   waits=w2, inc="pe")
                        last_at = S.op(
                            "vector", lambda v, s_=pst[:, slot:slot + 128],
                            o=at[:, kc * 128:(kc + 1) * 128]:
                            v.tensor_copy(o, s_), waits=[("pe", pv2)],
                            inc="dv")
                        tr_hist.append(last_at)
                    pv3 = None
                    hb2 = (h % 2) * 64
                    for kc in range(qt + 1):
                        pv3 = S.op("tensor", lambda t,
                                   o=pav[hb2:hb2 + 64, 0:128],
                                   l=vst[:, kc * 64:(kc + 1) * 64],
                                   r=at[:, kc * 128:(kc + 1) * 128],
                                   st_=(kc == 0), sp=(kc == qt),
                                   tp=(0, hb2):
                                   t.matmul(o, l, r, start=st_, stop=sp,
                                            tile_position=tp),
                                   waits=([("dv", last_at), lv] if kc == 0
                                          else []),
                                   inc=("pe" if kc == qt else None))
                    dav = S.op("vector", lambda v,
                               o=ao[hb2:hb2 + 64,
                                    (h // 2) * 2048 + qt * 128:
                                    (h // 2) * 2048 + (qt + 1) * 128],
                               i_=pav[hb2:hb2 + 64, 0:128]: v.tensor_copy(o, i_),
                               waits=[("pe", pv3)], inc="dv")
                    sm_hist.append((a1, dav))
                ld_hist[s * 16 + h] = S.v("pe")
            ao_pe[s] = S.v("pe")
            for tt in range(16):
                gi = s * 16 + tt
                b = tt % 2
                xev = xe[:, b * 1024:(b + 1) * 1024]
                wv = []
                if stl[b] is not None:
                    wv.append(stl[b])
                lx = S.dma("scalar", xev,
                           x_ap[s * 2048 + tt * 128: s * 2048 + (tt + 1) * 128, :],
                           wv, ch=f"acxl{b}")
                for n in range(2):
                    psv = psS[:, n * 512:(n + 1) * 512]
                    w3 = [("dv", sm_hist[-1][1])] if tt == 0 and n == 0 else []
                    if tt == 0 and n == 0:
                        w3.append(lo)
                    if len(evl) >= 2:
                        w3.append(("dv", evl[-2]))
                    pv = None
                    for c in range(DCH):
                        pv = S.op("tensor", lambda t, o=psv,
                                  l=ao[:, c * 2048 + tt * 128: c * 2048 + (tt + 1) * 128],
                                  r=wo[:, c * 1024 + n * 512: c * 1024 + n * 512 + 512],
                                  st_=(c == 0), sp=(c == DCH - 1):
                                  t.matmul(o, l, r, start=st_, stop=sp),
                                  waits=(w3 if c == 0 else []),
                                  inc=("pe" if c == DCH - 1 else None))
                    ev = S.op("vector", lambda v, o=xev[:, n * 512:(n + 1) * 512],
                              p_=psv:
                              v.tensor_tensor(o, o, p_, ALU.add),
                              waits=[("pe", pv), lx], inc="dv")
                    if variant == "l1":
                        ev = S.op("vector", lambda v,
                                  o=xev[:, n * 512:(n + 1) * 512],
                                  b_=bob[:, n * 512:(n + 1) * 512]:
                                  v.tensor_tensor(o, o, b_, ALU.add),
                                  waits=[("dv", ev)] +
                                  ([lb] if tt == 0 and n == 0 else []),
                                  inc="dv")
                    evl.append(ev)
                stl[b] = S.dma("gpsimd",
                               x_ap[s * 2048 + tt * 128: s * 2048 + (tt + 1) * 128, :],
                               xev, [("dv", evl[-1])], ch=f"acxs{b}")
        S.flush()


def emit_compose_cache(st, cA, cB, cout, cache_gds, tag):
    S, sb = st.S, st.sb
    a_d, b_d = st.dram[cA], st.dram[cB]
    o_d = st.dint(cout, (128, 8 * 3072), BF16)
    with contextlib.ExitStack() as ctx:
        asb = st.alloc(ctx, tag + "_a", [128, 8 * 1024], BF16)
        bsb = st.alloc(ctx, tag + "_b", [128, 8 * 3072], BF16)
        osb = st.alloc(ctx, tag + "_o", [128, 8 * 3072], BF16)
        ps = st.palloc(ctx, tag + "_p", [128, 2 * 512])
        la = S.dma("sync", asb[:, :], a_d.ap(), list(cache_gds), ch=tag + "a")
        lb = S.dma("sync", bsb[:, :], b_d.ap(), ch=tag + "b")
        evh = []
        for m in range(8):
            for n in range(6):
                pb = (m * 6 + n) % 2
                psv = ps[:, pb * 512:(pb + 1) * 512]
                w = [la, lb] if (m == 0 and n == 0) else []
                if len(evh) >= 2:
                    w.append(("ac", evh[-2]))
                pv = None
                for c in range(DCH):
                    pv = S.op("tensor", lambda t, o=psv,
                              l=asb[:, c * 1024 + m * 128: c * 1024 + (m + 1) * 128],
                              r=bsb[:, c * 3072 + n * 512: c * 3072 + (n + 1) * 512],
                              st_=(c == 0), sp=(c == DCH - 1):
                              t.matmul(o, l, r, start=st_, stop=sp),
                              waits=(w if c == 0 else []),
                              inc=("pe" if c == DCH - 1 else None))
                evh.append(S.op(
                    "scalar", lambda a, o=osb[:, m * 3072 + n * 512:
                                              m * 3072 + (n + 1) * 512],
                    i_=psv: a.activation(o, i_, AF.Copy, bias=0.0, scale=1.0),
                    waits=[("pe", pv)], inc="ac"))
        done = S.dma("gpsimd", o_d.ap(), osb[:, :], [("ac", evh[-1])],
                     ch=tag + "s")
        S.flush()
    return [done]


def emit_l2_fusionw(st):
    S, sb = st.S, st.sb
    fw_d = st.din("L2_attn_fusion_w", (1, 2))
    on_d = st.din("c_ones_f32", (1, 128))
    blo_d = st.din("c_l2_blo", (128, 1024))
    bgo_d = st.din("c_l2_bgo", (128, 1024))
    wsb = st.sb["l2a_w"]
    blo = st.sb["l2a_blo"]
    bgo = st.sb["l2a_bgo"]
    with contextlib.ExitStack() as ctx:
        fw = st.alloc(ctx, "l2a_fw", [1, 8], F32)
        on = st.alloc(ctx, "l2a_on", [1, 128], F32)
        pw = st.palloc(ctx, "l2a_pw", [128, 512])
        l1 = S.dma("sync", fw[:, 0:2], fw_d.ap(), ch="l2afw")
        l2 = S.dma("sync", on[:, :], on_d.ap(), ch="l2aon")
        l3 = S.dma("sync", blo[:, :], blo_d.ap(), ch="l2ablo")
        l4 = S.dma("sync", bgo[:, :], bgo_d.ap(), ch="l2abgo")
        d = S.op("vector", lambda v: v.reduce_max(fw[:, 2:3], fw[:, 0:2], AX.X,
                                                  negate=True),
                 waits=[l1], inc="dv")
        a = S.op("scalar", lambda a_: a_.activation(fw[:, 4:6], fw[:, 0:2],
                                                    AF.Exp, bias=fw[:, 2:3],
                                                    accum_out=fw[:, 3:4]),
                 waits=[("dv", d)], inc="ac")
        d = S.op("vector", lambda v: v.reciprocal(fw[:, 6:7], fw[:, 3:4]),
                 waits=[("ac", a)], inc="dv")
        d = S.op("vector", lambda v: v.tensor_scalar(fw[:, 4:6], fw[:, 4:6],
                                                     fw[:, 6:7], None,
                                                     ALU.mult),
                 waits=[("dv", d)], inc="dv")
        p = S.op("tensor", lambda t: t.matmul(pw[:, 0:2], on[:, :], fw[:, 4:6]),
                 waits=[("dv", d), l2], inc="pe")
        d = S.op("vector", lambda v: v.tensor_copy(wsb[:, :], pw[:, 0:2]),
                 waits=[("pe", p)], inc="dv")
        d = S.op("vector", lambda v: v.tensor_scalar(blo[:, :], blo[:, :],
                                                     wsb[:, 0:1], None,
                                                     ALU.mult),
                 waits=[("dv", d), l3], inc="dv")
        d = S.op("vector", lambda v: v.tensor_scalar(bgo[:, :], bgo[:, :],
                                                     wsb[:, 1:2], None,
                                                     ALU.mult),
                 waits=[("dv", d), l4], inc="dv")
        S.flush()


def emit_attn2_core(st, qkv_ap, win, fused_ap, wo_cache, first_branch,
                    wcol):
    S, sb = st.S, st.sb
    ident = sb["ident"]
    cwo = st.dram[wo_cache]
    wsb = sb["l2a_w"]
    bb = sb["l2a_blo"] if first_branch else sb["l2a_bgo"]
    NW = 2048 // win
    QT = win // 128
    tag = "w%d" % win
    with contextlib.ExitStack() as ctx:
        qst = st.alloc(ctx, tag + "_qst", [128, 1024], BF16)
        kst = st.alloc(ctx, tag + "_kst", [128, 1024], BF16)
        vst = st.alloc(ctx, tag + "_vst", [128, 1024], BF16)
        qf = st.alloc(ctx, tag + "_qf", [64, 2048], BF16)
        kf = st.alloc(ctx, tag + "_kf", [64, 2048], BF16)
        asb = st.alloc(ctx, tag + "_a", [128, 1024], BF16)
        at = st.alloc(ctx, tag + "_at", [128, 1024], BF16)
        ao = st.alloc(ctx, tag + "_ao", [128, 8 * 2048], BF16)
        rl = st.alloc(ctx, tag + "_rl", [128, 32], F32)
        nm = st.alloc(ctx, tag + "_nm", [128, 4], F32)
        msk = st.alloc(ctx, tag + "_msk", [128, 128], F32)
        wo = st.alloc(ctx, tag + "_wo", [128, 8 * 1024], BF16)
        xe = st.alloc(ctx, tag + "_xe", [128, 2 * 1024], F32)
        psS = st.palloc(ctx, tag + "_pS", [128, 1024])
        pst = st.palloc(ctx, tag + "_pt", [128, 2048], BF16)
        pav = st.palloc(ctx, tag + "_pav", [128, 512])

        msk_d = st.din("c_l2_mask", (128, 128))
        lm = S.dma("sync", msk[:, :], msk_d.ap(), ch=tag + "m")
        lo = S.dma("sync", wo[:, :], cwo.ap(), ch=tag + "wo")
        SCALE = 0.125

        tr_hist, sm_hist = [], []
        ld_hist = {}
        evl = []
        stl = {0: None, 1: None}
        for s in range(SEQ_PER_CORE):
            for h in range(16):
                wv = []
                prev = ld_hist.get(s * 16 + h - 1)
                if prev is not None:
                    wv = [("pe", prev)]
                lq = S.dma("sync",
                           bass.AP(qst[:, :].tensor, 0, [[1024, 128], [64, 16], [1, 64]]),
                           bass.AP(qkv_ap.tensor,
                                   qkv_ap.offset + (s * 2048) * 3072 + h * 64,
                                   [[3072, 128], [3072 * 128, 16], [1, 64]]),
                           wv, ch=tag + "q")
                lk = S.dma("sync",
                           bass.AP(kst[:, :].tensor, 0, [[1024, 128], [64, 16], [1, 64]]),
                           bass.AP(qkv_ap.tensor,
                                   qkv_ap.offset + (s * 2048) * 3072 + 1024 + h * 64,
                                   [[3072, 128], [3072 * 128, 16], [1, 64]]),
                           list(wv), ch=tag + "k")
                lv = S.dma("sync",
                           bass.AP(vst[:, :].tensor, 0, [[1024, 128], [64, 16], [1, 64]]),
                           bass.AP(qkv_ap.tensor,
                                   qkv_ap.offset + (s * 2048) * 3072 + 2048 + h * 64,
                                   [[3072, 128], [3072 * 128, 16], [1, 64]]),
                           list(wv), ch=tag + "v")
                for tens, stg_, fm, lds in ((0, qst, qf, lq), (1, kst, kf, lk)):
                    for tc2 in range(16):
                        k2 = len(tr_hist)
                        w2 = [lds] if tc2 == 0 else []
                        if len(tr_hist) >= 2:
                            w2.append(("dv", tr_hist[-2]))
                        slot = (k2 % 2) * 1024
                        pv = S.op("tensor", lambda t,
                                  s_=stg_[:, tc2 * 64:(tc2 + 1) * 64],
                                  o=pst[0:64, slot:slot + 128]:
                                  t.transpose(o, s_, ident[:, :]),
                                  waits=w2, inc="pe")
                        tr_hist.append(S.op(
                            "vector", lambda v, s_=pst[0:64, slot:slot + 128],
                            o=fm[:, tc2 * 128:(tc2 + 1) * 128]:
                            v.tensor_copy(o, s_),
                            waits=[("pe", pv)], inc="dv"))
                qk_dv = tr_hist[-1]
                for wdw in range(NW):
                    for qt in range(QT):
                        gqt = wdw * QT + qt
                        k0 = wdw * win
                        w0 = [("dv", qk_dv)] if wdw == 0 and qt == 0 else []
                        if sm_hist:
                            w0.append(("ac", sm_hist[-1][0]))
                        pv = None
                        nses = (win + 511) // 512
                        for sess2 in range(nses):
                            n0 = sess2 * 512
                            nn = min(512, win - n0)
                            pv = S.op("tensor", lambda t, o=psS[:, n0:n0 + nn],
                                      l=qf[:, gqt * 128:(gqt + 1) * 128],
                                      r=kf[:, k0 + n0:k0 + n0 + nn]:
                                      t.matmul(o, l, r, start=True, stop=True),
                                      waits=(w0 if sess2 == 0 else []),
                                      inc=("pe" if sess2 == nses - 1 else None))
                        dmv = S.op("vector", lambda v,
                                   o=psS[:, qt * 128:qt * 128 + 128]:
                                   v.tensor_tensor(o, o, msk[:, :], ALU.add),
                                   waits=[("pe", pv)] +
                                   ([lm] if s == 0 and h == 0 and wdw == 0
                                    and qt == 0 else []),
                                   inc="dv")
                        if (qt + 1) * 128 < win:
                            dmv = S.op("vector", lambda v,
                                       o=psS[:, (qt + 1) * 128:win]:
                                       v.tensor_scalar(o, o, 8.0, None,
                                                       ALU.add),
                                       waits=[("dv", dmv)], inc="dv")
                        d1 = S.op("vector", lambda v, o=nm[:, 0:1],
                                  i_=psS[:, 0:win]:
                                  v.reduce_max(o, i_, AX.X, negate=True),
                                  waits=[("dv", dmv)], inc="dv")
                        d2 = S.op("vector", lambda v, o=nm[:, 1:2],
                                  i_=nm[:, 0:1]:
                                  v.tensor_scalar(o, i_, SCALE, None, ALU.mult),
                                  waits=[("dv", d1)], inc="dv")
                        a1 = S.op("scalar", lambda a, o=asb[:, 0:win],
                                  i_=psS[:, 0:win], b_=nm[:, 1:2],
                                  l_=nm[:, 2:3]:
                                  a.activation(o, i_, AF.Exp, bias=b_,
                                               scale=SCALE, accum_out=l_),
                                  waits=[("dv", d2)] +
                                  ([("dv", sm_hist[-1][1])] if sm_hist else []),
                                  inc="ac")
                        dr = S.op("vector", lambda v, o=nm[:, 3:4],
                                  i_=nm[:, 2:3]: v.reciprocal(o, i_),
                                  waits=[("ac", a1)], inc="dv")
                        dn = S.op("vector", lambda v, o=asb[:, 0:win],
                                  s_=nm[:, 3:4]:
                                  v.tensor_scalar(o, o, s_, None, ALU.mult),
                                  waits=[("dv", dr)], inc="dv")
                        last_at = None
                        for kc in range(win // 128):
                            k2 = len(tr_hist)
                            w2 = [("dv", dn)] if kc == 0 else []
                            if len(tr_hist) >= 2:
                                w2.append(("dv", tr_hist[-2]))
                            slot = (k2 % 2) * 1024
                            pv2 = S.op("tensor", lambda t,
                                       s_=asb[:, kc * 128:(kc + 1) * 128],
                                       o=pst[:, slot:slot + 128]:
                                       t.transpose(o, s_, ident[:, :]),
                                       waits=w2, inc="pe")
                            last_at = S.op(
                                "vector", lambda v,
                                s_=pst[:, slot:slot + 128],
                                o=at[:, kc * 128:(kc + 1) * 128]:
                                v.tensor_copy(o, s_), waits=[("pe", pv2)],
                                inc="dv")
                            tr_hist.append(last_at)
                        pv3 = None
                        hb2 = (h % 2) * 64
                        for kc in range(win // 128):
                            kcg = wdw * QT + kc
                            pv3 = S.op("tensor", lambda t,
                                       o=pav[hb2:hb2 + 64, 0:128],
                                       l=vst[:, kcg * 64:(kcg + 1) * 64],
                                       r=at[:, kc * 128:(kc + 1) * 128],
                                       st_=(kc == 0), sp=(kc == win // 128 - 1),
                                       tp=(0, hb2):
                                       t.matmul(o, l, r, start=st_, stop=sp,
                                                tile_position=tp),
                                       waits=([("dv", last_at), lv] if kc == 0
                                              else []),
                                       inc=("pe" if kc == win // 128 - 1
                                            else None))
                        dav = S.op("vector", lambda v,
                                   o=ao[hb2:hb2 + 64,
                                        (h // 2) * 2048 + gqt * 128:
                                        (h // 2) * 2048 + (gqt + 1) * 128],
                                   i_=pav[hb2:hb2 + 64, 0:128]:
                                   v.tensor_copy(o, i_),
                                   waits=[("pe", pv3)], inc="dv")
                        sm_hist.append((a1, dav))
                        if h == 0 and s == 0:
                            pass
                ld_hist[s * 16 + h] = S.v("pe")
            for tt in range(16):
                b = tt % 2
                xev = xe[:, b * 1024:(b + 1) * 1024]
                wv = []
                if stl[b] is not None:
                    wv.append(stl[b])
                if not first_branch:
                    lx = S.dma("scalar", xev,
                               fused_ap[s * 2048 + tt * 128:
                                        s * 2048 + (tt + 1) * 128, :],
                               wv, ch=tag + f"xl{b}")
                evs = []
                for n in range(2):
                    psv = psS[:, n * 512:(n + 1) * 512]
                    w3 = [("dv", sm_hist[-1][1])] if tt == 0 and n == 0 else []
                    if tt == 0 and n == 0:
                        w3.append(lo)
                    if len(evl) >= 2:
                        w3.append(("dv", evl[-2]))
                    pv = None
                    for c in range(DCH):
                        pv = S.op("tensor", lambda t, o=psv,
                                  l=ao[:, c * 2048 + tt * 128: c * 2048 + (tt + 1) * 128],
                                  r=wo[:, c * 1024 + n * 512: c * 1024 + n * 512 + 512],
                                  st_=(c == 0), sp=(c == DCH - 1):
                                  t.matmul(o, l, r, start=st_, stop=sp),
                                  waits=(w3 if c == 0 else []),
                                  inc=("pe" if c == DCH - 1 else None))
                    if first_branch:
                        ev = S.op("vector", lambda v,
                                  o=xev[:, n * 512:(n + 1) * 512], p_=psv,
                                  s_=wsb[:, wcol:wcol + 1]:
                                  v.tensor_scalar(o, p_, s_, None, ALU.mult),
                                  waits=[("pe", pv)] + wv, inc="dv")
                    else:
                        ev = S.op("vector", lambda v,
                                  o=xev[:, n * 512:(n + 1) * 512], p_=psv,
                                  s_=wsb[:, wcol:wcol + 1]:
                                  v.scalar_tensor_tensor(o, p_, s_, o,
                                                         ALU.mult, ALU.add),
                                  waits=[("pe", pv), lx], inc="dv")
                    ev = S.op("vector", lambda v,
                              o=xev[:, n * 512:(n + 1) * 512],
                              b_=bb[:, n * 512:(n + 1) * 512]:
                              v.tensor_tensor(o, o, b_, ALU.add),
                              waits=[("dv", ev)], inc="dv")
                    evl.append(ev)
                stl[b] = S.dma("gpsimd",
                               fused_ap[s * 2048 + tt * 128:
                                        s * 2048 + (tt + 1) * 128, :],
                               xev, [("dv", evl[-1])], ch=tag + f"xs{b}")
        S.flush()


def emit_l2a_final(st, x_ap, fused_ap):
    S, sb = st.S, st.sb
    gm_d = st.din("c_l2a_gam", (128, 1024))
    bt_d = st.din("c_l2a_bet", (128, 1024))
    with contextlib.ExitStack() as ctx:
        fe = st.alloc(ctx, "l2f_fe", [128, 2 * 1024], F32)
        xe = st.alloc(ctx, "l2f_xe", [128, 2 * 1024], F32)
        gam = st.alloc(ctx, "l2f_g", [128, 1024], F32)
        bet = st.alloc(ctx, "l2f_b", [128, 1024], F32)
        stat = st.alloc(ctx, "l2f_st", [128, 12], F32)
        mv = st.alloc(ctx, "l2f_mv", [128, 2], F32)
        vec = st.alloc(ctx, "l2f_v", [128, 8], F32)
        lg = S.dma("sync", gam[:, :], gm_d.ap(), ch="l2fg")
        lb = S.dma("sync", bet[:, :], bt_d.ap(), ch="l2fb")
        dvh = []
        sth = {0: None, 1: None}
        for i in range(NTILE):
            b = i % 2
            fv = fe[:, b * 1024:(b + 1) * 1024]
            xv = xe[:, b * 1024:(b + 1) * 1024]
            wv = []
            if sth[b] is not None:
                wv.append(sth[b])
            l1 = S.dma("sync", fv, fused_ap[i * 128:(i + 1) * 128, :],
                       list(wv) + ([("dv", dvh[-2])] if len(dvh) >= 2 else []),
                       ch=f"l2ff{b}")
            l2 = S.dma("scalar", xv, x_ap[i * 128:(i + 1) * 128, :],
                       list(wv), ch=f"l2fx{b}")
            S.op("vector", lambda v, a=fv: v.bn_stats(stat[:, 0:6], a[:, 0:512]),
                 waits=[l1], inc="dv")
            S.op("vector", lambda v, a=fv: v.bn_stats(stat[:, 6:12],
                                                      a[:, 512:1024]),
                 waits=[("dv", S.v("dv"))], inc="dv")
            S.op("vector", lambda v: v.bn_aggr(mv[:, 0:2], stat[:, 0:12]),
                 waits=[("dv", S.v("dv"))], inc="dv")
            S.op("vector", lambda v: v.tensor_scalar(vec[:, 2:3], mv[:, 1:2],
                                                     1e-5, None, ALU.add),
                 waits=[("dv", S.v("dv"))], inc="dv")
            aa = S.op("scalar", lambda a: a.activation(vec[:, 3:4], vec[:, 2:3],
                                                       AF.Sqrt),
                      waits=[("dv", S.v("dv"))], inc="ac")
            S.op("vector", lambda v: v.reciprocal(vec[:, 0:1], vec[:, 3:4]),
                 waits=[("ac", aa)], inc="dv")
            S.op("vector", lambda v: v.tensor_scalar(vec[:, 1:2], mv[:, 0:1],
                                                     vec[:, 0:1], -1.0,
                                                     ALU.mult, ALU.mult),
                 waits=[("dv", S.v("dv"))], inc="dv")
            a2 = S.op("scalar", lambda a, o=fv, i_=fv:
                      a.activation(o, i_, AF.Identity, bias=vec[:, 1:2],
                                   scale=vec[:, 0:1]),
                      waits=[("dv", S.v("dv"))], inc="ac")
            d = S.op("vector", lambda v, o=fv: v.tensor_tensor(o, o, gam[:, :],
                                                               ALU.mult),
                     waits=[("ac", a2)] + ([lg] if i == 0 else []), inc="dv")
            d = S.op("vector", lambda v, o=fv: v.tensor_tensor(o, o, bet[:, :],
                                                               ALU.add),
                     waits=[("dv", d)] + ([lb] if i == 0 else []), inc="dv")
            d = S.op("vector", lambda v, o=xv, f_=fv:
                     v.tensor_tensor(o, o, f_, ALU.add),
                     waits=[("dv", d), l2], inc="dv")
            dvh.append(d)
            sth[b] = S.dma("gpsimd", x_ap[i * 128:(i + 1) * 128, :], xv,
                           [("dv", d)], ch=f"l2fs{b}")
        S.flush()


def emit_l3a(st, x_ap, ln_dump):
    S, sb = st.S, st.sb
    fre_d = st.din("t_fre", (2048, 2176), BF16)
    fim_d = st.din("t_fim", (2048, 2176), BF16)
    gre_d = st.din("t_gre", (2176, 2048), BF16)
    gim_d = st.din("t_gim", (2176, 2048), BF16)
    pe_d = st.din("c_posT", (64, 2048))
    w1_d = st.din("w_a3_w1", (64, 1024))
    w2_d = st.din("w_a3_w2", (1024, 1024))
    b1_d = st.din("c_a3_b1", (1, 1024))
    b2_d = st.din("c_a3_b2", (1, 1024))
    on_d = st.din("c_ones_f32", (1, 128))
    ident = sb["ident"]
    phre_d = st.dint("phi_re", (2176, 1024), BF16)
    phim_d = st.dint("phi_im", (2176, 1024), BF16)
    KC = 17
    with contextlib.ExitStack() as ctx:
        pef = st.alloc(ctx, "a3_pef", [64, 2048], F32)
        pe16 = st.alloc(ctx, "a3_pe16", [64, 2048], BF16)
        w1f = st.alloc(ctx, "a3_w1f", [64, 1024], F32)
        w116 = st.alloc(ctx, "a3_w116", [64, 1024], BF16)
        w2f = st.alloc(ctx, "a3_w2f", [128, 2 * 1024], F32)
        w216 = st.alloc(ctx, "a3_w216", [128, 8 * 1024], BF16)
        b1s = st.alloc(ctx, "a3_b1", [1, 1024], F32)
        b2s = st.alloc(ctx, "a3_b2", [1, 1024], F32)
        ones = st.alloc(ctx, "a3_on", [1, 128], F32)
        h116 = st.alloc(ctx, "a3_h116", [128, 2 * 1024], BF16)
        h1T = st.alloc(ctx, "a3_h1T", [128, 8 * 2048], BF16)
        filt = st.alloc(ctx, "a3_filt", [128, 16 * 1024], BF16)
        fsb = st.alloc(ctx, "a3_fsb", [128, 2 * 2048], BF16)
        stg = st.alloc(ctx, "a3_stg", [128, 2 * 1024], BF16)
        psA = st.palloc(ctx, "a3_pA", [128, 2 * 512])
        psB = st.palloc(ctx, "a3_pB", [128, 2 * 1024])
        pst = st.palloc(ctx, "a3_pt", [128, 2048], BF16)

        l1 = S.dma("sync", pef[:, :], pe_d.ap(), ch="a3pe")
        l2 = S.dma("sync", w1f[:, :], w1_d.ap(), ch="a3w1")
        l3 = S.dma("sync", b1s[:, :], b1_d.ap(), ch="a3b1")
        l4 = S.dma("sync", b2s[:, :], b2_d.ap(), ch="a3b2")
        l5 = S.dma("sync", ones[:, :], on_d.ap(), ch="a3on")
        c1 = S.op("vector", lambda v: v.tensor_copy(pe16[:, :], pef[:, :]),
                  waits=[l1], inc="dv")
        c2 = S.op("vector", lambda v: v.tensor_copy(w116[:, :], w1f[:, :]),
                  waits=[l2], inc="dv")
        cw2 = []
        for c in range(8):
            cb = c % 2
            lw = S.dma("sync", w2f[:, cb * 1024:(cb + 1) * 1024],
                       w2_d.ap()[c * 128:(c + 1) * 128, :],
                       [("dv", cw2[-2])] if len(cw2) >= 2 else [], ch=f"a3w2{cb}")
            cw2.append(S.op("vector", lambda v,
                            o=w216[:, c * 1024:(c + 1) * 1024],
                            i_=w2f[:, cb * 1024:(cb + 1) * 1024]:
                            v.tensor_copy(o, i_), waits=[lw], inc="dv"))
        trh = []
        geh = []
        for tt in range(16):
            b = tt % 2
            hv = h116[:, b * 1024:(b + 1) * 1024]
            for n in range(2):
                ps = psA[:, n * 512:(n + 1) * 512]
                w = [("dv", c1), ("dv", c2), l3, l5] if tt == 0 and n == 0 else []
                if len(geh) >= 2:
                    w.append(("ac", geh[-2]))
                S.op("tensor", lambda t, o=ps,
                     l=pe16[:, tt * 128:(tt + 1) * 128],
                     r=w116[:, n * 512:(n + 1) * 512]:
                     t.matmul(o, l, r, start=True, stop=False),
                     waits=w, inc=None)
                pv = S.op("tensor", lambda t, o=ps, l=ones[:, :],
                          r=b1s[:, n * 512:(n + 1) * 512]:
                          t.matmul(o, l, r, start=False, stop=True), inc="pe")
                geh.append(S.op("scalar", lambda a,
                                o=hv[:, n * 512:(n + 1) * 512], i_=ps:
                                a.activation(o, i_, AF.Gelu),
                                waits=[("pe", pv)], inc="ac"))
            for c in range(DCH):
                w2 = [("ac", geh[-1])] if c == 0 else []
                if len(trh) >= 2:
                    w2.append(("dv", trh[-2]))
                slot = (len(trh) % 2) * 1024
                pv = S.op("tensor", lambda t, s_=hv[:, c * 128:(c + 1) * 128],
                          o=pst[:, slot:slot + 128]:
                          t.transpose(o, s_, ident[:, :]), waits=w2, inc="pe")
                trh.append(S.op("vector", lambda v, s_=pst[:, slot:slot + 128],
                                o=h1T[:, c * 2048 + tt * 128:
                                      c * 2048 + (tt + 1) * 128]:
                                v.tensor_copy(o, s_), waits=[("pe", pv)],
                                inc="dv"))
        fch = []
        for tt in range(16):
            for n in range(2):
                ps = psA[:, n * 512:(n + 1) * 512]
                w = [("dv", trh[-1]), ("dv", cw2[-1]), l4] if tt == 0 and n == 0 else []
                if len(fch) >= 2:
                    w.append(("ac", fch[-2]))
                pv = None
                for c in range(DCH):
                    pv = S.op("tensor", lambda t, o=ps,
                              l=h1T[:, c * 2048 + tt * 128: c * 2048 + (tt + 1) * 128],
                              r=w216[:, c * 1024 + n * 512: c * 1024 + n * 512 + 512],
                              st_=(c == 0), sp=False:
                              t.matmul(o, l, r, start=st_, stop=sp),
                              waits=(w if c == 0 else []), inc=None)
                pv = S.op("tensor", lambda t, o=ps, l=ones[:, :],
                          r=b2s[:, n * 512:(n + 1) * 512]:
                          t.matmul(o, l, r, start=False, stop=True), inc="pe")
                fch.append(S.op("scalar", lambda a,
                                o=filt[:, tt * 1024 + n * 512:
                                       tt * 1024 + (n + 1) * 512],
                                i_=ps: a.activation(o, i_, AF.Copy, bias=0.0,
                                                    scale=1.0),
                                waits=[("pe", pv)], inc="ac"))
        sth = []
        for kc in range(KC):
            kb = kc % 2
            lf1 = S.dma("sync", fsb[:, 0:2048],
                        bass.AP(fre_d.ap().tensor, kc * 128,
                                [[2176, 128], [278528, 16], [1, 128]]),
                        [("pe", S.v("pe"))], ch="a3f1")
            lf2 = S.dma("sync", fsb[:, 2048:4096],
                        bass.AP(fim_d.ap().tensor, kc * 128,
                                [[2176, 128], [278528, 16], [1, 128]]),
                        ch="a3f2")
            for comp, lf, pho in ((0, lf1, phre_d), (1, lf2, phim_d)):
                for n in range(2):
                    ps = psB[:, comp * 1024 + n * 512: comp * 1024 + (n + 1) * 512]
                    w = [lf, ("ac", fch[-1])] if n == 0 else [lf]
                    if sth:
                        w.append(sth[-1])
                    pv = None
                    for tc in range(16):
                        pv = S.op("tensor", lambda t, o=ps,
                                  l=fsb[:, comp * 2048 + tc * 128:
                                        comp * 2048 + (tc + 1) * 128],
                                  r=filt[:, tc * 1024 + n * 512:
                                         tc * 1024 + (n + 1) * 512],
                                  st_=(tc == 0), sp=(tc == 15):
                                  t.matmul(o, l, r, start=st_, stop=sp),
                                  waits=(w if tc == 0 else []),
                                  inc=("pe" if tc == 15 else None))
                    slot4 = (len(sth) % 4) * 512
                    wcc = [("pe", pv)]
                    if len(sth) >= 4:
                        wcc.append(sth[-4])
                    cc = S.op("scalar", lambda a,
                              o=stg[:, slot4:slot4 + 512],
                              i_=ps: a.activation(o, i_, AF.Copy, bias=0.0,
                                                  scale=1.0),
                              waits=wcc, inc="ac")
                    sth.append(S.dma("gpsimd",
                                     pho.ap()[kc * 128:(kc + 1) * 128,
                                              n * 512:(n + 1) * 512],
                                     stg[:, slot4:slot4 + 512],
                                     [("ac", cc)], ch=f"a3ps{len(sth) % 2}"))
        S.flush()
    with contextlib.ExitStack() as ctx:
        lnsb = st.alloc(ctx, "a3_ln", [128, 16 * 1024], BF16)
        fsb = st.alloc(ctx, "a3_cf", [128, 2 * 2048], BF16)
        phsb = st.alloc(ctx, "a3_ph", [128, 2 * 1024], BF16)
        yre = st.alloc(ctx, "a3_yre", [128, 17 * 1024], BF16)
        yim = st.alloc(ctx, "a3_yim", [128, 17 * 1024], BF16)
        gsb = st.alloc(ctx, "a3_g", [128, 2 * 2176], BF16)
        sc1 = st.alloc(ctx, "a3_s1", [128, 1024], F32)
        sc2 = st.alloc(ctx, "a3_s2", [128, 1024], F32)
        xe = st.alloc(ctx, "a3_xe", [128, 2 * 1024], F32)
        psX = st.palloc(ctx, "a3_pX", [128, 2 * 1024])
        psO = st.palloc(ctx, "a3_pO", [128, 2 * 512])

        stl = {0: None, 1: None}
        evl = []
        for s in range(SEQ_PER_CORE):
            lnl = S.dma("sync",
                        bass.AP(lnsb[:, :].tensor, 0,
                                [[16384, 128], [1024, 16], [1, 1024]]),
                        bass.AP(ln_dump.tensor, ln_dump.offset + s * 2048 * 1024,
                                [[1024, 128], [131072, 16], [1, 1024]]),
                        [("pe", S.v("pe"))], ch="a3ln")
            ych = []
            for kc in range(KC):
                lf1 = S.dma("sync", fsb[:, 0:2048],
                            bass.AP(fre_d.ap().tensor, kc * 128,
                                    [[2176, 128], [278528, 16], [1, 128]]),
                            [("pe", S.v("pe"))], ch="a3c1")
                lf2 = S.dma("sync", fsb[:, 2048:4096],
                            bass.AP(fim_d.ap().tensor, kc * 128,
                                    [[2176, 128], [278528, 16], [1, 128]]),
                            ch="a3c2")
                lp1 = S.dma("sync", phsb[:, 0:1024],
                            phre_d.ap()[kc * 128:(kc + 1) * 128, :],
                            ([("dv", ych[-1])] if ych else []), ch="a3p1")
                lp2 = S.dma("sync", phsb[:, 1024:2048],
                            phim_d.ap()[kc * 128:(kc + 1) * 128, :], ch="a3p2")
                pvs = []
                for comp, lf in ((0, lf1), (1, lf2)):
                    for n in range(2):
                        ps = psX[:, comp * 1024 + n * 512:
                                 comp * 1024 + (n + 1) * 512]
                        w = [lf] + ([lnl] if kc == 0 and comp == 0 and n == 0
                                    else [])
                        if ych:
                            w.append(("dv", ych[-1]))
                        pv = None
                        for tc in range(16):
                            pv = S.op("tensor", lambda t, o=ps,
                                      l=fsb[:, comp * 2048 + tc * 128:
                                            comp * 2048 + (tc + 1) * 128],
                                      r=lnsb[:, tc * 1024 + n * 512:
                                             tc * 1024 + (n + 1) * 512],
                                      st_=(tc == 0), sp=(tc == 15):
                                      t.matmul(o, l, r, start=st_, stop=sp),
                                      waits=(w if tc == 0 else []),
                                      inc=("pe" if tc == 15 else None))
                        pvs.append(pv)
                d1 = S.op("vector", lambda v, o=sc1[:, :],
                          a=psX[:, 0:1024], b=phsb[:, 0:1024]:
                          v.tensor_tensor(o, a, b, ALU.mult),
                          waits=[("pe", pvs[-1]), lp1], inc="dv")
                d2 = S.op("vector", lambda v, o=sc2[:, :],
                          a=psX[:, 1024:2048], b=phsb[:, 1024:2048]:
                          v.tensor_tensor(o, a, b, ALU.mult),
                          waits=[("dv", d1), lp2], inc="dv")
                d3 = S.op("vector", lambda v, o=yre[:, kc * 1024:(kc + 1) * 1024],
                          a=sc1[:, :], b=sc2[:, :]:
                          v.tensor_tensor(o, a, b, ALU.subtract),
                          waits=[("dv", d2)], inc="dv")
                d4 = S.op("vector", lambda v, o=sc1[:, :],
                          a=psX[:, 0:1024], b=phsb[:, 1024:2048]:
                          v.tensor_tensor(o, a, b, ALU.mult),
                          waits=[("dv", d3)], inc="dv")
                d5 = S.op("vector", lambda v, o=sc2[:, :],
                          a=psX[:, 1024:2048], b=phsb[:, 0:1024]:
                          v.tensor_tensor(o, a, b, ALU.mult),
                          waits=[("dv", d4)], inc="dv")
                d6 = S.op("vector", lambda v, o=yim[:, kc * 1024:(kc + 1) * 1024],
                          a=sc1[:, :], b=sc2[:, :]:
                          v.tensor_tensor(o, a, b, ALU.add),
                          waits=[("dv", d5)], inc="dv")
                ych.append(d6)
            for tt in range(16):
                b = tt % 2
                lg1 = S.dma("sync", gsb[:, 0:2176],
                            bass.AP(gre_d.ap().tensor, tt * 128,
                                    [[2048, 128], [262144, 17], [1, 128]]),
                            [("pe", S.v("pe"))], ch="a3g1")
                lg2 = S.dma("sync", gsb[:, 2176:4352],
                            bass.AP(gim_d.ap().tensor, tt * 128,
                                    [[2048, 128], [262144, 17], [1, 128]]),
                            ch="a3g2")
                xev = xe[:, b * 1024:(b + 1) * 1024]
                wv = []
                if stl[b] is not None:
                    wv.append(stl[b])
                lx = S.dma("scalar", xev,
                           x_ap[s * 2048 + tt * 128: s * 2048 + (tt + 1) * 128, :],
                           wv, ch=f"a3xl{b}")
                evs = []
                for n in range(2):
                    ps = psO[:, n * 512:(n + 1) * 512]
                    w = [lg1, lg2]
                    if tt == 0 and n == 0:
                        w.append(("dv", ych[-1]))
                    if evl:
                        w.append(("dv", evl[-1][0]))
                    pv = None
                    for kc in range(KC):
                        pv = S.op("tensor", lambda t, o=ps,
                                  l=gsb[:, kc * 128:(kc + 1) * 128],
                                  r=yre[:, kc * 1024 + n * 512:
                                         kc * 1024 + (n + 1) * 512],
                                  st_=(kc == 0), sp=False:
                                  t.matmul(o, l, r, start=st_, stop=sp),
                                  waits=(w if kc == 0 else []), inc=None)
                        last = (kc == KC - 1)
                        pv = S.op("tensor", lambda t, o=ps,
                                  l=gsb[:, 2176 + kc * 128:
                                        2176 + (kc + 1) * 128],
                                  r=yim[:, kc * 1024 + n * 512:
                                         kc * 1024 + (n + 1) * 512],
                                  st_=False, sp=last:
                                  t.matmul(o, l, r, start=st_, stop=sp),
                                  inc=("pe" if last else None))
                    ev = S.op("vector", lambda v, o=xev[:, n * 512:(n + 1) * 512],
                              p_=ps: v.tensor_tensor(o, o, p_, ALU.add),
                              waits=[("pe", pv), lx], inc="dv")
                    evs.append(ev)
                evl.append((evs[-1], S.v("pe")))
                stl[b] = S.dma("gpsimd",
                               x_ap[s * 2048 + tt * 128: s * 2048 + (tt + 1) * 128, :],
                               xev, [("dv", evs[-1])], ch=f"a3xs{b}")
        S.flush()


def emit_moe_cache(st, srcs, caches, pieces, nexp, ffn, dscale=0.5, tag="wc"):
    S, sb = st.S, st.sb
    NP = len(pieces)
    NCH = max(pieces)
    wg = st.din(srcs[0], (nexp * 1024, ffn))
    wu = st.din(srcs[1], (nexp * 1024, ffn))
    wd = st.din(srcs[2], (nexp * ffn, 1024))
    cg = st.dint(caches[0], (nexp * NP * 128, NCH * 1024), BF16)
    cu = st.dint(caches[1], (nexp * NP * 128, NCH * 1024), BF16)
    cd = st.dint(caches[2], (nexp * NP * 128, NCH * 1024), BF16)
    offs = [sum(pieces[:p]) for p in range(NP)]
    with contextlib.ExitStack() as ctx:
        stg = st.alloc(ctx, tag + "_stage", [128, 2 * 1024], F32)
        img = st.alloc(ctx, tag + "_img", [128, 2 * NCH * 1024], BF16)
        cast_ac, out_gd = [], []
        kidx = 0
        li = 0
        for e in range(nexp):
            for p in range(NP):
                j = e * NP + p
                NF = pieces[p]
                off = offs[p]
                for kind, wsrc, cdst, scale in ((0, wg, cg, 1.0), (1, wu, cu, 1.0),
                                                (2, wd, cd, dscale)):
                    kb = kidx % 2
                    imgv = img[:, kb * NCH * 1024:(kb + 1) * NCH * 1024]
                    ncold = 8 if kind < 2 else NF
                    for c in range(ncold):
                        cb = li % 2
                        li += 1
                        if kind < 2:
                            r0 = e * 1024 + c * 128
                            src_ap = wsrc.ap()[r0:r0 + 128,
                                               off * 128:(off + NF) * 128]
                            dst_ap = imgv[:, c * NF * 128:(c + 1) * NF * 128]
                            ww = NF * 128
                        else:
                            r0 = e * ffn + (off + c) * 128
                            src_ap = wsrc.ap()[r0:r0 + 128, :]
                            dst_ap = imgv[:, c * 1024:(c + 1) * 1024]
                            ww = 1024
                        stgv = stg[:, cb * 1024:cb * 1024 + ww]
                        w = []
                        if len(cast_ac) >= 2:
                            w.append(("ac", cast_ac[-2]))
                        ld = S.dma("sync", stgv, src_ap, w, ch=tag + f"l{cb}")
                        w2 = [ld]
                        if len(out_gd) >= 2:
                            w2.append(out_gd[-2])
                        cast_ac.append(S.op(
                            "scalar", lambda a, i_=stgv, o=dst_ap, sc=scale:
                            a.activation(o, i_, AF.Copy, bias=0.0, scale=sc),
                            waits=w2, inc="ac"))
                    wout = NF * 1024
                    out_gd.append(S.dma("gpsimd",
                                        cdst.ap()[j * 128:(j + 1) * 128, :wout],
                                        imgv[:, :wout], [("ac", cast_ac[-1])],
                                        ch=tag + f"s{kidx % 2}"))
                    kidx += 1
        done = [out_gd[-1], out_gd[-2]]
        S.flush()
    return done


def emit_l2_router(st, comb_sb, ah_sb, rw16):
    S, sb = st.S, st.sb
    xT = sb["xT"]
    with contextlib.ExitStack() as ctx:
        el = st.alloc(ctx, "rt_el", [128, 2 * 8], F32)
        mk = st.alloc(ctx, "rt_mk", [128, 2 * 8], F32)
        vv = st.alloc(ctx, "rt_v", [128, 2 * 8], F32)
        psl = st.palloc(ctx, "rt_ps", [128, 1024])
        done_hist = []
        for i in range(NTILE):
            b = i % 2
            ps = psl[:, b * 512:b * 512 + 8]
            elv = el[:, b * 8:(b + 1) * 8]
            mkv = mk[:, b * 8:(b + 1) * 8]
            v0 = vv[:, b * 8:(b + 1) * 8]
            w0 = [("dv", done_hist[-2])] if len(done_hist) >= 2 else []
            pv = None
            for c in range(DCH):
                pv = S.op("tensor", lambda t, o=ps,
                          l=xT[:, c * 4096 + i * 128: c * 4096 + (i + 1) * 128],
                          r=rw16[:, c * 8:(c + 1) * 8],
                          st_=(c == 0), sp=(c == DCH - 1):
                          t.matmul(o, l, r, start=st_, stop=sp),
                          waits=(w0 if c == 0 else []),
                          inc=("pe" if c == DCH - 1 else None))
            def vop(fn, waits, inc="dv"):
                return S.op("vector", fn, waits=waits, inc=inc)
            d = vop(lambda v, o=v0[:, 0:1], p_=ps:
                    v.reduce_max(o, p_, AX.X, negate=True), [("pe", pv)])
            a1 = S.op("scalar", lambda a, o=elv, p_=ps, b_=v0[:, 0:1],
                      s_=v0[:, 1:2]:
                      a.activation(o, p_, AF.Exp, bias=b_, accum_out=s_),
                      waits=[("dv", d)], inc="ac")
            d = vop(lambda v, o=v0[:, 2:3], i_=v0[:, 1:2]:
                    v.reciprocal(o, i_), [("ac", a1)])
            d = vop(lambda v, o=elv, s_=v0[:, 2:3]:
                    v.tensor_scalar(o, o, s_, None, ALU.mult), [("dv", d)])
            d = vop(lambda v, o=v0[:, 3:4], i_=elv:
                    v.reduce_max(o, i_, AX.X), [("dv", d)])
            d = vop(lambda v, o=mkv, i_=elv, m_=v0[:, 3:4]:
                    v.tensor_scalar(o, i_, m_, None, ALU.is_ge), [("dv", d)])
            d = vop(lambda v, o=mkv, i_=elv:
                    v.scalar_tensor_tensor(o, o, -1e3, i_, ALU.mult, ALU.add),
                    [("dv", d)])
            d = vop(lambda v, o=v0[:, 4:5], i_=mkv:
                    v.reduce_max(o, i_, AX.X), [("dv", d)])
            d = vop(lambda v, o=v0[:, 5:6], a_=v0[:, 3:4], b_=v0[:, 4:5]:
                    v.tensor_tensor(o, a_, b_, ALU.add), [("dv", d)])
            d = vop(lambda v, o=v0[:, 6:7], i_=v0[:, 5:6]:
                    v.reciprocal(o, i_), [("dv", d)])
            d = vop(lambda v, o=mkv, i_=elv, m_=v0[:, 4:5]:
                    v.tensor_scalar(o, i_, m_, None, ALU.is_ge), [("dv", d)])
            d = vop(lambda v, o=mkv, i_=elv:
                    v.tensor_tensor(o, o, i_, ALU.mult), [("dv", d)])
            d = vop(lambda v, o=comb_sb[:, i * 8:(i + 1) * 8], i_=mkv,
                    r_=v0[:, 6:7], a_=ah_sb[:, 0:1]:
                    v.tensor_scalar(o, i_, r_, a_, ALU.mult, ALU.mult),
                    [("dv", d)])
            done_hist.append(d)
        S.flush()


def emit_moe_ffn(st, x_ap, ln_loads, caches, pieces, nexp=8, combine=None,
                 cache_gd=0, tag="f1"):
    S, sb = st.S, st.sb
    xT = sb["xT"]
    cg, cu, cd = (st.dram[n] for n in caches)
    NP = len(pieces)
    NCH = max(pieces)
    with contextlib.ExitStack() as ctx:
        wgs = st.alloc(ctx, tag + "_wg", [128, 2 * NCH * 1024], BF16)
        wus = st.alloc(ctx, tag + "_wu", [128, 2 * NCH * 1024], BF16)
        wds = st.alloc(ctx, tag + "_wd", [128, 2 * NCH * 1024], BF16)
        hsb = st.alloc(ctx, tag + "_h", [128, NCH * 512], BF16)
        sg = st.alloc(ctx, tag + "_sg", [128, 2 * 512], BF16)
        xe = st.alloc(ctx, tag + "_xe", [128, 2 * 1024], F32)
        psg = st.palloc(ctx, tag + "_pg", [128, 2 * 512])
        psu = st.palloc(ctx, tag + "_pu", [128, 2 * 512])
        psy = st.palloc(ctx, tag + "_py", [128, 2 * 512])

        pe_done, gd_out = {}, {}
        last_buf_store = {0: None, 1: None}
        silu_ac, hmul_dv, yevac_dv = [], [], []
        last_gd = None
        for e in range(nexp):
            for p in range(NP):
                j = e * NP + p
                NF = pieces[p]
                jb = j % 2
                wgv = wgs[:, jb * NCH * 1024:jb * NCH * 1024 + NF * 1024]
                wuv = wus[:, jb * NCH * 1024:jb * NCH * 1024 + NF * 1024]
                wdv = wds[:, jb * NCH * 1024:jb * NCH * 1024 + NF * 1024]
                w = list(cache_gd) if (j == 0 and cache_gd) else []
                if j >= 2:
                    w.append(("pe", pe_done[j - 2]))
                lg = S.dma("sync", wgv, cg.ap()[j * 128:(j + 1) * 128, :NF * 1024],
                           w, ch=tag + f"wg{jb}")
                lu = S.dma("sync", wuv, cu.ap()[j * 128:(j + 1) * 128, :NF * 1024],
                           ch=tag + f"wu{jb}")
                ld = S.dma("sync", wdv, cd.ap()[j * 128:(j + 1) * 128, :NF * 1024],
                           ch=tag + f"wd{jb}")
                for s in range(8):
                    for f in range(NF):
                        fb = f % 2
                        gps = psg[:, fb * 512:(fb + 1) * 512]
                        ups = psu[:, fb * 512:(fb + 1) * 512]
                        w_g = [lg] if (s == 0 and f == 0) else []
                        if len(silu_ac) >= 2:
                            w_g.append(("ac", silu_ac[-2]))
                        w_u = [lu] if (s == 0 and f == 0) else []
                        if len(hmul_dv) >= 2:
                            w_u.append(("dv", hmul_dv[-2]))
                        peg = peu = None
                        for c in range(DCH):
                            peg = S.op(
                                "tensor", lambda t, o=gps,
                                l=wgv[:, c * NF * 128 + f * 128: c * NF * 128 + (f + 1) * 128],
                                r=xT[:, c * 4096 + s * 512: c * 4096 + (s + 1) * 512],
                                st_=(c == 0), sp=(c == DCH - 1):
                                t.matmul(o, l, r, start=st_, stop=sp),
                                waits=(w_g if c == 0 else []),
                                inc=("pe" if c == DCH - 1 else None))
                        for c in range(DCH):
                            peu = S.op(
                                "tensor", lambda t, o=ups,
                                l=wuv[:, c * NF * 128 + f * 128: c * NF * 128 + (f + 1) * 128],
                                r=xT[:, c * 4096 + s * 512: c * 4096 + (s + 1) * 512],
                                st_=(c == 0), sp=(c == DCH - 1):
                                t.matmul(o, l, r, start=st_, stop=sp),
                                waits=(w_u if c == 0 else []),
                                inc=("pe" if c == DCH - 1 else None))
                        sgv = sg[:, fb * 512:(fb + 1) * 512]
                        w_s = [("pe", peg)]
                        if len(hmul_dv) >= 2:
                            w_s.append(("dv", hmul_dv[-2]))
                        silu_ac.append(S.op("scalar", lambda a, o=sgv, i_=gps:
                                            a.activation(o, i_, AF.Silu),
                                            waits=w_s, inc="ac"))
                        hmul_dv.append(S.op(
                            "vector", lambda v, o=hsb[:, f * 512:(f + 1) * 512],
                            a=sgv, b=ups: v.tensor_tensor(o, a, b, ALU.mult),
                            waits=[("ac", silu_ac[-1]), ("pe", peu)], inc="dv"))
                    h_dv = S.v("dv")
                    for m in range(4):
                        idx = s * 4 + m
                        rows = slice(idx * 128, (idx + 1) * 128)
                        xb = idx % 2
                        xev = xe[:, xb * 1024:(xb + 1) * 1024]
                        wx = []
                        if (j - 1, idx) in gd_out:
                            wx.append(gd_out[(j - 1, idx)])
                        if last_buf_store[xb] is not None:
                            wx.append(last_buf_store[xb])
                        lx = S.dma("scalar", xev, x_ap[rows, :], wx,
                                   ch=tag + f"xl{idx % 2}")
                        evs = []
                        for n in range(2):
                            yb = (m * 2 + n) % 2
                            yps = psy[:, yb * 512:(yb + 1) * 512]
                            wy = []
                            if m == 0 and n == 0:
                                wy.append(("dv", h_dv))
                                if s == 0:
                                    wy.append(ld)
                            if len(yevac_dv) >= 2:
                                wy.append(("dv", yevac_dv[-2]))
                            pey = None
                            for f in range(NF):
                                pey = S.op(
                                    "tensor", lambda t, o=yps,
                                    l=hsb[:, f * 512 + m * 128: f * 512 + (m + 1) * 128],
                                    r=wdv[:, f * 1024 + n * 512: f * 1024 + n * 512 + 512],
                                    st_=(f == 0), sp=(f == NF - 1):
                                    t.matmul(o, l, r, start=st_, stop=sp),
                                    waits=(wy if f == 0 else []),
                                    inc=("pe" if f == NF - 1 else None))
                            if combine is None:
                                yevac_dv.append(S.op(
                                    "vector",
                                    lambda v, o=xev[:, n * 512:(n + 1) * 512], p_=yps:
                                    v.tensor_tensor(o, o, p_, ALU.add),
                                    waits=[("pe", pey), lx], inc="dv"))
                            else:
                                comb_sb, stride = combine
                                sc = comb_sb[:, idx * stride + e: idx * stride + e + 1]
                                yevac_dv.append(S.op(
                                    "vector",
                                    lambda v, o=xev[:, n * 512:(n + 1) * 512], p_=yps,
                                    s_=sc:
                                    v.scalar_tensor_tensor(o, p_, s_, o,
                                                           ALU.mult, ALU.add),
                                    waits=[("pe", pey), lx], inc="dv"))
                            evs.append(yevac_dv[-1])
                        wout = [("dv", evs[-1])]
                        if j == 0:
                            wout += ln_loads
                        gd_out[(j, idx)] = S.dma("gpsimd", x_ap[rows, :], xev,
                                                 wout, ch=tag + f"xs{idx % 2}")
                        last_buf_store[xb] = gd_out[(j, idx)]
                        last_gd = gd_out[(j, idx)]
                pe_done[j] = S.v("pe")
        S.flush()
    return last_gd


def build_stage(subblocks):
    st = Stage(subblocks)
    S = st.S
    xin = st.din("xin", (NT, D))
    xout = st.dout("xout", (NT, D))
    xres = st.dint("xres", (NT, D))
    with contextlib.ExitStack() as ctx:
        S.make_sems(ctx)
        st.alloc(ctx, "xT", [128, 8 * 4096], BF16)
        st.alloc(ctx, "ident", [128, 128], BF16)
        ide = st.din("c_ident16", (128, 128), BF16)
        S.dma("sync", st.sb["ident"][:, :], ide.ap(), ch="misc")
        st.x_ready = [S.dma("sync", xres.ap(), xin.ap(), ch="xcp")]
        S.flush(drain_dma=True)

        need_l1 = any(s == "L1f" for s in subblocks)
        need_l2 = any(s == "L2f" for s in subblocks)
        need_l0 = any(s == "L0f" for s in subblocks)
        need_l3 = any(s == "L3f" for s in subblocks)
        cache_gd = cache_gd2 = cache_gd0 = cache_gd3 = 0
        acg = {}
        if any(s == "L0a" for s in subblocks):
            acg["a0q"] = emit_plain_cache(st, "w_a0_qkv", "c_a0_qkv",
                                          1024, 3072, "ca0q")
            acg["a0o"] = emit_plain_cache(st, "w_a0_wo", "c_a0_wo",
                                          1024, 1024, "ca0o")
        if any(s == "L1a" for s in subblocks):
            acg["a1q"] = emit_plain_cache(st, "w_a1_qkv", "c_a1_qkv",
                                          1024, 3072, "ca1q")
            acg["a1o"] = emit_plain_cache(st, "w_a1_wo", "c_a1_wo",
                                          1024, 1024, "ca1o")
        if any(s == "L2a" for s in subblocks):
            g1 = emit_plain_cache(st, "w_a2_plT", "c_a2_plT", 1024, 1024, "cpl")
            g2 = emit_plain_cache(st, "w_a2_wlT", "c_a2_wlT", 1024, 3072, "cwl")
            acg["a2lq"] = emit_compose_cache(st, "c_a2_plT", "c_a2_wlT",
                                             "c_a2l_qkv", g1 + g2, "cmpl")
            g3 = emit_plain_cache(st, "w_a2_pgT", "c_a2_pgT", 1024, 1024, "cpg")
            g4 = emit_plain_cache(st, "w_a2_wgT", "c_a2_wgT", 1024, 3072, "cwg")
            acg["a2gq"] = emit_compose_cache(st, "c_a2_pgT", "c_a2_wgT",
                                             "c_a2g_qkv", g3 + g4, "cmpg")
            acg["a2lo"] = emit_plain_cache(st, "w_a2_wlo", "c_a2l_wo",
                                           1024, 1024, "clo")
            acg["a2go"] = emit_plain_cache(st, "w_a2_wgo", "c_a2g_wo",
                                           1024, 1024, "cgo")
        if need_l0:
            cache_gd0 = emit_l0_cache(st)
        if need_l3:
            cache_gd3 = emit_l3_cache(st)
        if need_l1:
            cache_gd = emit_moe_cache(st, ("w_l1_g", "w_l1_u", "w_l1_d"),
                                      ("c_l1_g", "c_l1_u", "c_l1_d"),
                                      [8, 8, 8, 8], 8, 4096, tag="wc1")
        if need_l2:
            cache_gd2 = emit_moe_cache(st, ("w_l2_g", "w_l2_u", "w_l2_d"),
                                       ("c_l2_g", "c_l2_u", "c_l2_d"),
                                       [6, 6, 5, 5], 8, 2816, tag="wc2")

        for name in subblocks:
            if name == "L3a":
                lnd = st.dint("ln16_s", (NT, 1024), BF16)
                emit_ln_to_xT(st, xres.ap(), mode="dump", wb=lnd.ap())
                emit_l3a(st, xres.ap(), lnd.ap())
            elif name == "L2a":
                qkvd = st.dint("qkv_s", (NT, 3072), BF16)
                fused = st.dint("fused_l2", (NT, D))
                with contextlib.ExitStack() as c3:
                    st.alloc(c3, "l2a_w", [128, 2], F32)
                    st.alloc(c3, "l2a_blo", [128, 1024], F32)
                    st.alloc(c3, "l2a_bgo", [128, 1024], F32)
                    emit_l2_fusionw(st)
                    emit_ln_to_xT(st, xres.ap())
                    emit_attn01_qkv(st, "l2", qkvd.ap(), "c_a2l_qkv",
                                    cache_gd=acg["a2lq"],
                                    bias_name="c_l2_blin", tag="a2l")
                    emit_attn2_core(st, qkvd.ap(), 128, fused.ap(),
                                    "c_a2l_wo", True, 0)
                    emit_attn01_qkv(st, "l2", qkvd.ap(), "c_a2g_qkv",
                                    cache_gd=acg["a2gq"],
                                    bias_name="c_l2_bgin", tag="a2g")
                    emit_attn2_core(st, qkvd.ap(), 1024, fused.ap(),
                                    "c_a2g_wo", False, 1)
                    emit_l2a_final(st, xres.ap(), fused.ap())
            elif name in ("L0a", "L1a"):
                var = "l0" if name == "L0a" else "l1"
                qkvd = st.dint("qkv_s", (NT, 3072), BF16)
                emit_ln_to_xT(st, xres.ap())
                emit_attn01_qkv(st, var, qkvd.ap(),
                                "c_a%s_qkv" % var[1],
                                cache_gd=acg.get("a%sq" % var[1], 0))
                emit_attn01_core(st, var, qkvd.ap(), xres.ap(),
                                 "c_a%s_wo" % var[1])
            elif name == "L3f":
                hbuf = st.dint("h_l3", (NT, D))
                emit_ln_to_xT(st, xres.ap(), mode="l3", wb=hbuf.ap())
                emit_l3_ffn(st, xres.ap(), hbuf.ap(), cache_gd=cache_gd3)
            elif name == "L0f":
                ln_loads = emit_ln_to_xT(st, xres.ap(), mode="l0")
                emit_l0_main(st, xres.ap(), ln_loads, cache_gd=cache_gd0)
            elif name == "L1f":
                ln_loads = emit_ln_to_xT(st, xres.ap())
                emit_moe_ffn(st, xres.ap(), ln_loads,
                             ("c_l1_g", "c_l1_u", "c_l1_d"), pieces=[8, 8, 8, 8],
                             cache_gd=cache_gd, tag="f1")
            elif name == "L2f":
                with contextlib.ExitStack() as c2:
                    comb = st.alloc(c2, "l2_comb", [128, NTILE * 8], F32)
                    ah = st.alloc(c2, "l2_ah", [128, 1], F32)
                    rw16 = st.alloc(c2, "l2_rw", [128, 64], BF16)
                    wbc = st.alloc(c2, "w_bcast", [128, 1024], BF16)
                    wbcf = st.alloc(c2, "w_bcastf", [128, 1024], F32)
                    asb = st.alloc(c2, "l2_asb", [1, 1], F32)
                    ones = st.alloc(c2, "ones_f32", [1, 128], F32)
                    rwf = st.alloc(c2, "l2_rwf", [128, 64], F32)
                    pa = st.palloc(c2, "l2_pa", [128, 512])
                    rw_d = st.din("w_l2_router", (128, 64))
                    nw_d = st.din("c_l2_normw", (128, 1024))
                    al_d = st.din("s_l2_alpha", (1, 1))
                    on_d = st.din("c_ones_f32", (1, 128))
                    l1 = S.dma("sync", rwf[:, :], rw_d.ap(), ch="l2c")
                    l2_ = S.dma("sync", wbcf[:, :], nw_d.ap(), ch="l2c2")
                    S.op("vector", lambda v: v.tensor_copy(wbc[:, :], wbcf[:, :]),
                         waits=[l2_], inc="dv")
                    l3 = S.dma("sync", asb[:, :], al_d.ap(), ch="l2c3")
                    l4 = S.dma("sync", ones[:, :], on_d.ap(), ch="l2c4")
                    S.op("vector", lambda v: v.tensor_copy(rw16[:, :], rwf[:, :]),
                         waits=[l1], inc="dv")
                    a_ = S.op("vector", lambda v: v.tensor_scalar(
                        asb[:, :], asb[:, :], -0.5, 2.0, ALU.max, ALU.min),
                        waits=[l3], inc="dv")
                    p_ = S.op("tensor", lambda t: t.matmul(
                        pa[:, 0:1], ones[:, :], asb[:, :]),
                        waits=[("dv", a_), l4], inc="pe")
                    S.op("scalar", lambda a: a.activation(
                        ah[:, :], pa[:, 0:1], AF.Copy, bias=0.0, scale=0.5),
                        waits=[("pe", p_)], inc="ac")
                    S.flush()
                    ln_loads = emit_ln_to_xT(st, xres.ap(), mode="l2",
                                             wb=xres.ap())
                    emit_l2_router(st, comb, ah, rw16)
                    emit_moe_ffn(st, xres.ap(), ln_loads,
                                 ("c_l2_g", "c_l2_u", "c_l2_d"),
                                 pieces=[6, 6, 5, 5], combine=(comb, 8),
                                 cache_gd=cache_gd2, tag="f2")
            else:
                raise NotImplementedError(name)

        allw = [(ch, S.cnt[ch]) for ch in list(S.cnt) if ch.startswith("D_")
                and S.cnt[ch] > 0]
        allw += [(s, S.v(s)) for s in ("pe", "dv", "ac") if S.v(s) > 0]
        last = S.dma("sync", xout.ap(), xres.ap(), allw, ch="xcp")
        S.op("sync", lambda e: e.nop(), waits=[last])
        S.flush()
    return st


ALL = ["L0f", "L0a", "L0f", "L1f", "L1a", "L1f", "L2f", "L2a", "L2f", "L3f", "L3a", "L3f"]
STAGES = [[s] for s in ALL]


def kernel(**inputs):
    need = set(s for stage in STAGES for s in stage)
    shards = shard_x(inputs)
    host = host_prep(inputs, need)
    x_cur = shards
    for subblocks in STAGES:
        st = build_stage(subblocks)
        in_maps = []
        for c in range(NCORES):
            m = {"xin": x_cur[c]}
            for nm in st.ext_inputs:
                if nm != "xin":
                    m[nm] = host[nm]
            in_maps.append(m)
        res = run_bass_kernel_spmd(st.nc, in_maps, core_ids=list(range(NCORES)))
        x_cur = [np.asarray(res.results[c]["xout"]) for c in range(NCORES)]
    out = np.concatenate([np.asarray(x, np.float32).reshape(SEQ_PER_CORE, T, D)
                          for x in x_cur], axis=0)
    return np.ascontiguousarray(out)


# version 02
# speedup vs baseline: 1.0834x; 1.0834x over previous
import contextlib
import numpy as np
import ml_dtypes
import concourse.bass as bass
import concourse.mybir as mybir
from concourse.bass_utils import run_bass_kernel_spmd

F32 = mybir.dt.float32
BF16 = mybir.dt.bfloat16
I32 = mybir.dt.int32
AF = mybir.ActivationFunctionType
ALU = mybir.AluOpType
AX = mybir.AxisListType

NCORES = 8
T = 2048
SEQ_PER_CORE = 2
NT = T * SEQ_PER_CORE
D = 1024
NTILE = NT // 128
DCH = D // 128

ENGINES = ("tensor", "vector", "scalar", "gpsimd", "sync")
DMA_SEM = {"sync": "sd", "scalar": "ad", "gpsimd": "gd", "vector": "vd"}


class Sched:

    def __init__(self, nc):
        self.nc = nc
        self.sem_names = ["pe", "dv", "ac", "po"]
        self.sems = {}
        self.cnt = {n: 0 for n in self.sem_names}
        self.ops = {e: [] for e in ENGINES}
        self.chan_eng = {}
        self.ctx = None

    NPOOL = {"sync": 20, "scalar": 8, "gpsimd": 12, "vector": 4}

    def make_sems(self, ctx):
        self.ctx = ctx
        for n in self.sem_names:
            self.sems[n] = ctx.enter_context(self.nc.semaphore(n))
        self.pool_next = {e: 0 for e in self.NPOOL}
        self.chan_map = {}

    def _chan(self, name, qeng):
        key = (qeng, name)
        if key in self.chan_map:
            return self.chan_map[key]
        np_ = self.NPOOL[qeng]
        i = self.pool_next[qeng]
        assert i < np_, f"too many {qeng} DMA channels in one epoch"
        hname = f"H_{qeng}_{i}"
        if hname not in self.sems:
            self.sems[hname] = self.ctx.enter_context(self.nc.semaphore(hname))
            self.cnt[hname] = 0
        self.pool_next[qeng] = i + 1
        self.chan_map[key] = hname
        return hname

    def op(self, eng, fn, waits=(), inc=None, n=1):
        for s, v in waits:
            assert v <= self.cnt[s], f"wait on future: {s} {v} > {self.cnt[s]}"
        out = None
        if inc is not None:
            self.cnt[inc] += n
            out = self.cnt[inc]
        self.ops[eng].append((fn, [(s, v) for s, v in waits if v > 0], inc, n))
        return out

    def dma(self, qeng, dst, src, waits=(), ch="misc"):
        ch = self._chan(ch, qeng)
        self.chan_eng[ch] = qeng
        v = self.op(qeng, lambda e, d=dst, s=src: e.dma_start(d, s),
                    waits, inc=ch, n=16)
        return (ch, v)

    def v(self, sem):
        return self.cnt[sem]

    def chv(self, ch):
        for (qe, nm), h in self.chan_map.items():
            if nm == ch:
                return (h, self.cnt[h])
        raise KeyError(ch)

    def flush(self, drain_dma=True):
        if drain_dma:
            per_eng = {}
            for ch, eng in self.chan_eng.items():
                if self.cnt.get(ch, 0) > 0:
                    per_eng.setdefault(eng, []).append((ch, self.cnt[ch]))
            for eng, waits in per_eng.items():
                self.op(eng, lambda e: e.nop(), waits=waits)
            self.chan_eng = {}
        self.chan_map = {}
        self.pool_next = {e: 0 for e in self.NPOOL}
        ops = self.ops
        self.ops = {e: [] for e in ENGINES}
        if not any(ops.values()):
            return
        sems = self.sems
        with self.nc.Block() as blk:
            for eng_name in ENGINES:
                lst = ops[eng_name]
                if not lst:
                    continue

                def section(eng, lst=lst):
                    for fn, waits, inc, n in lst:
                        for s, v in waits:
                            eng.wait_ge(sems[s], v)
                        ins = fn(eng)
                        if inc is not None:
                            ins.then_inc(sems[inc], n)
                getattr(blk, eng_name)(section)


def host_prep(inputs, need):
    f32 = np.float32

    def T_(a):
        return np.ascontiguousarray(np.asarray(a, f32).T)

    d = {}
    d["c_ident16"] = np.eye(128, dtype=ml_dtypes.bfloat16)
    if "L0f" in need:
        d["w_l0_g"] = np.ascontiguousarray(np.asarray(inputs["L0_ffn_w_gate"], f32))
        d["w_l0_u"] = np.ascontiguousarray(np.asarray(inputs["L0_ffn_w_up"], f32))
        d["w_l0_d"] = np.ascontiguousarray(np.asarray(inputs["L0_ffn_w_down"], f32))
    if "L0a" in need or "L1a" in need:
        d["c_causal_mask"] = np.triu(np.full((128, 128), -1e30, f32), 1)
        d["c_ones_f32"] = np.ones((1, 128), f32)
    if "L0a" in need:
        d["w_a0_qkv"] = T_(inputs["L0_attn_wqkv"])
        d["w_a0_wo"] = T_(inputs["L0_attn_wo"])
        t = np.arange(2048, dtype=np.float64)
        inv = 1.0 / (10000.0 ** (np.arange(0, 64, 2, dtype=np.float64) / 64))
        fr = t[:, None] * inv[None, :]
        cos = np.cos(fr).astype(f32)
        sin = np.sin(fr).astype(f32)
        d["c_rope_cos"] = np.ascontiguousarray(
            cos.reshape(16, 128, 32).transpose(1, 0, 2).reshape(128, 512))
        d["c_rope_sin"] = np.ascontiguousarray(
            sin.reshape(16, 128, 32).transpose(1, 0, 2).reshape(128, 512))
        qn = np.tile(np.asarray(inputs["L0_attn_q_norm_w"], f32), 16)
        kn = np.tile(np.asarray(inputs["L0_attn_k_norm_w"], f32), 16)
        d["c_qknw"] = np.ascontiguousarray(np.broadcast_to(
            np.concatenate([qn, kn]), (128, 2048)))
    if "L1a" in need:
        d["w_a1_qkv"] = np.ascontiguousarray(np.asarray(inputs["L1_attn_w_qkv"], f32))
        d["w_a1_wo"] = np.ascontiguousarray(np.asarray(inputs["L1_attn_w_out"], f32))
        from math import log2
        start = 2 ** (-2 ** (-(log2(16) - 3)))
        slopes = np.array([start * start ** i for i in range(16)], f32)
        k = np.arange(2048, dtype=f32)
        d["c_alibi"] = np.ascontiguousarray(8.0 * slopes[:, None] * k[None, :])
        d["c_l1_bob"] = np.ascontiguousarray(np.broadcast_to(
            np.asarray(inputs["L1_attn_b_out"], f32), (128, 1024)))
    if "L3a" in need:
        d["c_ones_f32"] = np.ones((1, 128), f32)
        d["w_a3_w1"] = np.ascontiguousarray(np.asarray(inputs["L3_attn_W1"], f32))
        d["w_a3_w2"] = np.ascontiguousarray(np.asarray(inputs["L3_attn_W2"], f32))
        d["c_a3_b1"] = np.asarray(inputs["L3_attn_b1"], f32).reshape(1, 1024)
        d["c_a3_b2"] = np.asarray(inputs["L3_attn_b2"], f32).reshape(1, 1024)
        pos = np.arange(2048, dtype=np.float64)[:, None]
        div = np.exp(np.arange(0, 64, 2, dtype=np.float64) * (-np.log(10000.0) / 64))
        ang = pos * div
        pe_ = np.zeros((2048, 64), np.float32)
        pe_[:, 0::2] = np.sin(ang)
        pe_[:, 1::2] = np.cos(ang)
        d["c_posT"] = np.ascontiguousarray(pe_.T)
        N_ = 4096
        k_ = np.arange(2176, dtype=np.float64)
        t_ = np.arange(2048, dtype=np.float64)
        th = 2.0 * np.pi * np.outer(t_, k_) / N_
        fre = np.cos(th); fim = -np.sin(th)
        fre[:, 2049:] = 0.0; fim[:, 2049:] = 0.0
        import ml_dtypes as _mld
        d["t_fre"] = fre.astype(_mld.bfloat16)
        d["t_fim"] = fim.astype(_mld.bfloat16)
        alp = np.full(2176, 2.0); alp[0] = 1.0; alp[2048] = 1.0; alp[2049:] = 0.0
        thg = 2.0 * np.pi * np.outer(k_, t_) / N_
        gre = (alp[:, None] * np.cos(thg)) / N_
        gim = (-alp[:, None] * np.sin(thg)) / N_
        d["t_gre"] = gre.astype(_mld.bfloat16)
        d["t_gim"] = gim.astype(_mld.bfloat16)
    if "L2a" in need:
        d["c_l2_mask"] = 8.0 * np.triu(np.ones((128, 128), f32), 1)
        d["c_ones_f32"] = np.ones((1, 128), f32)
        d["w_a2_plT"] = T_(inputs["L2_attn_proj_local"])
        d["w_a2_pgT"] = T_(inputs["L2_attn_proj_global"])
        d["w_a2_wlT"] = T_(inputs["L2_attn_Wl_in"])
        d["w_a2_wgT"] = T_(inputs["L2_attn_Wg_in"])
        d["w_a2_wlo"] = T_(inputs["L2_attn_Wl_out"])
        d["w_a2_wgo"] = T_(inputs["L2_attn_Wg_out"])
        d["c_l2_blin"] = np.asarray(inputs["L2_attn_bl_in"], f32).reshape(1, 3072)
        d["c_l2_bgin"] = np.asarray(inputs["L2_attn_bg_in"], f32).reshape(1, 3072)
        d["c_l2_blo"] = np.ascontiguousarray(np.broadcast_to(
            np.asarray(inputs["L2_attn_bl_out"], f32), (128, 1024)))
        d["c_l2_bgo"] = np.ascontiguousarray(np.broadcast_to(
            np.asarray(inputs["L2_attn_bg_out"], f32), (128, 1024)))
        d["c_l2a_gam"] = np.ascontiguousarray(np.broadcast_to(
            np.asarray(inputs["L2_attn_ln_gamma"], f32), (128, 1024)))
        d["c_l2a_bet"] = np.ascontiguousarray(np.broadcast_to(
            np.asarray(inputs["L2_attn_ln_beta"], f32), (128, 1024)))
        d["L2_attn_fusion_w"] = np.asarray(inputs["L2_attn_fusion_w"],
                                           f32).reshape(1, 2)
    if "L3f" in need:
        q = np.asarray(inputs["L3_ffn_qweights"], np.int32)
        d["w_l3_q"] = np.ascontiguousarray(q.transpose(0, 2, 1)).reshape(6144, 1024)
        sc = np.asarray(inputs["L3_ffn_scales"], f32)
        scr = np.repeat(sc, 16, axis=2)
        d["w_l3_sc"] = np.ascontiguousarray(scr.transpose(0, 2, 1)).reshape(6144, 1024)
        la = np.asarray(inputs["L3_ffn_lora_a"], f32)
        laT = la.transpose(0, 2, 1)
        d["w_l3_la"] = np.ascontiguousarray(
            laT.reshape(6, 8, 128, 32).transpose(0, 2, 1, 3).reshape(6 * 128, 256))
        lb = np.asarray(inputs["L3_ffn_lora_b"], f32)
        d["w_l3_lb"] = np.ascontiguousarray(lb.transpose(0, 2, 1)).reshape(6 * 32, 1024)
        d["w_l3_bias"] = np.ascontiguousarray(np.asarray(inputs["L3_ffn_biases"], f32))
        d["c_l3_gam"] = np.ascontiguousarray(np.broadcast_to(
            np.asarray(inputs["L3_ffn_ln_gamma"], f32)[:, None, :],
            (3, 128, 1024))).reshape(384, 1024)
        d["c_l3_bet"] = np.ascontiguousarray(np.broadcast_to(
            np.asarray(inputs["L3_ffn_ln_beta"], f32)[:, None, :],
            (3, 128, 1024))).reshape(384, 1024)
        d["c_ones_f32"] = np.ones((1, 128), f32)
    if "L2f" in need:
        gu = np.asarray(inputs["L2_ffn_gate_up_weight"], f32)
        guT = np.ascontiguousarray(gu.transpose(0, 2, 1))
        d["w_l2_g"] = np.ascontiguousarray(guT[:, :, :2816]).reshape(8 * 1024, 2816)
        d["w_l2_u"] = np.ascontiguousarray(guT[:, :, 2816:]).reshape(8 * 1024, 2816)
        dw = np.asarray(inputs["L2_ffn_down_weight"], f32)
        d["w_l2_d"] = np.ascontiguousarray(dw.transpose(0, 2, 1)).reshape(8 * 2816, 1024)
        rw = np.asarray(inputs["L2_ffn_router_weight"], f32).T
        d["w_l2_router"] = np.ascontiguousarray(
            rw.reshape(8, 128, 8).transpose(1, 0, 2).reshape(128, 64))
        d["c_l2_normw"] = np.ascontiguousarray(np.broadcast_to(
            np.asarray(inputs["L2_ffn_norm_weight"], f32), (128, 1024)))
        d["s_l2_alpha"] = np.asarray(inputs["L2_ffn_residual_alpha"],
                                     f32).reshape(1, 1)
        d["c_ones_f32"] = np.ones((1, 128), f32)
    if "L1f" in need:
        d["w_l1_g"] = np.asarray(inputs["L1_ffn_Wg"], f32).reshape(8 * 1024, 4096)
        d["w_l1_u"] = np.asarray(inputs["L1_ffn_Wu"], f32).reshape(8 * 1024, 4096)
        d["w_l1_d"] = np.asarray(inputs["L1_ffn_Wd"], f32).reshape(8 * 4096, 1024)
    return d


def shard_x(inputs):
    x = np.asarray(inputs["L0_attn_x"], np.float32)
    return [np.ascontiguousarray(x[2 * c:2 * c + 2].reshape(NT, D))
            for c in range(NCORES)]


class Stage:
    def __init__(self, subblocks):
        self.subblocks = subblocks
        self.nc = bass.Bass("TRN2", target_bir_lowering=False)
        self.S = Sched(self.nc)
        self.dram = {}
        self.ext_inputs = set()
        self.sb = {}
        self._names = {}
        self.x_ready = []

    def din(self, name, shape, dtype=F32):
        if name not in self.dram:
            self.dram[name] = self.nc.dram_tensor(name, list(shape), dtype,
                                                  kind="ExternalInput")
            self.ext_inputs.add(name)
        return self.dram[name]

    def dint(self, name, shape, dtype=F32):
        if name not in self.dram:
            self.dram[name] = self.nc.dram_tensor(name, list(shape), dtype,
                                                  kind="Internal")
        return self.dram[name]

    def dout(self, name, shape, dtype=F32):
        if name not in self.dram:
            self.dram[name] = self.nc.dram_tensor(name, list(shape), dtype,
                                                  kind="ExternalOutput")
        return self.dram[name]

    def _uniq(self, name):
        n = self._names.get(name, 0)
        self._names[name] = n + 1
        return name if n == 0 else f"{name}_{n}"

    def alloc(self, ctx, name, shape, dtype):
        t = ctx.enter_context(self.nc.sbuf_tensor(self._uniq(name),
                                                  list(shape), dtype))
        self.sb[name] = t
        return t

    def palloc(self, ctx, name, shape, dtype=F32):
        t = ctx.enter_context(self.nc.psum_tensor(self._uniq(name),
                                                  list(shape), dtype))
        self.sb[name] = t
        return t


def emit_ln_to_xT(st, x_src, mode="plain", wb=None):
    S, sb = st.S, st.sb
    xT, ident = sb["xT"], sb["ident"]
    with contextlib.ExitStack() as ctx:
        xt = st.alloc(ctx, "ln_xt", [128, 2 * 1024], F32)
        ln16 = st.alloc(ctx, "ln_16", [128, 2 * 1024], BF16)
        stat = st.alloc(ctx, "ln_stat", [128, 12], F32)
        mv = st.alloc(ctx, "ln_mv", [128, 2], F32)
        vec = st.alloc(ctx, "ln_vec", [128, 8], F32)
        pst = st.palloc(ctx, "ln_ptr", [128, 2048], BF16)
        if mode == "l0":
            lnf = st.alloc(ctx, "ln_f32", [128, 2 * 1024], F32)
            qs = alloc_quant(st, ctx, "lnq")
        if mode == "l3":
            lnf = st.alloc(ctx, "ln_f32", [128, 2 * 1024], F32)
        dump16 = (mode == "dump")

        stats_dv, ln_ac, tr_pe, copy_dv = [], [], [], []
        for i in range(NTILE):
            b = i % 2
            xt_b = xt[:, b * 1024:(b + 1) * 1024]
            ln_b = ln16[:, b * 1024:(b + 1) * 1024]
            waits = list(st.x_ready)
            if i >= 2:
                waits += [("dv", stats_dv[i - 2])]
                if mode == "l2":
                    waits.append(S.chv(f"lnwb{i % 2}"))
                elif mode == "l0":
                    waits.append(("ac", S.v("ac")))
                elif mode == "l3":
                    waits.append(S.chv(f"lnwb{i % 2}"))
                else:
                    waits.append(("ac", ln_ac[i - 2]))
            ld = S.dma("sync", xt_b, x_src[i * 128:(i + 1) * 128, :], waits,
                       ch=f"ln{i % 2}")
            S.op("vector", lambda v, a=xt_b: v.bn_stats(stat[:, 0:6], a[:, 0:512]),
                 waits=[ld])
            stats_dv.append(S.op("vector", lambda v, a=xt_b:
                                 v.bn_stats(stat[:, 6:12], a[:, 512:1024]),
                                 inc="dv"))
            S.op("vector", lambda v: v.bn_aggr(mv[:, 0:2], stat[:, 0:12]),
                 waits=[("dv", S.v("dv"))], inc="dv")
            S.op("vector", lambda v: v.tensor_scalar(vec[:, 2:3], mv[:, 1:2],
                                                     1e-5, None, ALU.add),
                 waits=[("dv", S.v("dv"))], inc="dv")
            S.op("scalar", lambda a: a.activation(vec[:, 3:4], vec[:, 2:3], AF.Sqrt),
                 waits=[("dv", S.v("dv"))], inc="ac")
            S.op("vector", lambda v: v.reciprocal(vec[:, 0:1], vec[:, 3:4]),
                 waits=[("ac", S.v("ac"))], inc="dv")
            S.op("vector", lambda v: v.tensor_scalar(vec[:, 1:2], mv[:, 0:1],
                                                     vec[:, 0:1], -1.0,
                                                     ALU.mult, ALU.mult),
                 waits=[("dv", S.v("dv"))], inc="dv")
            if mode == "l2":
                S.op("vector", lambda v: v.tensor_tensor(vec[:, 4:5], vec[:, 0:1],
                                                         vec[:, 0:1], ALU.mult),
                     waits=[("dv", S.v("dv"))], inc="dv")
                S.op("vector", lambda v: v.scalar_tensor_tensor(
                    vec[:, 4:5], vec[:, 4:5], 1e-6, mv[:, 1:2],
                    ALU.bypass, ALU.mult), waits=[("dv", S.v("dv"))], inc="dv")
                S.op("vector", lambda v: v.tensor_scalar(vec[:, 4:5], vec[:, 4:5],
                                                         1e-6, None, ALU.add),
                     waits=[("dv", S.v("dv"))], inc="dv")
                S.op("scalar", lambda a: a.activation(vec[:, 5:6], vec[:, 4:5],
                                                      AF.Sqrt),
                     waits=[("dv", S.v("dv"))], inc="ac")
                S.op("vector", lambda v: v.reciprocal(vec[:, 6:7], vec[:, 5:6]),
                     waits=[("ac", S.v("ac"))], inc="dv")
                S.op("vector", lambda v: v.tensor_tensor(vec[:, 6:7], vec[:, 6:7],
                                                         vec[:, 0:1], ALU.mult),
                     waits=[("dv", S.v("dv"))], inc="dv")
                S.op("vector", lambda v: v.tensor_scalar(vec[:, 7:8], mv[:, 0:1],
                                                         vec[:, 6:7], -1.0,
                                                         ALU.mult, ALU.mult),
                     waits=[("dv", S.v("dv"))], inc="dv")
                scol, bcol = 6, 7
            else:
                scol, bcol = 0, 1
            wl = [("dv", S.v("dv"))]
            if i >= 2:
                wl.append(("pe", tr_pe[i - 2]))
            if mode == "l0":
                lnf_b = lnf[:, b * 1024:(b + 1) * 1024]
                av = S.op("scalar", lambda a, xi=xt_b, o=lnf_b,
                          s_=vec[:, scol:scol + 1], b_=vec[:, bcol:bcol + 1]:
                          a.activation(o, xi, AF.Identity, bias=b_, scale=s_),
                          waits=wl, inc="ac")
                ln_ac.append(av)
                qdv = emit_quant(st, lnf_b, ln_b, 1024, qs,
                                 waits_in=[("ac", av)])
                ln_ac[-1] = qdv
            elif mode == "l3":
                lnf_b = lnf[:, b * 1024:(b + 1) * 1024]
                wl2 = wl + ([S.chv(f"lnwb{i % 2}")]
                            if i >= 2 else [])
                av = S.op("scalar", lambda a, xi=xt_b, o=lnf_b,
                          s_=vec[:, scol:scol + 1], b_=vec[:, bcol:bcol + 1]:
                          a.activation(o, xi, AF.Identity, bias=b_, scale=s_),
                          waits=wl2, inc="ac")
                cpv = S.op("vector", lambda v, o=ln_b, i_=lnf_b:
                           v.tensor_copy(o, i_), waits=[("ac", av)], inc="dv")
                ln_ac.append(cpv)
                S.dma("gpsimd", wb[i * 128:(i + 1) * 128, :], lnf_b,
                      [("dv", cpv)], ch=f"lnwb{i % 2}")
            else:
                wl2 = wl
                if dump16 and i >= 2:
                    wl2 = wl + [S.chv(f"lnwb{i % 2}")]
                ln_ac.append(S.op("scalar", lambda a, xi=xt_b, o=ln_b,
                                  s_=vec[:, scol:scol + 1],
                                  b_=vec[:, bcol:bcol + 1]:
                                  a.activation(o, xi, AF.Identity, bias=b_,
                                               scale=s_),
                                  waits=wl2, inc="ac"))
                if dump16:
                    S.dma("gpsimd", wb[i * 128:(i + 1) * 128, :], ln_b,
                          [("ac", ln_ac[-1])], ch=f"lnwb{i % 2}")
            if mode == "l2":
                ln_ac[-1] = S.op("vector", lambda v, o=ln_b:
                                 v.tensor_tensor(o, o, sb["w_bcast"][:, :],
                                                 ALU.mult),
                                 waits=[("ac", ln_ac[-1])], inc="dv")
                S.op("vector", lambda v: v.tensor_scalar(vec[:, 4:5], vec[:, 0:1],
                                                         0.5, 1.0, ALU.mult,
                                                         ALU.add),
                     waits=[("dv", S.v("dv"))], inc="dv")
                S.op("vector", lambda v: v.tensor_scalar(vec[:, 5:6], vec[:, 1:2],
                                                         0.5, None, ALU.mult),
                     waits=[("dv", S.v("dv"))], inc="dv")
                wb_dv = S.op("vector", lambda v, o=xt_b:
                             v.tensor_scalar(o, o, vec[:, 4:5], vec[:, 5:6],
                                             ALU.mult, ALU.add),
                             waits=[("dv", S.v("dv"))], inc="dv")
                S.dma("gpsimd", wb[i * 128:(i + 1) * 128, :], xt_b,
                      [("dv", wb_dv)], ch=f"lnwb{i % 2}")
            lnsem = "dv" if mode in ("l2", "l0", "l3") else "ac"
            if mode == "dump":
                tr_pe.append(S.v("pe"))
                continue
            for c in range(DCH):
                w = [(lnsem, ln_ac[-1])] if c == 0 else []
                if len(copy_dv) >= 2:
                    w.append(("dv", copy_dv[-2]))
                slot = (c % 2) * 1024
                pv = S.op("tensor", lambda t, src=ln_b[:, c * 128:(c + 1) * 128],
                          dst=pst[:, slot:slot + 128]:
                          t.transpose(dst, src, ident[:, :]), waits=w, inc="pe")
                copy_dv.append(S.op(
                    "vector",
                    lambda v, src=pst[:, slot:slot + 128],
                    dst=xT[:, c * 4096 + i * 128: c * 4096 + (i + 1) * 128]:
                    v.tensor_copy(dst, src), waits=[("pe", pv)], inc="dv"))
            tr_pe.append(S.v("pe"))
        ln_loads = [S.chv("ln0"), S.chv("ln1")]
        S.flush()
    return ln_loads


def _w3(ap, G, blk):
    return bass.AP(ap.tensor, ap.offset, [list(ap.ap[0]), [blk, G], [1, blk]])


def _b3(ap, G, blk):
    return bass.AP(ap.tensor, ap.offset, [list(ap.ap[0]), [1, G], [0, blk]])


def emit_quant(st, src_ap, dst_ap, W, qs, waits_in=()):
    S = st.S
    G = W // 16
    am, xn, aa, q2, sg = qs["am"], qs["xn"], qs["aa"], qs["q2"], qs["sg"]
    amv = am[:, 0:G]
    s6v = am[:, 176:176 + G]
    inv = am[:, 352:352 + G]
    xnv, aav, q2v = xn[:, 0:W], aa[:, 0:W], q2[:, 0:W]
    sgv = sg[:, 0:W]
    src3 = _w3(src_ap, G, 16)
    d = S.op("vector", lambda v: v.tensor_reduce(amv, src3, AX.X, ALU.max,
                                                 apply_absolute_value=True),
             waits=list(waits_in), inc="dv")
    d = S.op("vector", lambda v: v.tensor_scalar(s6v, amv, 1.0 / 6.0, 1e-12,
                                                 ALU.mult, ALU.max),
             waits=[("dv", d)], inc="dv")
    d = S.op("vector", lambda v: v.reciprocal(inv, s6v), waits=[("dv", d)],
             inc="dv")
    d = S.op("vector", lambda v: v.tensor_tensor(_w3(xnv, G, 16), src3,
                                                 _b3(inv, G, 16), ALU.mult),
             waits=[("dv", d)], inc="dv")
    a1 = S.op("scalar", lambda a: a.activation(sgv, xnv, AF.Sign, bias=0.0),
              waits=[("dv", d)], inc="ac")
    a2 = S.op("scalar", lambda a: a.activation(aav, xnv, AF.Abs),
              waits=[("ac", a1)], inc="ac")
    d = S.op("vector", lambda v: v.tensor_scalar(xnv, aav, 0.25, None,
                                                 ALU.is_gt),
             waits=[("ac", a2)], inc="dv")
    for thr in (0.75, 1.25, 1.75):
        d = S.op("vector", lambda v, t=thr: v.scalar_tensor_tensor(
            xnv, aav, t, xnv, ALU.is_gt, ALU.add), waits=[("dv", d)], inc="dv")
    d = S.op("vector", lambda v: v.tensor_scalar(q2v, aav, 2.5, None,
                                                 ALU.is_gt),
             waits=[("dv", d)], inc="dv")
    d = S.op("vector", lambda v: v.scalar_tensor_tensor(
        q2v, aav, 3.5, q2v, ALU.is_gt, ALU.add), waits=[("dv", d)], inc="dv")
    d = S.op("vector", lambda v: v.tensor_scalar(xnv, xnv, 0.5, None,
                                                 ALU.mult),
             waits=[("dv", d)], inc="dv")
    d = S.op("vector", lambda v: v.tensor_tensor(xnv, xnv, q2v, ALU.add),
             waits=[("dv", d)], inc="dv")
    d = S.op("vector", lambda v: v.tensor_scalar(q2v, aav, 5.0, None,
                                                 ALU.is_gt),
             waits=[("dv", d)], inc="dv")
    d = S.op("vector", lambda v: v.scalar_tensor_tensor(
        xnv, q2v, 2.0, xnv, ALU.mult, ALU.add), waits=[("dv", d)], inc="dv")
    d = S.op("vector", lambda v: v.tensor_tensor(xnv, xnv, sgv, ALU.mult),
             waits=[("dv", d)], inc="dv")
    d = S.op("vector", lambda v: v.tensor_tensor(_w3(dst_ap, G, 16),
                                                 _w3(xnv, G, 16),
                                                 _b3(s6v, G, 16), ALU.mult),
             waits=[("dv", d)], inc="dv")
    return d


def alloc_quant(st, ctx, tag="q"):
    return {
        "am": st.alloc(ctx, tag + "_am", [128, 3 * 176], F32),
        "xn": st.alloc(ctx, tag + "_xn", [128, 2816], F32),
        "aa": st.alloc(ctx, tag + "_aa", [128, 2816], F32),
        "q2": st.alloc(ctx, tag + "_q2", [128, 2816], F32),
        "sg": st.alloc(ctx, tag + "_sg", [128, 2816], BF16),
    }


L0_PIECES = [6, 6, 5, 5]


def emit_l0_cache(st):
    S, sb = st.S, st.sb
    ident = sb["ident"]
    wg = st.din("w_l0_g", (2816, 1024))
    wu = st.din("w_l0_u", (2816, 1024))
    wd = st.din("w_l0_d", (1024, 2816))
    cg = st.dint("c_l0_g", (4 * 128, 6 * 1024), BF16)
    cu = st.dint("c_l0_u", (4 * 128, 6 * 1024), BF16)
    cd = st.dint("c_l0_d", (4 * 128, 6 * 1024), BF16)
    offs = [sum(L0_PIECES[:p]) for p in range(4)]
    with contextlib.ExitStack() as ctx:
        qs = alloc_quant(st, ctx, "wq")
        stg = st.alloc(ctx, "l0c_stage", [128, 2 * 2816], F32)
        q16 = st.alloc(ctx, "l0c_q16", [128, 2 * 2816], BF16)
        img = st.alloc(ctx, "l0c_img", [128, 2 * 6144], BF16)
        pst = st.palloc(ctx, "l0c_pt", [128, 2048], BF16)
        copy_hist = []
        store_hist = []
        qd_hist = []
        for kind, wsrc, cdst in ((0, wg, cg), (1, wu, cu)):
            for p in range(4):
                NF = L0_PIECES[p]
                off = offs[p]
                ib = (kind * 4 + p) % 2
                imgv = img[:, ib * 6144:(ib + 1) * 6144]
                for f in range(NF):
                    rc = off + f
                    cb = (len(qd_hist)) % 2
                    stgv = stg[:, cb * 2816:cb * 2816 + 1024]
                    q16v = q16[:, cb * 2816:cb * 2816 + 1024]
                    w = [("dv", qd_hist[-2])] if len(qd_hist) >= 2 else []
                    ld = S.dma("sync", stgv, wsrc.ap()[rc * 128:(rc + 1) * 128, :],
                               w, ch=f"l0c{cb}")
                    qd = emit_quant(st, stgv, q16v, 1024, qs, waits_in=[ld])
                    qd_hist.append(qd)
                    for c in range(DCH):
                        wt = [("dv", qd)] if c == 0 else []
                        if len(copy_hist) >= 2:
                            wt.append(("dv", copy_hist[-2]))
                        slot = (len(copy_hist) % 2) * 1024
                        pv = S.op("tensor", lambda t, s=q16v[:, c * 128:(c + 1) * 128],
                                  o=pst[:, slot:slot + 128]:
                                  t.transpose(o, s, ident[:, :]), waits=wt,
                                  inc="pe")
                        dst = imgv[:, c * NF * 128 + f * 128: c * NF * 128 + (f + 1) * 128]
                        wcp = [("pe", pv)]
                        if len(store_hist) >= 2 and f == 0 and c == 0:
                            wcp.append(store_hist[-2])
                        copy_hist.append(S.op(
                            "vector", lambda v, s=pst[:, slot:slot + 128], o=dst:
                            v.tensor_copy(o, s), waits=wcp, inc="dv"))
                store_hist.append(S.dma(
                    "gpsimd", cdst.ap()[p * 128:(p + 1) * 128, :NF * 1024],
                    imgv[:, :NF * 1024], [("dv", copy_hist[-1])],
                    ch=f"l0cs{(kind * 4 + p) % 2}"))
        dstrip = st.alloc(ctx, "l0c_dstrip", [128, 2 * 128], BF16)
        dsc_hist = []
        for hc in range(8):
            cb = (len(qd_hist)) % 2
            stgv = stg[:, cb * 2816:cb * 2816 + 2816]
            q16v = q16[:, cb * 2816:cb * 2816 + 2816]
            w = [("dv", qd_hist[-2])] if len(qd_hist) >= 2 else []
            ld = S.dma("sync", stgv, wd.ap()[hc * 128:(hc + 1) * 128, :],
                       w, ch=f"l0c{cb}")
            qd = emit_quant(st, stgv, q16v, 2816, qs, waits_in=[ld])
            qd_hist.append(qd)
            for ic in range(22):
                p = next(pp for pp in range(4)
                         if offs[pp] <= ic < offs[pp] + L0_PIECES[pp])
                c_local = ic - offs[p]
                k = len(dsc_hist)
                wt = [("dv", qd)] if ic == 0 else []
                if k >= 2:
                    wt.append(("ac", dsc_hist[-2]))
                slot = (k % 2) * 1024
                pv = S.op("tensor", lambda t, s=q16v[:, ic * 128:(ic + 1) * 128],
                          o=pst[:, slot:slot + 128]:
                          t.transpose(o, s, ident[:, :]), waits=wt, inc="pe")
                sv = dstrip[:, (k % 2) * 128:(k % 2 + 1) * 128]
                wcp = [("pe", pv)]
                if k >= 2:
                    wcp.append(store_hist[-2])
                cpd = S.op("scalar", lambda a, s=pst[:, slot:slot + 128], o=sv:
                           a.activation(o, s, AF.Copy, bias=0.0, scale=0.5),
                           waits=wcp, inc="ac")
                dsc_hist.append(cpd)
                store_hist.append(S.dma(
                    "gpsimd",
                    cd.ap()[p * 128:(p + 1) * 128,
                            c_local * 1024 + hc * 128: c_local * 1024 + (hc + 1) * 128],
                    sv, [("ac", cpd)], ch=f"l0cs{k % 2}"))
        done = [store_hist[-1], store_hist[-2]]
        S.flush()
    return done


def emit_l0_main(st, x_ap, ln_loads, cache_gd=0):
    S, sb = st.S, st.sb
    xT = sb["xT"]
    ident = sb["ident"]
    cg, cu, cd = (st.dram[n] for n in ("c_l0_g", "c_l0_u", "c_l0_d"))
    offs = [sum(L0_PIECES[:p]) for p in range(4)]
    with contextlib.ExitStack() as ctx:
        qs = alloc_quant(st, ctx, "hq")
        wgs = st.alloc(ctx, "l0_wg", [128, 2 * 6144], BF16)
        wus = st.alloc(ctx, "l0_wu", [128, 2 * 6144], BF16)
        wds = st.alloc(ctx, "l0_wd", [128, 2 * 6144], BF16)
        hge = st.alloc(ctx, "l0_h", [128, 768], F32)
        hq = st.alloc(ctx, "l0_hq", [128, 768], BF16)
        hqT = st.alloc(ctx, "l0_hqT", [128, 2 * 768], BF16)
        xe = st.alloc(ctx, "l0_xe", [128, 2 * 1024], F32)
        psg = st.palloc(ctx, "l0_pg", [128, 1024])
        psu = st.palloc(ctx, "l0_pu", [128, 1024])
        psy = st.palloc(ctx, "l0_py", [128, 2 * 512])
        pst = st.palloc(ctx, "l0_pt", [128, 2048], BF16)

        pe_done = {}
        gd_out = {}
        last_buf_store = {0: None, 1: None}
        sem_hist = {"hmul": [], "yev": [], "tcp": [], "hqd": []}
        for p in range(4):
            NF = L0_PIECES[p]
            W = NF * 128
            jb = p % 2
            wgv = wgs[:, jb * 6144:jb * 6144 + NF * 1024]
            wuv = wus[:, jb * 6144:jb * 6144 + NF * 1024]
            wdv = wds[:, jb * 6144:jb * 6144 + NF * 1024]
            w = list(cache_gd) if (p == 0 and cache_gd) else []
            if p >= 2:
                w.append(("pe", pe_done[p - 2]))
            lg = S.dma("sync", wgv, cg.ap()[p * 128:(p + 1) * 128, :NF * 1024],
                       w, ch=f"l0wg{jb}")
            lu = S.dma("sync", wuv, cu.ap()[p * 128:(p + 1) * 128, :NF * 1024],
                       ch=f"l0wu{jb}")
            ld = S.dma("sync", wdv, cd.ap()[p * 128:(p + 1) * 128, :NF * 1024],
                       ch=f"l0wd{jb}")
            for i in range(NTILE):
                rows = slice(i * 128, (i + 1) * 128)
                pegs = []
                for kind, ps, wv, lw in ((0, psg, wgv, lg), (1, psu, wuv, lu)):
                    for sess in range((W + 511) // 512):
                        n0 = sess * 512
                        nn = min(512, W - n0)
                        wfirst = []
                        if i == 0 and sess == 0:
                            wfirst.append(lw)
                        if kind == 0 and sess == 0:
                            if len(sem_hist["hmul"]) >= 1:
                                wfirst.append(("dv", sem_hist["hmul"][-1]))
                        pv = None
                        for c in range(DCH):
                            pv = S.op(
                                "tensor", lambda t, o=ps[:, n0:n0 + nn],
                                l=xT[:, c * 4096 + i * 128: c * 4096 + (i + 1) * 128],
                                r=wv[:, c * W + n0: c * W + n0 + nn],
                                st_=(c == 0), sp=(c == DCH - 1):
                                t.matmul(o, l, r, start=st_, stop=sp),
                                waits=(wfirst if c == 0 else []),
                                inc=("pe" if c == DCH - 1 else None))
                        pegs.append(pv)
                a1 = S.op("scalar", lambda a, o=hge[:, :W], i_=psg[:, :W]:
                          a.activation(o, i_, AF.Silu),
                          waits=[("pe", pegs[1])], inc="ac")
                hm = S.op("vector", lambda v, o=hge[:, :W], b=psu[:, :W]:
                          v.tensor_tensor(o, o, b, ALU.mult),
                          waits=[("ac", a1), ("pe", pegs[-1])], inc="dv")
                sem_hist["hmul"].append(hm)
                qd = emit_quant(st, hge[:, :W], hq[:, :W], W, qs,
                                waits_in=[("dv", hm)])
                sem_hist["hqd"].append(qd)
                hb = (i % 2) * 768
                for f in range(NF):
                    wt = [("dv", qd)] if f == 0 else []
                    if len(sem_hist["tcp"]) >= 2:
                        wt.append(("dv", sem_hist["tcp"][-2]))
                    slot = (len(sem_hist["tcp"]) % 2) * 1024
                    pv = S.op("tensor", lambda t, s=hq[:, f * 128:(f + 1) * 128],
                              o=pst[:, slot:slot + 128]:
                              t.transpose(o, s, ident[:, :]), waits=wt,
                              inc="pe")
                    sem_hist["tcp"].append(S.op(
                        "vector", lambda v, s=pst[:, slot:slot + 128],
                        o=hqT[:, hb + f * 128: hb + (f + 1) * 128]:
                        v.tensor_copy(o, s), waits=[("pe", pv)], inc="dv"))
                tcp_done = sem_hist["tcp"][-1]
                xb = i % 2
                xev = xe[:, xb * 1024:(xb + 1) * 1024]
                wx = []
                if (p - 1, i) in gd_out:
                    wx.append(gd_out[(p - 1, i)])
                if last_buf_store[xb] is not None:
                    wx.append(last_buf_store[xb])
                lx = S.dma("scalar", xev, x_ap[rows, :], wx, ch=f"l0xl{i % 2}")
                evs = []
                for n in range(2):
                    yps = psy[:, n * 512:(n + 1) * 512]
                    wy = [("dv", tcp_done)]
                    if i == 0 and n == 0:
                        wy.append(ld)
                    if len(sem_hist["yev"]) >= 2:
                        wy.append(("dv", sem_hist["yev"][-2]))
                    pv = None
                    for f in range(NF):
                        pv = S.op("tensor", lambda t, o=yps,
                                  l=hqT[:, hb + f * 128: hb + (f + 1) * 128],
                                  r=wdv[:, f * 1024 + n * 512: f * 1024 + n * 512 + 512],
                                  st_=(f == 0), sp=(f == NF - 1):
                                  t.matmul(o, l, r, start=st_, stop=sp),
                                  waits=(wy if f == 0 else []),
                                  inc=("pe" if f == NF - 1 else None))
                    ev = S.op("vector", lambda v, o=xev[:, n * 512:(n + 1) * 512],
                              p_=yps: v.tensor_tensor(o, o, p_, ALU.add),
                              waits=[("pe", pv), lx], inc="dv")
                    sem_hist["yev"].append(ev)
                    evs.append(ev)
                wout = [("dv", evs[-1])]
                if p == 0:
                    wout += ln_loads
                gd_out[(p, i)] = S.dma("gpsimd", x_ap[rows, :], xev, wout,
                                       ch=f"l0xs{i % 2}")
                last_buf_store[xb] = gd_out[(p, i)]
            pe_done[p] = S.v("pe")
        S.flush()
    return gd_out[(3, NTILE - 1)]


def emit_transpose_only(st, src_dram, tag="tp"):
    S, sb = st.S, st.sb
    xT, ident = sb["xT"], sb["ident"]
    with contextlib.ExitStack() as ctx:
        xt = st.alloc(ctx, tag + "_xt", [128, 2 * 1024], F32)
        b16 = st.alloc(ctx, tag + "_16", [128, 2 * 1024], BF16)
        pst = st.palloc(ctx, tag + "_pt", [128, 2048], BF16)
        cast_ac, tr_pe, copy_dv = [], [], []
        for i in range(NTILE):
            b = i % 2
            xt_b = xt[:, b * 1024:(b + 1) * 1024]
            b_b = b16[:, b * 1024:(b + 1) * 1024]
            w = []
            if i >= 2:
                w.append(("ac", cast_ac[i - 2]))
            ld = S.dma("sync", xt_b, src_dram[i * 128:(i + 1) * 128, :], w,
                       ch=tag + f"{i % 2}")
            wl = [ld]
            if i >= 2:
                wl.append(("pe", tr_pe[i - 2]))
            cast_ac.append(S.op("scalar", lambda a, o=b_b, i_=xt_b:
                                a.activation(o, i_, AF.Copy, bias=0.0,
                                             scale=1.0),
                                waits=wl, inc="ac"))
            for c in range(DCH):
                w2 = [("ac", cast_ac[-1])] if c == 0 else []
                if len(copy_dv) >= 2:
                    w2.append(("dv", copy_dv[-2]))
                slot = (c % 2) * 1024
                pv = S.op("tensor", lambda t, s=b_b[:, c * 128:(c + 1) * 128],
                          o=pst[:, slot:slot + 128]:
                          t.transpose(o, s, ident[:, :]), waits=w2, inc="pe")
                copy_dv.append(S.op(
                    "vector", lambda v, s=pst[:, slot:slot + 128],
                    o=xT[:, c * 4096 + i * 128: c * 4096 + (i + 1) * 128]:
                    v.tensor_copy(o, s), waits=[("pe", pv)], inc="dv"))
            tr_pe.append(S.v("pe"))
        S.flush()


def emit_l3_cache(st):
    S, sb = st.S, st.sb
    qT = st.din("w_l3_q", (6 * 1024, 1024), I32)
    scT = st.din("w_l3_sc", (6 * 1024, 1024))
    cw = st.dint("c_l3_w", (6 * 128, 8192), BF16)
    with contextlib.ExitStack() as ctx:
        qs_ = st.alloc(ctx, "l3c_q", [128, 2 * 1024], I32)
        sc_ = st.alloc(ctx, "l3c_s", [128, 2 * 1024], F32)
        t_ = st.alloc(ctx, "l3c_t", [128, 2 * 1024], F32)
        o_ = st.alloc(ctx, "l3c_o", [128, 2 * 1024], BF16)
        hist, st_hist = [], []
        for k in range(48):
            mi, c = k // 8, k % 8
            b = k % 2
            qv = qs_[:, b * 1024:(b + 1) * 1024]
            sv = sc_[:, b * 1024:(b + 1) * 1024]
            tv = t_[:, b * 1024:(b + 1) * 1024]
            ov = o_[:, b * 1024:(b + 1) * 1024]
            w = [("dv", hist[-2])] if len(hist) >= 2 else []
            l1 = S.dma("sync", qv, qT.ap()[k * 128:(k + 1) * 128, :], w,
                       ch=f"l3cq{b}")
            l2 = S.dma("sync", sv, scT.ap()[k * 128:(k + 1) * 128, :],
                       ch=f"l3cs{b}")
            d = S.op("vector", lambda v, o=tv, i_=qv: v.tensor_copy(o, i_),
                     waits=[l1], inc="dv")
            a = S.op("scalar", lambda a_, o=tv, i_=tv:
                     a_.activation(o, i_, AF.Copy, bias=-1.0, scale=1.0 / 7.5),
                     waits=[("dv", d)], inc="ac")
            w2 = [("ac", a), l2]
            if len(st_hist) >= 2:
                w2.append(st_hist[-2])
            d = S.op("vector", lambda v, o=ov, a_=tv, b_=sv:
                     v.tensor_tensor(o, a_, b_, ALU.mult), waits=w2, inc="dv")
            hist.append(d)
            st_hist.append(S.dma(
                "gpsimd",
                cw.ap()[mi * 128:(mi + 1) * 128, c * 1024:(c + 1) * 1024],
                ov, [("dv", d)], ch=f"l3cw{k % 2}"))
        done = [st_hist[-1], st_hist[-2]]
        S.flush()
    return done


def emit_l3_ffn(st, x_ap, h_ap, cache_gd=0):
    S, sb = st.S, st.sb
    nc = st.nc
    cw = st.dram["c_l3_w"]
    la_d = st.din("w_l3_la", (6 * 128, 256))
    lb_d = st.din("w_l3_lb", (6 * 32, 1024))
    bi_d = st.din("w_l3_bias", (6, 1024))
    gm_d = st.din("c_l3_gam", (3 * 128, 1024))
    bt_d = st.din("c_l3_bet", (3 * 128, 1024))
    on_d = st.din("c_ones_f32", (1, 128))
    ident = sb["ident"]
    xT = sb["xT"]
    with contextlib.ExitStack() as ctx:
        xT2 = st.alloc(ctx, "l3_xT2", [128, 8 * 4096], BF16)
        wsb = st.alloc(ctx, "l3_w", [128, 8192], BF16)
        laf = st.alloc(ctx, "l3_laf", [128, 256], F32)
        la16 = st.alloc(ctx, "l3_la", [128, 256], BF16)
        lbf = st.alloc(ctx, "l3_lbf", [32, 1024], F32)
        lb16 = st.alloc(ctx, "l3_lb", [32, 1024], BF16)
        bia = st.alloc(ctx, "l3_bia", [1, 1024], F32)
        ones = st.alloc(ctx, "l3_ones", [1, 128], F32)
        mid = st.alloc(ctx, "l3_mid", [32, 4096], BF16)
        gam = st.alloc(ctx, "l3_gam", [128, 1024], F32)
        bet = st.alloc(ctx, "l3_bet", [128, 1024], F32)
        ysb = st.alloc(ctx, "l3_ysb", [128, 2 * 1024], F32)
        g16 = st.alloc(ctx, "l3_g16", [128, 2 * 1024], BF16)
        stat = st.alloc(ctx, "l3_stat", [128, 12], F32)
        mv = st.alloc(ctx, "l3_mv", [128, 2], F32)
        vec = st.alloc(ctx, "l3_vec", [128, 8], F32)
        he = st.alloc(ctx, "l3_he", [128, 2 * 1024], F32)
        psm = st.palloc(ctx, "l3_pm", [32, 2 * 512])
        psy = st.palloc(ctx, "l3_py", [128, 2 * 512])
        pst = st.palloc(ctx, "l3_pt", [128, 2048], BF16)

        lon = S.dma("sync", ones[:, :], on_d.ap(), ch="l3on")

        prev_c = [0]

        def load_linear(mi, prev_pe, extra=()):
            w0 = [("pe", prev_pe)] + list(extra)
            if prev_c[0]:
                w0.append(("dv", prev_c[0]))
            lw = S.dma("sync", wsb[:, :], cw.ap()[mi * 128:(mi + 1) * 128, :],
                       w0, ch="l3w")
            l1 = S.dma("sync", laf[:, :], la_d.ap()[mi * 128:(mi + 1) * 128, :],
                       ch="l3la")
            l2 = S.dma("sync", lbf[:, :], lb_d.ap()[mi * 32:(mi + 1) * 32, :],
                       ch="l3lb")
            l3_ = S.dma("sync", bia[:, :], bi_d.ap()[mi:mi + 1, :], ch="l3bi")
            c1 = S.op("vector", lambda v: v.tensor_copy(la16[:, :], laf[:, :]),
                      waits=[l1, ("pe", prev_pe)], inc="dv")
            c2 = S.op("vector", lambda v: v.tensor_copy(lb16[:, :], lbf[:, :]),
                      waits=[l2], inc="dv")
            prev_c[0] = c2
            return lw, c1, c2, l3_

        def compute_mid(srcT, c1):
            cps = []
            for sess in range(8):
                pb = sess % 2
                ps = psm[:, pb * 512:(pb + 1) * 512]
                w = [("dv", c1)] if sess == 0 else []
                if len(cps) >= 2:
                    w.append(("dv", cps[-2]))
                pv = None
                for c in range(DCH):
                    pv = S.op("tensor", lambda t, o=ps,
                              l=la16[:, c * 32:(c + 1) * 32],
                              r=srcT[:, c * 4096 + sess * 512:
                                     c * 4096 + (sess + 1) * 512],
                              st_=(c == 0), sp=(c == DCH - 1):
                              t.matmul(o, l, r, start=st_, stop=sp),
                              waits=(w if c == 0 else []),
                              inc=("pe" if c == DCH - 1 else None))
                cps.append(S.op("vector", lambda v,
                                o=mid[:, sess * 512:(sess + 1) * 512], s=ps:
                                v.tensor_copy(o, s), waits=[("pe", pv)],
                                inc="dv"))
            return cps[-1]

        prev_pe = 0
        cg = cache_gd
        for blk in range(3):
            i0, i1 = 2 * blk, 2 * blk + 1
            lw, c1, c2, lb_ = load_linear(i0, prev_pe,
                                          extra=(list(cg) if cg else ()))
            cg = 0
            lgm = S.dma("sync", gam[:, :],
                        gm_d.ap()[blk * 128:(blk + 1) * 128, :],
                        [("dv", S.v("dv"))], ch="l3gm")
            lbt = S.dma("sync", bet[:, :],
                        bt_d.ap()[blk * 128:(blk + 1) * 128, :], ch="l3bt")
            mid_dv = compute_mid(xT, c1)
            ev_hist, tr_pe, cast_hist, tcp_list = [], [], [], []
            for i in range(NTILE):
                b = i % 2
                yv = ysb[:, b * 1024:(b + 1) * 1024]
                t1v = yv
                g16v = g16[:, b * 1024:(b + 1) * 1024]
                for n in range(2):
                    ps = psy[:, n * 512:(n + 1) * 512]
                    w = []
                    if i == 0 and n == 0:
                        w += [lw, ("dv", mid_dv), lon, lb_]
                        w.append(("dv", c2))
                    if len(ev_hist) >= 2:
                        w.append(("dv", ev_hist[-2]))
                    pv = None
                    for c in range(DCH):
                        pv = S.op("tensor", lambda t, o=ps,
                                  l=xT[:, c * 4096 + i * 128:
                                       c * 4096 + (i + 1) * 128],
                                  r=wsb[:, c * 1024 + n * 512:
                                        c * 1024 + n * 512 + 512],
                                  st_=(c == 0), sp=False:
                                  t.matmul(o, l, r, start=st_, stop=sp),
                                  waits=(w if c == 0 else []),
                                  inc=None)
                    pv = S.op("tensor", lambda t, o=ps,
                              l=mid[:, i * 128:(i + 1) * 128],
                              r=lb16[:, n * 512:n * 512 + 512]:
                              t.matmul(o, l, r, start=False, stop=False),
                              inc=None)
                    pv = S.op("tensor", lambda t, o=ps,
                              l=ones[:, :],
                              r=bia[:, n * 512:n * 512 + 512]:
                              t.matmul(o, l, r, start=False, stop=True),
                              inc="pe")
                    ev_hist.append(S.op(
                        "vector", lambda v, o=yv[:, n * 512:(n + 1) * 512],
                        s=ps: v.tensor_copy(o, s), waits=[("pe", pv)],
                        inc="dv"))
                S.op("vector", lambda v, a=yv: v.bn_stats(stat[:, 0:6],
                                                          a[:, 0:512]),
                     waits=[("dv", ev_hist[-1])], inc="dv")
                S.op("vector", lambda v, a=yv: v.bn_stats(stat[:, 6:12],
                                                          a[:, 512:1024]),
                     waits=[("dv", S.v("dv"))], inc="dv")
                S.op("vector", lambda v: v.bn_aggr(mv[:, 0:2], stat[:, 0:12]),
                     waits=[("dv", S.v("dv"))], inc="dv")
                S.op("vector", lambda v: v.tensor_scalar(vec[:, 2:3],
                                                         mv[:, 1:2], 1e-5,
                                                         None, ALU.add),
                     waits=[("dv", S.v("dv"))], inc="dv")
                aa = S.op("scalar", lambda a: a.activation(vec[:, 3:4],
                                                           vec[:, 2:3],
                                                           AF.Sqrt),
                          waits=[("dv", S.v("dv"))], inc="ac")
                S.op("vector", lambda v: v.reciprocal(vec[:, 0:1],
                                                      vec[:, 3:4]),
                     waits=[("ac", aa)], inc="dv")
                S.op("vector", lambda v: v.tensor_scalar(vec[:, 1:2],
                                                         mv[:, 0:1],
                                                         vec[:, 0:1], -1.0,
                                                         ALU.mult, ALU.mult),
                     waits=[("dv", S.v("dv"))], inc="dv")
                w3 = [("dv", S.v("dv")), lgm, lbt]
                a2 = S.op("scalar", lambda a, o=t1v, i_=yv:
                          a.activation(o, i_, AF.Identity, bias=vec[:, 1:2],
                                       scale=vec[:, 0:1]), waits=w3, inc="ac")
                d = S.op("vector", lambda v, o=t1v: v.tensor_tensor(
                    o, o, gam[:, :], ALU.mult), waits=[("ac", a2)], inc="dv")
                d = S.op("vector", lambda v, o=t1v: v.tensor_tensor(
                    o, o, bet[:, :], ALU.add), waits=[("dv", d)], inc="dv")
                wg = [("dv", d)]
                if len(tr_pe) >= 2:
                    wg.append(("pe", tr_pe[-2]))
                a3 = S.op("scalar", lambda a, o=g16v, i_=t1v:
                          a.activation(o, i_, AF.Gelu), waits=wg, inc="ac")
                cast_hist.append(a3)
                for c in range(DCH):
                    w2 = [("ac", a3)] if c == 0 else []
                    if len(tcp_list) >= 2:
                        w2.append(("dv", tcp_list[-2]))
                    slot = (c % 2) * 1024
                    pv2 = S.op("tensor", lambda t,
                               s=g16v[:, c * 128:(c + 1) * 128],
                               o=pst[:, slot:slot + 128]:
                               t.transpose(o, s, ident[:, :]), waits=w2,
                               inc="pe")
                    tcp_list.append(S.op(
                        "vector", lambda v, s=pst[:, slot:slot + 128],
                        o=xT2[:, c * 4096 + i * 128: c * 4096 + (i + 1) * 128]:
                        v.tensor_copy(o, s), waits=[("pe", pv2)], inc="dv"))
                tr_pe.append(S.v("pe"))
            prev_pe = S.v("pe")
            lw, c1, c2, lb_ = load_linear(i1, prev_pe)
            mid_dv = compute_mid(xT2, c1)
            ev_hist = []
            hst = {0: None, 1: None}
            prev_st = {}
            for i in range(NTILE):
                b = i % 2
                hv = he[:, b * 1024:(b + 1) * 1024]
                wx = []
                if hst[b] is not None:
                    wx.append(hst[b])
                if (blk, i) in prev_st:
                    wx.append(prev_st[(blk, i)])
                lx = S.dma("scalar", hv, h_ap[i * 128:(i + 1) * 128, :], wx,
                           ch=f"l3hl{b}")
                evs = []
                for n in range(2):
                    ps = psy[:, n * 512:(n + 1) * 512]
                    w = []
                    if i == 0 and n == 0:
                        w += [lw, ("dv", mid_dv), lb_, ("dv", c2)]
                    if len(ev_hist) >= 2:
                        w.append(("dv", ev_hist[-2]))
                    pv = None
                    for c in range(DCH):
                        pv = S.op("tensor", lambda t, o=ps,
                                  l=xT2[:, c * 4096 + i * 128:
                                        c * 4096 + (i + 1) * 128],
                                  r=wsb[:, c * 1024 + n * 512:
                                        c * 1024 + n * 512 + 512],
                                  st_=(c == 0), sp=False:
                                  t.matmul(o, l, r, start=st_, stop=sp),
                                  waits=(w if c == 0 else []), inc=None)
                    pv = S.op("tensor", lambda t, o=ps,
                              l=mid[:, i * 128:(i + 1) * 128],
                              r=lb16[:, n * 512:n * 512 + 512]:
                              t.matmul(o, l, r, start=False, stop=False),
                              inc=None)
                    pv = S.op("tensor", lambda t, o=ps,
                              l=ones[:, :], r=bia[:, n * 512:n * 512 + 512]:
                              t.matmul(o, l, r, start=False, stop=True),
                              inc="pe")
                    ev = S.op("vector", lambda v,
                              o=hv[:, n * 512:(n + 1) * 512], s=ps:
                              v.tensor_tensor(o, o, s, ALU.add),
                              waits=[("pe", pv), lx], inc="dv")
                    ev_hist.append(ev)
                    evs.append(ev)
                hst[b] = S.dma("gpsimd", h_ap[i * 128:(i + 1) * 128, :], hv,
                               [("dv", evs[-1])], ch=f"l3hs{b}")
                prev_st[(blk + 1, i)] = hst[b]
            prev_pe = S.v("pe")
            if blk < 2:
                S.flush()
                emit_transpose_only(st, h_ap, tag=f"l3tp{blk}")
        S.flush()
        fin_dv, fin_st = [], {0: None, 1: None}
        for i in range(NTILE):
            b = i % 2
            xv = ysb[:, b * 1024:(b + 1) * 1024]
            hv = he[:, b * 1024:(b + 1) * 1024]
            wx = []
            if fin_st[b] is not None:
                wx.append(fin_st[b])
            l1 = S.dma("sync", hv, h_ap[i * 128:(i + 1) * 128, :], wx,
                       ch=f"l3fh{b}")
            l2 = S.dma("scalar", xv, x_ap[i * 128:(i + 1) * 128, :],
                       list(wx), ch=f"l3fx{b}")
            d = S.op("vector", lambda v, o=xv, h_=hv:
                     v.scalar_tensor_tensor(o, h_, 0.5, o, ALU.mult, ALU.add),
                     waits=[l1, l2], inc="dv")
            fin_dv.append(d)
            fin_st[b] = S.dma("gpsimd", x_ap[i * 128:(i + 1) * 128, :], xv,
                              [("dv", d)], ch=f"l3fs{b}")
        S.flush()
    return fin_st


def emit_plain_cache(st, src_name, cache_name, K, N, tag):
    S, sb = st.S, st.sb
    wsrc = st.din(src_name, (K, N))
    cds = st.dint(cache_name, (128, (K // 128) * N), BF16)
    KC = K // 128
    with contextlib.ExitStack() as ctx:
        stg = st.alloc(ctx, tag + "_stage", [128, 2 * N], F32)
        img = st.alloc(ctx, tag + "_img", [128, KC * N], BF16)
        cast_ac = []
        for c in range(KC):
            cb = c % 2
            stgv = stg[:, cb * N:(cb + 1) * N]
            w = [("ac", cast_ac[-2])] if len(cast_ac) >= 2 else []
            ld = S.dma("sync", stgv, wsrc.ap()[c * 128:(c + 1) * 128, :], w,
                       ch=tag + f"l{cb}")
            cast_ac.append(S.op(
                "scalar", lambda a, i_=stgv, o=img[:, c * N:(c + 1) * N]:
                a.activation(o, i_, AF.Copy, bias=0.0, scale=1.0),
                waits=[ld], inc="ac"))
        done = S.dma("gpsimd", cds.ap(), img[:, :], [("ac", cast_ac[-1])],
                     ch=tag + "s")
        S.flush()
    return [done]


def _sv(t_ap, off, pairs):
    return bass.AP(t_ap.tensor, t_ap.offset + off, pairs)


def emit_attn01_qkv(st, variant, qkv_dram, cache_name, cache_gd=0,
                    bias_name=None, tag="aq"):
    S, sb = st.S, st.sb
    xT = sb["xT"]
    cw = st.dram[cache_name]
    with contextlib.ExitStack() as ctx:
        wq = st.alloc(ctx, tag + "_w", [128, 8 * 3072], BF16)
        qkvt = st.alloc(ctx, tag + "_t", [128, 2 * 3072], BF16)
        ps = st.palloc(ctx, tag + "_ps", [128, 2 * 512])
        msq = S.op("vector", lambda v: v.memset(qkvt[:, :], 0.0), inc="dv")
        lw = S.dma("sync", wq[:, :], cw.ap(), list(cache_gd) if cache_gd else [],
                   ch=tag + "w")
        lbi = lon2 = None
        if bias_name is not None:
            bi = st.alloc(ctx, tag + "_bi", [1, 3072], F32)
            on2 = st.alloc(ctx, tag + "_on", [1, 128], F32)
            bi_d = st.din(bias_name, (1, 3072))
            on2_d = st.din("c_ones_f32", (1, 128))
            lbi = S.dma("sync", bi[:, :], bi_d.ap(), ch=tag + "bi")
            lon2 = S.dma("sync", on2[:, :], on2_d.ap(), ch=tag + "on")
        lnw = None
        if variant == "l0":
            qf32 = st.alloc(ctx, tag + "_f32", [128, 2 * 1024], F32)
            sq = st.alloc(ctx, tag + "_sq", [128, 1024], F32)
            rb = st.alloc(ctx, tag + "_rb", [128, 1024], F32)
            ss = st.alloc(ctx, tag + "_ss", [128, 64], F32)
            ms0 = S.op("vector", lambda v: v.memset(qf32[:, :], 0.0), inc="dv")
            ms0 = S.op("vector", lambda v: v.memset(ss[:, :], 0.0),
                       waits=[("dv", ms0)], inc="dv")
            ms0 = S.op("vector", lambda v: v.memset(sq[:, :], 0.0),
                       waits=[("dv", ms0)], inc="dv")
            ms0 = S.op("vector", lambda v: v.memset(rb[:, :], 0.0),
                       waits=[("dv", ms0)], inc="dv")
            cs = st.alloc(ctx, tag + "_cs", [128, 2 * 512], F32)
            qknw = st.alloc(ctx, tag + "_nw", [128, 2048], F32)
            cos_d = st.din("c_rope_cos", (128, 16 * 32))
            sin_d = st.din("c_rope_sin", (128, 16 * 32))
            qnw_d = st.din("c_qknw", (128, 2048))
            lnw = S.dma("sync", qknw[:, :], qnw_d.ap(), ch=tag + "c3")
        ev_hist = []
        epi_hist = []
        st_hist = {0: None, 1: None}
        for i in range(NTILE):
            b = i % 2
            tq = qkvt[:, b * 3072:(b + 1) * 3072]
            if variant == "l0":
                tc = i % 16
                lc = S.dma("sync", cs[:, 0:32],
                           cos_d.ap()[:, tc * 32:(tc + 1) * 32],
                           [("dv", epi_hist[-1])] if epi_hist else [],
                           ch=tag + "cs")
                ls2 = S.dma("sync", cs[:, 512:544],
                            sin_d.ap()[:, tc * 32:(tc + 1) * 32], ch=tag + "sn")
            for sess in range(6):
                pb = sess % 2
                psv = ps[:, pb * 512:(pb + 1) * 512]
                w = [lw] if (i == 0 and sess == 0) else []
                if i == 0 and sess == 0:
                    w.append(("dv", msq))
                if len(ev_hist) >= 2:
                    w.append(ev_hist[-2])
                if sess == 0 and st_hist[b] is not None:
                    w.append(st_hist[b])
                pv = None
                haslast = bias_name is None
                for c in range(DCH):
                    pv = S.op("tensor", lambda t, o=psv,
                              l=xT[:, c * 4096 + i * 128: c * 4096 + (i + 1) * 128],
                              r=wq[:, c * 3072 + sess * 512: c * 3072 + (sess + 1) * 512],
                              st_=(c == 0), sp=(haslast and c == DCH - 1):
                              t.matmul(o, l, r, start=st_, stop=sp),
                              waits=(w if c == 0 else []),
                              inc=("pe" if haslast and c == DCH - 1 else None))
                if bias_name is not None:
                    wb2 = [lbi, lon2] if (i == 0 and sess == 0) else []
                    pv = S.op("tensor", lambda t, o=psv,
                              l=on2[:, :],
                              r=bi[:, sess * 512:(sess + 1) * 512]:
                              t.matmul(o, l, r, start=False, stop=True),
                              waits=wb2, inc="pe")
                if variant in ("l1", "l2") or sess >= 4:
                    ev_hist.append(("ac", S.op(
                        "scalar", lambda a, o=tq[:, sess * 512:(sess + 1) * 512],
                        i_=psv: a.activation(o, i_, AF.Copy, bias=0.0,
                                             scale=1.0),
                        waits=[("pe", pv)], inc="ac")))
                else:
                    qb = sess // 2
                    dst = qf32[:, qb * 1024 + (sess % 2) * 512:
                               qb * 1024 + (sess % 2 + 1) * 512]
                    wcp = [("pe", pv)]
                    if i == 0:
                        wcp.append(("dv", ms0))
                    ev_hist.append(("dv", S.op(
                        "vector", lambda v, o=dst, i_=psv: v.tensor_copy(o, i_),
                        waits=wcp, inc="dv")))
                if variant == "l0" and sess in (1, 3):
                    qb = sess // 2
                    src = qf32[:, qb * 1024:(qb + 1) * 1024]
                    ssv = ss[:, qb * 32:qb * 32 + 16]
                    rsv = ss[:, qb * 32 + 16:qb * 32 + 32]
                    wv = [ev_hist[-1]]
                    if i == 0 and sess == 1:
                        wv.append(lnw)
                    a1 = S.op("scalar", lambda a, o=sq[:, :], i_=src:
                              a.activation(o, i_, AF.Square), waits=wv,
                              inc="ac")
                    d = S.op("vector", lambda v, o=ssv:
                             v.tensor_reduce(o, _sv(sq[:, :], 0,
                                                    [[1024, 128], [64, 16],
                                                     [1, 64]]),
                                             AX.X, ALU.add),
                             waits=[("ac", a1)], inc="dv")
                    d = S.op("vector", lambda v, o=ssv:
                             v.tensor_scalar(o, o, 1.0 / 64, 1e-6, ALU.mult,
                                             ALU.add), waits=[("dv", d)],
                             inc="dv")
                    a2 = S.op("scalar", lambda a, o=rsv, i_=ssv:
                              a.activation(o, i_, AF.Sqrt),
                              waits=[("dv", d)], inc="ac")
                    d = S.op("vector", lambda v, o=rsv: v.reciprocal(o, o),
                             waits=[("ac", a2)], inc="dv")
                    sv3 = _sv(src, 0, [[2048, 128], [64, 16], [1, 64]])
                    rb3 = _sv(rsv, 0, [[64, 128], [1, 16], [0, 64]])
                    d = S.op("vector", lambda v, a_=sv3, b_=rb3: v.tensor_tensor(
                        a_, a_, b_, ALU.mult), waits=[("dv", d)], inc="dv")
                    d = S.op("vector", lambda v, o=src, nw=qknw[:, qb * 1024:
                                                                (qb + 1) * 1024]:
                             v.tensor_tensor(o, o, nw, ALU.mult),
                             waits=[("dv", d)], inc="dv")
                    x1 = _sv(src, 0, [[2048, 128], [64, 16], [1, 32]])
                    x2 = _sv(src, 32, [[2048, 128], [64, 16], [1, 32]])
                    r1 = _sv(rb[:, :], 0, [[1024, 128], [64, 16], [1, 32]])
                    r2 = _sv(rb[:, :], 32, [[1024, 128], [64, 16], [1, 32]])
                    u1 = _sv(sq[:, :], 0, [[1024, 128], [64, 16], [1, 32]])
                    cosb = _sv(cs[:, :], 0, [[1024, 128], [0, 16], [1, 32]])
                    sinb = _sv(cs[:, :], 512, [[1024, 128], [0, 16], [1, 32]])
                    o1 = _sv(tq[:, :], qb * 1024, [[3072 * 2, 128], [64, 16], [1, 32]])
                    o2 = _sv(tq[:, :], qb * 1024 + 32, [[3072 * 2, 128], [64, 16], [1, 32]])
                    wv2 = [("dv", d), lc, ls2] if sess == 1 else [("dv", d)]
                    d = S.op("vector", lambda v, a_=r1, b_=x1, c_=cosb:
                             v.tensor_tensor(a_, b_, c_, ALU.mult),
                             waits=wv2, inc="dv")
                    d = S.op("vector", lambda v, a_=r2, b_=x1, c_=sinb:
                             v.tensor_tensor(a_, b_, c_, ALU.mult),
                             waits=[("dv", d)], inc="dv")
                    d = S.op("vector", lambda v, a_=u1, b_=x2, c_=sinb:
                             v.tensor_tensor(a_, b_, c_, ALU.mult),
                             waits=[("dv", d)], inc="dv")
                    w_o1 = [("dv", d)]
                    if st_hist[b] is not None:
                        w_o1.append(st_hist[b])
                    d = S.op("vector", lambda v, a_=o1, b_=r1, c_=u1:
                             v.tensor_tensor(a_, b_, c_, ALU.subtract),
                             waits=w_o1, inc="dv")
                    d = S.op("vector", lambda v, a_=u1, b_=x2, c_=cosb:
                             v.tensor_tensor(a_, b_, c_, ALU.mult),
                             waits=[("dv", d)], inc="dv")
                    w_o2 = [("dv", d)]
                    if st_hist[b] is not None:
                        w_o2.append(st_hist[b])
                    d = S.op("vector", lambda v, a_=o2, b_=r2, c_=u1:
                             v.tensor_tensor(a_, b_, c_, ALU.add),
                             waits=w_o2, inc="dv")
                    epi_hist.append(d)
                    ev_hist.append(("dv", d))
            wv = [ev_hist[-1]]
            if variant == "l0":
                wv.append(("dv", epi_hist[-1]))
            st_hist[b] = S.dma("gpsimd", qkv_dram[i * 128:(i + 1) * 128, :],
                               tq, wv, ch=tag + f"s{b}")
        S.flush()


def emit_attn01_core(st, variant, qkv_dram, x_ap, wo_cache, bout_name=None):
    S, sb = st.S, st.sb
    ident = sb["ident"]
    cwo = st.dram[wo_cache]
    with contextlib.ExitStack() as ctx:
        qst = st.alloc(ctx, "ac_qst", [128, 1024], BF16)
        kst = st.alloc(ctx, "ac_kst", [128, 1024], BF16)
        vst = st.alloc(ctx, "ac_vst", [128, 1024], BF16)
        qf = st.alloc(ctx, "ac_qf", [64, 2048], BF16)
        kf = st.alloc(ctx, "ac_kf", [64, 2048], BF16)
        asb = st.alloc(ctx, "ac_a", [128, 2048], BF16)
        at = st.alloc(ctx, "ac_at", [128, 2048], BF16)
        ao = st.alloc(ctx, "ac_ao", [128, 8 * 2048], BF16)
        rl = st.alloc(ctx, "ac_rl", [128, 32], F32)
        nm = st.alloc(ctx, "ac_nm", [128, 4], F32)
        msk = st.alloc(ctx, "ac_msk", [128, 128], F32)
        alb = st.alloc(ctx, "ac_alb", [1, 2048], F32)
        wo = st.alloc(ctx, "ac_wo", [128, 8 * 1024], BF16)
        bo = st.alloc(ctx, "ac_bo", [1, 1024], F32)
        ones = st.alloc(ctx, "ac_on", [1, 128], F32)
        xe = st.alloc(ctx, "ac_xe", [128, 2 * 1024], F32)
        psS = st.palloc(ctx, "ac_pS", [128, 2048])
        pst = st.palloc(ctx, "ac_pt", [128, 2048], BF16)
        pav = st.palloc(ctx, "ac_pav", [128, 512])

        msk_d = st.din("c_causal_mask", (128, 128))
        on_d = st.din("c_ones_f32", (1, 128))
        lm = S.dma("sync", msk[:, :], msk_d.ap(), ch="acm")
        lo = S.dma("sync", wo[:, :], cwo.ap(), ch="acwo")
        lon = S.dma("sync", ones[:, :], on_d.ap(), ch="acon")
        if variant == "l1":
            alb_d = st.din("c_alibi", (16, 2048))
            bob = st.alloc(ctx, "ac_bob", [128, 1024], F32)
            bo_d = st.din("c_l1_bob", (128, 1024))
            lb = S.dma("sync", bob[:, :], bo_d.ap(), ch="acbo")
        SCALE = 0.125

        tr_hist = []
        sm_hist = []
        ao_pe = {}
        ld_hist = {}
        evl = []
        stl = {0: None, 1: None}
        for s in range(SEQ_PER_CORE):
            for h in range(16):
                wv = []
                if (s, h) in ld_hist:
                    pass
                prev = ld_hist.get(s * 16 + h - 1)
                if prev is not None:
                    wv = [("pe", prev)]
                qsrc = _sv(qkv_dram, (s * 2048) * 3072 + h * 64,
                           [[3072 * 128, 16], [3072, 128], [1, 64]])
                ksrc = _sv(qkv_dram, (s * 2048) * 3072 + 1024 + h * 64,
                           [[3072 * 128, 16], [3072, 128], [1, 64]])
                vsrc = _sv(qkv_dram, (s * 2048) * 3072 + 2048 + h * 64,
                           [[3072 * 128, 16], [3072, 128], [1, 64]])
                lq = S.dma("sync",
                           bass.AP(qst[:, :].tensor, 0, [[1024, 128], [64, 16], [1, 64]]),
                           bass.AP(qkv_dram.tensor,
                                   (s * 2048) * 3072 + h * 64,
                                   [[3072, 128], [3072 * 128, 16], [1, 64]]),
                           wv, ch="acq")
                lk = S.dma("sync",
                           bass.AP(kst[:, :].tensor, 0, [[1024, 128], [64, 16], [1, 64]]),
                           bass.AP(qkv_dram.tensor,
                                   (s * 2048) * 3072 + 1024 + h * 64,
                                   [[3072, 128], [3072 * 128, 16], [1, 64]]),
                           list(wv), ch="ack")
                lv = S.dma("sync",
                           bass.AP(vst[:, :].tensor, 0, [[1024, 128], [64, 16], [1, 64]]),
                           bass.AP(qkv_dram.tensor,
                                   (s * 2048) * 3072 + 2048 + h * 64,
                                   [[3072, 128], [3072 * 128, 16], [1, 64]]),
                           list(wv), ch="acv")
                if variant == "l1":
                    la = S.dma("sync", alb[:, :],
                               st.dram["c_alibi"].ap()[h:h + 1, :],
                               list(wv), ch="acal")
                tq_pe = []
                for tens, stg_, fm, lds in ((0, qst, qf, lq), (1, kst, kf, lk)):
                    for tc2 in range(16):
                        k2 = len(tr_hist)
                        w2 = [lds] if tc2 == 0 else []
                        if len(tr_hist) >= 2:
                            w2.append(("dv", tr_hist[-2]))
                        slot = (k2 % 2) * 1024
                        pv = S.op("tensor", lambda t,
                                  s_=stg_[:, tc2 * 64:(tc2 + 1) * 64],
                                  o=pst[0:64, slot:slot + 128]:
                                  t.transpose(o, s_, ident[:, :]),
                                  waits=w2, inc="pe")
                        tr_hist.append(S.op(
                            "vector", lambda v, s_=pst[0:64, slot:slot + 128],
                            o=fm[:, tc2 * 128:(tc2 + 1) * 128]:
                            v.tensor_copy(o, s_),
                            waits=[("pe", pv)], inc="dv"))
                qk_dv = tr_hist[-1]
                for qt in range(16):
                    kext = (qt + 1) * 128
                    nses = (kext + 511) // 512
                    w0 = [("dv", qk_dv)] if qt == 0 else []
                    if sm_hist:
                        w0.append(("ac", sm_hist[-1][0]))
                    pv = None
                    for sess2 in range(nses):
                        n0 = sess2 * 512
                        nn = min(512, kext - n0)
                        pv = S.op("tensor", lambda t, o=psS[:, n0:n0 + nn],
                                  l=qf[:, qt * 128:(qt + 1) * 128],
                                  r=kf[:, n0:n0 + nn]:
                                  t.matmul(o, l, r, start=True, stop=(variant == "l0")),
                                  waits=(w0 if sess2 == 0 else []),
                                  inc=("pe" if variant == "l0" and
                                       sess2 == nses - 1 else None))
                        if variant == "l1":
                            wal = [la] if sess2 == 0 and qt == 0 else []
                            if sess2 == 0 and qt == 0 and h == 0 and s == 0:
                                wal.append(lon)
                            pv = S.op("tensor", lambda t, o=psS[:, n0:n0 + nn],
                                      l=ones[:, :], r=alb[:, n0:n0 + nn]:
                                      t.matmul(o, l, r, start=False, stop=True),
                                      waits=wal,
                                      inc=("pe" if sess2 == nses - 1 else None))
                    dmv = S.op("vector", lambda v,
                               o=psS[:, qt * 128:qt * 128 + 128]:
                               v.tensor_tensor(o, o, msk[:, :], ALU.add),
                               waits=[("pe", pv)] + ([lm] if qt == 0 and h == 0
                                                     and s == 0 else []),
                               inc="dv")
                    d1 = S.op("vector", lambda v, o=nm[:, 0:1],
                              i_=psS[:, 0:kext]:
                              v.reduce_max(o, i_, AX.X, negate=True),
                              waits=[("dv", dmv)], inc="dv")
                    d2 = S.op("vector", lambda v, o=nm[:, 1:2], i_=nm[:, 0:1]:
                              v.tensor_scalar(o, i_, SCALE, None, ALU.mult),
                              waits=[("dv", d1)], inc="dv")
                    a1 = S.op("scalar", lambda a, o=asb[:, 0:kext],
                              i_=psS[:, 0:kext], b_=nm[:, 1:2],
                              l_=nm[:, 2:3]:
                              a.activation(o, i_, AF.Exp, bias=b_, scale=SCALE,
                                           accum_out=l_),
                              waits=[("dv", d2)] +
                              ([("dv", sm_hist[-1][1])] if sm_hist else []),
                              inc="ac")
                    dr = S.op("vector", lambda v, o=nm[:, 3:4],
                              i_=nm[:, 2:3]: v.reciprocal(o, i_),
                              waits=[("ac", a1)], inc="dv")
                    dn = S.op("vector", lambda v, o=asb[:, 0:kext],
                              s_=nm[:, 3:4]:
                              v.tensor_scalar(o, o, s_, None, ALU.mult),
                              waits=[("dv", dr)], inc="dv")
                    last_at = None
                    for kc in range(qt + 1):
                        k2 = len(tr_hist)
                        w2 = [("dv", dn)] if kc == 0 else []
                        if len(tr_hist) >= 2:
                            w2.append(("dv", tr_hist[-2]))
                        slot = (k2 % 2) * 1024
                        pv2 = S.op("tensor", lambda t,
                                   s_=asb[:, kc * 128:(kc + 1) * 128],
                                   o=pst[:, slot:slot + 128]:
                                   t.transpose(o, s_, ident[:, :]),
                                   waits=w2, inc="pe")
                        last_at = S.op(
                            "vector", lambda v, s_=pst[:, slot:slot + 128],
                            o=at[:, kc * 128:(kc + 1) * 128]:
                            v.tensor_copy(o, s_), waits=[("pe", pv2)],
                            inc="dv")
                        tr_hist.append(last_at)
                    pv3 = None
                    hb2 = (h % 2) * 64
                    for kc in range(qt + 1):
                        pv3 = S.op("tensor", lambda t,
                                   o=pav[hb2:hb2 + 64, 0:128],
                                   l=vst[:, kc * 64:(kc + 1) * 64],
                                   r=at[:, kc * 128:(kc + 1) * 128],
                                   st_=(kc == 0), sp=(kc == qt),
                                   tp=(0, hb2):
                                   t.matmul(o, l, r, start=st_, stop=sp,
                                            tile_position=tp),
                                   waits=([("dv", last_at), lv] if kc == 0
                                          else []),
                                   inc=("pe" if kc == qt else None))
                    dav = S.op("vector", lambda v,
                               o=ao[hb2:hb2 + 64,
                                    (h // 2) * 2048 + qt * 128:
                                    (h // 2) * 2048 + (qt + 1) * 128],
                               i_=pav[hb2:hb2 + 64, 0:128]: v.tensor_copy(o, i_),
                               waits=[("pe", pv3)], inc="dv")
                    sm_hist.append((a1, dav))
                ld_hist[s * 16 + h] = S.v("pe")
            ao_pe[s] = S.v("pe")
            for tt in range(16):
                gi = s * 16 + tt
                b = tt % 2
                xev = xe[:, b * 1024:(b + 1) * 1024]
                wv = []
                if stl[b] is not None:
                    wv.append(stl[b])
                lx = S.dma("scalar", xev,
                           x_ap[s * 2048 + tt * 128: s * 2048 + (tt + 1) * 128, :],
                           wv, ch=f"acxl{b}")
                for n in range(2):
                    psv = psS[:, n * 512:(n + 1) * 512]
                    w3 = [("dv", sm_hist[-1][1])] if tt == 0 and n == 0 else []
                    if tt == 0 and n == 0:
                        w3.append(lo)
                    if len(evl) >= 2:
                        w3.append(("dv", evl[-2]))
                    pv = None
                    for c in range(DCH):
                        pv = S.op("tensor", lambda t, o=psv,
                                  l=ao[:, c * 2048 + tt * 128: c * 2048 + (tt + 1) * 128],
                                  r=wo[:, c * 1024 + n * 512: c * 1024 + n * 512 + 512],
                                  st_=(c == 0), sp=(c == DCH - 1):
                                  t.matmul(o, l, r, start=st_, stop=sp),
                                  waits=(w3 if c == 0 else []),
                                  inc=("pe" if c == DCH - 1 else None))
                    ev = S.op("vector", lambda v, o=xev[:, n * 512:(n + 1) * 512],
                              p_=psv:
                              v.tensor_tensor(o, o, p_, ALU.add),
                              waits=[("pe", pv), lx], inc="dv")
                    if variant == "l1":
                        ev = S.op("vector", lambda v,
                                  o=xev[:, n * 512:(n + 1) * 512],
                                  b_=bob[:, n * 512:(n + 1) * 512]:
                                  v.tensor_tensor(o, o, b_, ALU.add),
                                  waits=[("dv", ev)] +
                                  ([lb] if tt == 0 and n == 0 else []),
                                  inc="dv")
                    evl.append(ev)
                stl[b] = S.dma("gpsimd",
                               x_ap[s * 2048 + tt * 128: s * 2048 + (tt + 1) * 128, :],
                               xev, [("dv", evl[-1])], ch=f"acxs{b}")
        S.flush()


def emit_compose_cache(st, cA, cB, cout, cache_gds, tag):
    S, sb = st.S, st.sb
    a_d, b_d = st.dram[cA], st.dram[cB]
    o_d = st.dint(cout, (128, 8 * 3072), BF16)
    with contextlib.ExitStack() as ctx:
        asb = st.alloc(ctx, tag + "_a", [128, 8 * 1024], BF16)
        bsb = st.alloc(ctx, tag + "_b", [128, 8 * 3072], BF16)
        osb = st.alloc(ctx, tag + "_o", [128, 8 * 3072], BF16)
        ps = st.palloc(ctx, tag + "_p", [128, 2 * 512])
        la = S.dma("sync", asb[:, :], a_d.ap(), list(cache_gds), ch=tag + "a")
        lb = S.dma("sync", bsb[:, :], b_d.ap(), ch=tag + "b")
        evh = []
        for m in range(8):
            for n in range(6):
                pb = (m * 6 + n) % 2
                psv = ps[:, pb * 512:(pb + 1) * 512]
                w = [la, lb] if (m == 0 and n == 0) else []
                if len(evh) >= 2:
                    w.append(("ac", evh[-2]))
                pv = None
                for c in range(DCH):
                    pv = S.op("tensor", lambda t, o=psv,
                              l=asb[:, c * 1024 + m * 128: c * 1024 + (m + 1) * 128],
                              r=bsb[:, c * 3072 + n * 512: c * 3072 + (n + 1) * 512],
                              st_=(c == 0), sp=(c == DCH - 1):
                              t.matmul(o, l, r, start=st_, stop=sp),
                              waits=(w if c == 0 else []),
                              inc=("pe" if c == DCH - 1 else None))
                evh.append(S.op(
                    "scalar", lambda a, o=osb[:, m * 3072 + n * 512:
                                              m * 3072 + (n + 1) * 512],
                    i_=psv: a.activation(o, i_, AF.Copy, bias=0.0, scale=1.0),
                    waits=[("pe", pv)], inc="ac"))
        done = S.dma("gpsimd", o_d.ap(), osb[:, :], [("ac", evh[-1])],
                     ch=tag + "s")
        S.flush()
    return [done]


def emit_l2_fusionw(st):
    S, sb = st.S, st.sb
    fw_d = st.din("L2_attn_fusion_w", (1, 2))
    on_d = st.din("c_ones_f32", (1, 128))
    blo_d = st.din("c_l2_blo", (128, 1024))
    bgo_d = st.din("c_l2_bgo", (128, 1024))
    wsb = st.sb["l2a_w"]
    blo = st.sb["l2a_blo"]
    bgo = st.sb["l2a_bgo"]
    with contextlib.ExitStack() as ctx:
        fw = st.alloc(ctx, "l2a_fw", [1, 8], F32)
        on = st.alloc(ctx, "l2a_on", [1, 128], F32)
        pw = st.palloc(ctx, "l2a_pw", [128, 512])
        l1 = S.dma("sync", fw[:, 0:2], fw_d.ap(), ch="l2afw")
        l2 = S.dma("sync", on[:, :], on_d.ap(), ch="l2aon")
        l3 = S.dma("sync", blo[:, :], blo_d.ap(), ch="l2ablo")
        l4 = S.dma("sync", bgo[:, :], bgo_d.ap(), ch="l2abgo")
        d = S.op("vector", lambda v: v.reduce_max(fw[:, 2:3], fw[:, 0:2], AX.X,
                                                  negate=True),
                 waits=[l1], inc="dv")
        a = S.op("scalar", lambda a_: a_.activation(fw[:, 4:6], fw[:, 0:2],
                                                    AF.Exp, bias=fw[:, 2:3],
                                                    accum_out=fw[:, 3:4]),
                 waits=[("dv", d)], inc="ac")
        d = S.op("vector", lambda v: v.reciprocal(fw[:, 6:7], fw[:, 3:4]),
                 waits=[("ac", a)], inc="dv")
        d = S.op("vector", lambda v: v.tensor_scalar(fw[:, 4:6], fw[:, 4:6],
                                                     fw[:, 6:7], None,
                                                     ALU.mult),
                 waits=[("dv", d)], inc="dv")
        p = S.op("tensor", lambda t: t.matmul(pw[:, 0:2], on[:, :], fw[:, 4:6]),
                 waits=[("dv", d), l2], inc="pe")
        d = S.op("vector", lambda v: v.tensor_copy(wsb[:, :], pw[:, 0:2]),
                 waits=[("pe", p)], inc="dv")
        d = S.op("vector", lambda v: v.tensor_scalar(blo[:, :], blo[:, :],
                                                     wsb[:, 0:1], None,
                                                     ALU.mult),
                 waits=[("dv", d), l3], inc="dv")
        d = S.op("vector", lambda v: v.tensor_scalar(bgo[:, :], bgo[:, :],
                                                     wsb[:, 1:2], None,
                                                     ALU.mult),
                 waits=[("dv", d), l4], inc="dv")
        S.flush()


def emit_attn2_core(st, qkv_ap, win, fused_ap, wo_cache, first_branch,
                    wcol):
    S, sb = st.S, st.sb
    ident = sb["ident"]
    cwo = st.dram[wo_cache]
    wsb = sb["l2a_w"]
    bb = sb["l2a_blo"] if first_branch else sb["l2a_bgo"]
    NW = 2048 // win
    QT = win // 128
    tag = "w%d" % win
    with contextlib.ExitStack() as ctx:
        qst = st.alloc(ctx, tag + "_qst", [128, 1024], BF16)
        kst = st.alloc(ctx, tag + "_kst", [128, 1024], BF16)
        vst = st.alloc(ctx, tag + "_vst", [128, 1024], BF16)
        qf = st.alloc(ctx, tag + "_qf", [64, 2048], BF16)
        kf = st.alloc(ctx, tag + "_kf", [64, 2048], BF16)
        asb = st.alloc(ctx, tag + "_a", [128, 1024], BF16)
        at = st.alloc(ctx, tag + "_at", [128, 1024], BF16)
        ao = st.alloc(ctx, tag + "_ao", [128, 8 * 2048], BF16)
        rl = st.alloc(ctx, tag + "_rl", [128, 32], F32)
        nm = st.alloc(ctx, tag + "_nm", [128, 4], F32)
        msk = st.alloc(ctx, tag + "_msk", [128, 128], F32)
        wo = st.alloc(ctx, tag + "_wo", [128, 8 * 1024], BF16)
        xe = st.alloc(ctx, tag + "_xe", [128, 2 * 1024], F32)
        psS = st.palloc(ctx, tag + "_pS", [128, 1024])
        pst = st.palloc(ctx, tag + "_pt", [128, 2048], BF16)
        pav = st.palloc(ctx, tag + "_pav", [128, 512])

        msk_d = st.din("c_l2_mask", (128, 128))
        lm = S.dma("sync", msk[:, :], msk_d.ap(), ch=tag + "m")
        lo = S.dma("sync", wo[:, :], cwo.ap(), ch=tag + "wo")
        SCALE = 0.125

        tr_hist, sm_hist = [], []
        ld_hist = {}
        evl = []
        stl = {0: None, 1: None}
        for s in range(SEQ_PER_CORE):
            for h in range(16):
                wv = []
                prev = ld_hist.get(s * 16 + h - 1)
                if prev is not None:
                    wv = [("pe", prev)]
                lq = S.dma("sync",
                           bass.AP(qst[:, :].tensor, 0, [[1024, 128], [64, 16], [1, 64]]),
                           bass.AP(qkv_ap.tensor,
                                   qkv_ap.offset + (s * 2048) * 3072 + h * 64,
                                   [[3072, 128], [3072 * 128, 16], [1, 64]]),
                           wv, ch=tag + "q")
                lk = S.dma("sync",
                           bass.AP(kst[:, :].tensor, 0, [[1024, 128], [64, 16], [1, 64]]),
                           bass.AP(qkv_ap.tensor,
                                   qkv_ap.offset + (s * 2048) * 3072 + 1024 + h * 64,
                                   [[3072, 128], [3072 * 128, 16], [1, 64]]),
                           list(wv), ch=tag + "k")
                lv = S.dma("sync",
                           bass.AP(vst[:, :].tensor, 0, [[1024, 128], [64, 16], [1, 64]]),
                           bass.AP(qkv_ap.tensor,
                                   qkv_ap.offset + (s * 2048) * 3072 + 2048 + h * 64,
                                   [[3072, 128], [3072 * 128, 16], [1, 64]]),
                           list(wv), ch=tag + "v")
                for tens, stg_, fm, lds in ((0, qst, qf, lq), (1, kst, kf, lk)):
                    for tc2 in range(16):
                        k2 = len(tr_hist)
                        w2 = [lds] if tc2 == 0 else []
                        if len(tr_hist) >= 2:
                            w2.append(("dv", tr_hist[-2]))
                        slot = (k2 % 2) * 1024
                        pv = S.op("tensor", lambda t,
                                  s_=stg_[:, tc2 * 64:(tc2 + 1) * 64],
                                  o=pst[0:64, slot:slot + 128]:
                                  t.transpose(o, s_, ident[:, :]),
                                  waits=w2, inc="pe")
                        tr_hist.append(S.op(
                            "vector", lambda v, s_=pst[0:64, slot:slot + 128],
                            o=fm[:, tc2 * 128:(tc2 + 1) * 128]:
                            v.tensor_copy(o, s_),
                            waits=[("pe", pv)], inc="dv"))
                qk_dv = tr_hist[-1]
                for wdw in range(NW):
                    for qt in range(QT):
                        gqt = wdw * QT + qt
                        k0 = wdw * win
                        w0 = [("dv", qk_dv)] if wdw == 0 and qt == 0 else []
                        if sm_hist:
                            w0.append(("ac", sm_hist[-1][0]))
                        pv = None
                        nses = (win + 511) // 512
                        for sess2 in range(nses):
                            n0 = sess2 * 512
                            nn = min(512, win - n0)
                            pv = S.op("tensor", lambda t, o=psS[:, n0:n0 + nn],
                                      l=qf[:, gqt * 128:(gqt + 1) * 128],
                                      r=kf[:, k0 + n0:k0 + n0 + nn]:
                                      t.matmul(o, l, r, start=True, stop=True),
                                      waits=(w0 if sess2 == 0 else []),
                                      inc=("pe" if sess2 == nses - 1 else None))
                        dmv = S.op("vector", lambda v,
                                   o=psS[:, qt * 128:qt * 128 + 128]:
                                   v.tensor_tensor(o, o, msk[:, :], ALU.add),
                                   waits=[("pe", pv)] +
                                   ([lm] if s == 0 and h == 0 and wdw == 0
                                    and qt == 0 else []),
                                   inc="dv")
                        if (qt + 1) * 128 < win:
                            dmv = S.op("vector", lambda v,
                                       o=psS[:, (qt + 1) * 128:win]:
                                       v.tensor_scalar(o, o, 8.0, None,
                                                       ALU.add),
                                       waits=[("dv", dmv)], inc="dv")
                        d1 = S.op("vector", lambda v, o=nm[:, 0:1],
                                  i_=psS[:, 0:win]:
                                  v.reduce_max(o, i_, AX.X, negate=True),
                                  waits=[("dv", dmv)], inc="dv")
                        d2 = S.op("vector", lambda v, o=nm[:, 1:2],
                                  i_=nm[:, 0:1]:
                                  v.tensor_scalar(o, i_, SCALE, None, ALU.mult),
                                  waits=[("dv", d1)], inc="dv")
                        a1 = S.op("scalar", lambda a, o=asb[:, 0:win],
                                  i_=psS[:, 0:win], b_=nm[:, 1:2],
                                  l_=nm[:, 2:3]:
                                  a.activation(o, i_, AF.Exp, bias=b_,
                                               scale=SCALE, accum_out=l_),
                                  waits=[("dv", d2)] +
                                  ([("dv", sm_hist[-1][1])] if sm_hist else []),
                                  inc="ac")
                        dr = S.op("vector", lambda v, o=nm[:, 3:4],
                                  i_=nm[:, 2:3]: v.reciprocal(o, i_),
                                  waits=[("ac", a1)], inc="dv")
                        dn = S.op("vector", lambda v, o=asb[:, 0:win],
                                  s_=nm[:, 3:4]:
                                  v.tensor_scalar(o, o, s_, None, ALU.mult),
                                  waits=[("dv", dr)], inc="dv")
                        last_at = None
                        for kc in range(win // 128):
                            k2 = len(tr_hist)
                            w2 = [("dv", dn)] if kc == 0 else []
                            if len(tr_hist) >= 2:
                                w2.append(("dv", tr_hist[-2]))
                            slot = (k2 % 2) * 1024
                            pv2 = S.op("tensor", lambda t,
                                       s_=asb[:, kc * 128:(kc + 1) * 128],
                                       o=pst[:, slot:slot + 128]:
                                       t.transpose(o, s_, ident[:, :]),
                                       waits=w2, inc="pe")
                            last_at = S.op(
                                "vector", lambda v,
                                s_=pst[:, slot:slot + 128],
                                o=at[:, kc * 128:(kc + 1) * 128]:
                                v.tensor_copy(o, s_), waits=[("pe", pv2)],
                                inc="dv")
                            tr_hist.append(last_at)
                        pv3 = None
                        hb2 = (h % 2) * 64
                        for kc in range(win // 128):
                            kcg = wdw * QT + kc
                            pv3 = S.op("tensor", lambda t,
                                       o=pav[hb2:hb2 + 64, 0:128],
                                       l=vst[:, kcg * 64:(kcg + 1) * 64],
                                       r=at[:, kc * 128:(kc + 1) * 128],
                                       st_=(kc == 0), sp=(kc == win // 128 - 1),
                                       tp=(0, hb2):
                                       t.matmul(o, l, r, start=st_, stop=sp,
                                                tile_position=tp),
                                       waits=([("dv", last_at), lv] if kc == 0
                                              else []),
                                       inc=("pe" if kc == win // 128 - 1
                                            else None))
                        dav = S.op("vector", lambda v,
                                   o=ao[hb2:hb2 + 64,
                                        (h // 2) * 2048 + gqt * 128:
                                        (h // 2) * 2048 + (gqt + 1) * 128],
                                   i_=pav[hb2:hb2 + 64, 0:128]:
                                   v.tensor_copy(o, i_),
                                   waits=[("pe", pv3)], inc="dv")
                        sm_hist.append((a1, dav))
                        if h == 0 and s == 0:
                            pass
                ld_hist[s * 16 + h] = S.v("pe")
            for tt in range(16):
                b = tt % 2
                xev = xe[:, b * 1024:(b + 1) * 1024]
                wv = []
                if stl[b] is not None:
                    wv.append(stl[b])
                if not first_branch:
                    lx = S.dma("scalar", xev,
                               fused_ap[s * 2048 + tt * 128:
                                        s * 2048 + (tt + 1) * 128, :],
                               wv, ch=tag + f"xl{b}")
                evs = []
                for n in range(2):
                    psv = psS[:, n * 512:(n + 1) * 512]
                    w3 = [("dv", sm_hist[-1][1])] if tt == 0 and n == 0 else []
                    if tt == 0 and n == 0:
                        w3.append(lo)
                    if len(evl) >= 2:
                        w3.append(("dv", evl[-2]))
                    pv = None
                    for c in range(DCH):
                        pv = S.op("tensor", lambda t, o=psv,
                                  l=ao[:, c * 2048 + tt * 128: c * 2048 + (tt + 1) * 128],
                                  r=wo[:, c * 1024 + n * 512: c * 1024 + n * 512 + 512],
                                  st_=(c == 0), sp=(c == DCH - 1):
                                  t.matmul(o, l, r, start=st_, stop=sp),
                                  waits=(w3 if c == 0 else []),
                                  inc=("pe" if c == DCH - 1 else None))
                    if first_branch:
                        ev = S.op("vector", lambda v,
                                  o=xev[:, n * 512:(n + 1) * 512], p_=psv,
                                  s_=wsb[:, wcol:wcol + 1]:
                                  v.tensor_scalar(o, p_, s_, None, ALU.mult),
                                  waits=[("pe", pv)] + wv, inc="dv")
                    else:
                        ev = S.op("vector", lambda v,
                                  o=xev[:, n * 512:(n + 1) * 512], p_=psv,
                                  s_=wsb[:, wcol:wcol + 1]:
                                  v.scalar_tensor_tensor(o, p_, s_, o,
                                                         ALU.mult, ALU.add),
                                  waits=[("pe", pv), lx], inc="dv")
                    ev = S.op("vector", lambda v,
                              o=xev[:, n * 512:(n + 1) * 512],
                              b_=bb[:, n * 512:(n + 1) * 512]:
                              v.tensor_tensor(o, o, b_, ALU.add),
                              waits=[("dv", ev)], inc="dv")
                    evl.append(ev)
                stl[b] = S.dma("gpsimd",
                               fused_ap[s * 2048 + tt * 128:
                                        s * 2048 + (tt + 1) * 128, :],
                               xev, [("dv", evl[-1])], ch=tag + f"xs{b}")
        S.flush()


def emit_l2a_final(st, x_ap, fused_ap):
    S, sb = st.S, st.sb
    gm_d = st.din("c_l2a_gam", (128, 1024))
    bt_d = st.din("c_l2a_bet", (128, 1024))
    with contextlib.ExitStack() as ctx:
        fe = st.alloc(ctx, "l2f_fe", [128, 2 * 1024], F32)
        xe = st.alloc(ctx, "l2f_xe", [128, 2 * 1024], F32)
        gam = st.alloc(ctx, "l2f_g", [128, 1024], F32)
        bet = st.alloc(ctx, "l2f_b", [128, 1024], F32)
        stat = st.alloc(ctx, "l2f_st", [128, 12], F32)
        mv = st.alloc(ctx, "l2f_mv", [128, 2], F32)
        vec = st.alloc(ctx, "l2f_v", [128, 8], F32)
        lg = S.dma("sync", gam[:, :], gm_d.ap(), ch="l2fg")
        lb = S.dma("sync", bet[:, :], bt_d.ap(), ch="l2fb")
        dvh = []
        sth = {0: None, 1: None}
        for i in range(NTILE):
            b = i % 2
            fv = fe[:, b * 1024:(b + 1) * 1024]
            xv = xe[:, b * 1024:(b + 1) * 1024]
            wv = []
            if sth[b] is not None:
                wv.append(sth[b])
            l1 = S.dma("sync", fv, fused_ap[i * 128:(i + 1) * 128, :],
                       list(wv) + ([("dv", dvh[-2])] if len(dvh) >= 2 else []),
                       ch=f"l2ff{b}")
            l2 = S.dma("scalar", xv, x_ap[i * 128:(i + 1) * 128, :],
                       list(wv), ch=f"l2fx{b}")
            S.op("vector", lambda v, a=fv: v.bn_stats(stat[:, 0:6], a[:, 0:512]),
                 waits=[l1], inc="dv")
            S.op("vector", lambda v, a=fv: v.bn_stats(stat[:, 6:12],
                                                      a[:, 512:1024]),
                 waits=[("dv", S.v("dv"))], inc="dv")
            S.op("vector", lambda v: v.bn_aggr(mv[:, 0:2], stat[:, 0:12]),
                 waits=[("dv", S.v("dv"))], inc="dv")
            S.op("vector", lambda v: v.tensor_scalar(vec[:, 2:3], mv[:, 1:2],
                                                     1e-5, None, ALU.add),
                 waits=[("dv", S.v("dv"))], inc="dv")
            aa = S.op("scalar", lambda a: a.activation(vec[:, 3:4], vec[:, 2:3],
                                                       AF.Sqrt),
                      waits=[("dv", S.v("dv"))], inc="ac")
            S.op("vector", lambda v: v.reciprocal(vec[:, 0:1], vec[:, 3:4]),
                 waits=[("ac", aa)], inc="dv")
            S.op("vector", lambda v: v.tensor_scalar(vec[:, 1:2], mv[:, 0:1],
                                                     vec[:, 0:1], -1.0,
                                                     ALU.mult, ALU.mult),
                 waits=[("dv", S.v("dv"))], inc="dv")
            a2 = S.op("scalar", lambda a, o=fv, i_=fv:
                      a.activation(o, i_, AF.Identity, bias=vec[:, 1:2],
                                   scale=vec[:, 0:1]),
                      waits=[("dv", S.v("dv"))], inc="ac")
            d = S.op("vector", lambda v, o=fv: v.tensor_tensor(o, o, gam[:, :],
                                                               ALU.mult),
                     waits=[("ac", a2)] + ([lg] if i == 0 else []), inc="dv")
            d = S.op("vector", lambda v, o=fv: v.tensor_tensor(o, o, bet[:, :],
                                                               ALU.add),
                     waits=[("dv", d)] + ([lb] if i == 0 else []), inc="dv")
            d = S.op("vector", lambda v, o=xv, f_=fv:
                     v.tensor_tensor(o, o, f_, ALU.add),
                     waits=[("dv", d), l2], inc="dv")
            dvh.append(d)
            sth[b] = S.dma("gpsimd", x_ap[i * 128:(i + 1) * 128, :], xv,
                           [("dv", d)], ch=f"l2fs{b}")
        S.flush()


def emit_l3a(st, x_ap, ln_dump):
    S, sb = st.S, st.sb
    fre_d = st.din("t_fre", (2048, 2176), BF16)
    fim_d = st.din("t_fim", (2048, 2176), BF16)
    gre_d = st.din("t_gre", (2176, 2048), BF16)
    gim_d = st.din("t_gim", (2176, 2048), BF16)
    pe_d = st.din("c_posT", (64, 2048))
    w1_d = st.din("w_a3_w1", (64, 1024))
    w2_d = st.din("w_a3_w2", (1024, 1024))
    b1_d = st.din("c_a3_b1", (1, 1024))
    b2_d = st.din("c_a3_b2", (1, 1024))
    on_d = st.din("c_ones_f32", (1, 128))
    ident = sb["ident"]
    phre_d = st.dint("phi_re", (2176, 1024), BF16)
    phim_d = st.dint("phi_im", (2176, 1024), BF16)
    KC = 17
    with contextlib.ExitStack() as ctx:
        pef = st.alloc(ctx, "a3_pef", [64, 2048], F32)
        pe16 = st.alloc(ctx, "a3_pe16", [64, 2048], BF16)
        w1f = st.alloc(ctx, "a3_w1f", [64, 1024], F32)
        w116 = st.alloc(ctx, "a3_w116", [64, 1024], BF16)
        w2f = st.alloc(ctx, "a3_w2f", [128, 2 * 1024], F32)
        w216 = st.alloc(ctx, "a3_w216", [128, 8 * 1024], BF16)
        b1s = st.alloc(ctx, "a3_b1", [1, 1024], F32)
        b2s = st.alloc(ctx, "a3_b2", [1, 1024], F32)
        ones = st.alloc(ctx, "a3_on", [1, 128], F32)
        h116 = st.alloc(ctx, "a3_h116", [128, 2 * 1024], BF16)
        h1T = st.alloc(ctx, "a3_h1T", [128, 8 * 2048], BF16)
        filt = st.alloc(ctx, "a3_filt", [128, 16 * 1024], BF16)
        fsb = st.alloc(ctx, "a3_fsb", [128, 2 * 2048], BF16)
        stg = st.alloc(ctx, "a3_stg", [128, 2 * 1024], BF16)
        psA = st.palloc(ctx, "a3_pA", [128, 2 * 512])
        psB = st.palloc(ctx, "a3_pB", [128, 2 * 1024])
        pst = st.palloc(ctx, "a3_pt", [128, 2048], BF16)

        l1 = S.dma("sync", pef[:, :], pe_d.ap(), ch="a3pe")
        l2 = S.dma("sync", w1f[:, :], w1_d.ap(), ch="a3w1")
        l3 = S.dma("sync", b1s[:, :], b1_d.ap(), ch="a3b1")
        l4 = S.dma("sync", b2s[:, :], b2_d.ap(), ch="a3b2")
        l5 = S.dma("sync", ones[:, :], on_d.ap(), ch="a3on")
        c1 = S.op("vector", lambda v: v.tensor_copy(pe16[:, :], pef[:, :]),
                  waits=[l1], inc="dv")
        c2 = S.op("vector", lambda v: v.tensor_copy(w116[:, :], w1f[:, :]),
                  waits=[l2], inc="dv")
        cw2 = []
        for c in range(8):
            cb = c % 2
            lw = S.dma("sync", w2f[:, cb * 1024:(cb + 1) * 1024],
                       w2_d.ap()[c * 128:(c + 1) * 128, :],
                       [("dv", cw2[-2])] if len(cw2) >= 2 else [], ch=f"a3w2{cb}")
            cw2.append(S.op("vector", lambda v,
                            o=w216[:, c * 1024:(c + 1) * 1024],
                            i_=w2f[:, cb * 1024:(cb + 1) * 1024]:
                            v.tensor_copy(o, i_), waits=[lw], inc="dv"))
        trh = []
        geh = []
        for tt in range(16):
            b = tt % 2
            hv = h116[:, b * 1024:(b + 1) * 1024]
            for n in range(2):
                ps = psA[:, n * 512:(n + 1) * 512]
                w = [("dv", c1), ("dv", c2), l3, l5] if tt == 0 and n == 0 else []
                if len(geh) >= 2:
                    w.append(("ac", geh[-2]))
                S.op("tensor", lambda t, o=ps,
                     l=pe16[:, tt * 128:(tt + 1) * 128],
                     r=w116[:, n * 512:(n + 1) * 512]:
                     t.matmul(o, l, r, start=True, stop=False),
                     waits=w, inc=None)
                pv = S.op("tensor", lambda t, o=ps, l=ones[:, :],
                          r=b1s[:, n * 512:(n + 1) * 512]:
                          t.matmul(o, l, r, start=False, stop=True), inc="pe")
                geh.append(S.op("scalar", lambda a,
                                o=hv[:, n * 512:(n + 1) * 512], i_=ps:
                                a.activation(o, i_, AF.Gelu),
                                waits=[("pe", pv)], inc="ac"))
            for c in range(DCH):
                w2 = [("ac", geh[-1])] if c == 0 else []
                if len(trh) >= 2:
                    w2.append(("dv", trh[-2]))
                slot = (len(trh) % 2) * 1024
                pv = S.op("tensor", lambda t, s_=hv[:, c * 128:(c + 1) * 128],
                          o=pst[:, slot:slot + 128]:
                          t.transpose(o, s_, ident[:, :]), waits=w2, inc="pe")
                trh.append(S.op("vector", lambda v, s_=pst[:, slot:slot + 128],
                                o=h1T[:, c * 2048 + tt * 128:
                                      c * 2048 + (tt + 1) * 128]:
                                v.tensor_copy(o, s_), waits=[("pe", pv)],
                                inc="dv"))
        fch = []
        for tt in range(16):
            for n in range(2):
                ps = psA[:, n * 512:(n + 1) * 512]
                w = [("dv", trh[-1]), ("dv", cw2[-1]), l4] if tt == 0 and n == 0 else []
                if len(fch) >= 2:
                    w.append(("ac", fch[-2]))
                pv = None
                for c in range(DCH):
                    pv = S.op("tensor", lambda t, o=ps,
                              l=h1T[:, c * 2048 + tt * 128: c * 2048 + (tt + 1) * 128],
                              r=w216[:, c * 1024 + n * 512: c * 1024 + n * 512 + 512],
                              st_=(c == 0), sp=False:
                              t.matmul(o, l, r, start=st_, stop=sp),
                              waits=(w if c == 0 else []), inc=None)
                pv = S.op("tensor", lambda t, o=ps, l=ones[:, :],
                          r=b2s[:, n * 512:(n + 1) * 512]:
                          t.matmul(o, l, r, start=False, stop=True), inc="pe")
                fch.append(S.op("scalar", lambda a,
                                o=filt[:, tt * 1024 + n * 512:
                                       tt * 1024 + (n + 1) * 512],
                                i_=ps: a.activation(o, i_, AF.Copy, bias=0.0,
                                                    scale=1.0),
                                waits=[("pe", pv)], inc="ac"))
        sth = []
        for kc in range(KC):
            kb = kc % 2
            lf1 = S.dma("sync", fsb[:, 0:2048],
                        bass.AP(fre_d.ap().tensor, kc * 128,
                                [[2176, 128], [278528, 16], [1, 128]]),
                        [("pe", S.v("pe"))], ch="a3f1")
            lf2 = S.dma("sync", fsb[:, 2048:4096],
                        bass.AP(fim_d.ap().tensor, kc * 128,
                                [[2176, 128], [278528, 16], [1, 128]]),
                        ch="a3f2")
            for comp, lf, pho in ((0, lf1, phre_d), (1, lf2, phim_d)):
                for n in range(2):
                    ps = psB[:, comp * 1024 + n * 512: comp * 1024 + (n + 1) * 512]
                    w = [lf, ("ac", fch[-1])] if n == 0 else [lf]
                    if sth:
                        w.append(sth[-1])
                    pv = None
                    for tc in range(16):
                        pv = S.op("tensor", lambda t, o=ps,
                                  l=fsb[:, comp * 2048 + tc * 128:
                                        comp * 2048 + (tc + 1) * 128],
                                  r=filt[:, tc * 1024 + n * 512:
                                         tc * 1024 + (n + 1) * 512],
                                  st_=(tc == 0), sp=(tc == 15):
                                  t.matmul(o, l, r, start=st_, stop=sp),
                                  waits=(w if tc == 0 else []),
                                  inc=("pe" if tc == 15 else None))
                    slot4 = (len(sth) % 4) * 512
                    wcc = [("pe", pv)]
                    if len(sth) >= 4:
                        wcc.append(sth[-4])
                    cc = S.op("scalar", lambda a,
                              o=stg[:, slot4:slot4 + 512],
                              i_=ps: a.activation(o, i_, AF.Copy, bias=0.0,
                                                  scale=1.0),
                              waits=wcc, inc="ac")
                    sth.append(S.dma("gpsimd",
                                     pho.ap()[kc * 128:(kc + 1) * 128,
                                              n * 512:(n + 1) * 512],
                                     stg[:, slot4:slot4 + 512],
                                     [("ac", cc)], ch=f"a3ps{len(sth) % 2}"))
        S.flush()
    with contextlib.ExitStack() as ctx:
        lnsb = st.alloc(ctx, "a3_ln", [128, 16 * 1024], BF16)
        fsb = st.alloc(ctx, "a3_cf", [128, 2 * 2048], BF16)
        phsb = st.alloc(ctx, "a3_ph", [128, 2 * 1024], BF16)
        yre = st.alloc(ctx, "a3_yre", [128, 17 * 1024], BF16)
        yim = st.alloc(ctx, "a3_yim", [128, 17 * 1024], BF16)
        gsb = st.alloc(ctx, "a3_g", [128, 2 * 2176], BF16)
        sc1 = st.alloc(ctx, "a3_s1", [128, 1024], F32)
        sc2 = st.alloc(ctx, "a3_s2", [128, 1024], F32)
        xe = st.alloc(ctx, "a3_xe", [128, 2 * 1024], F32)
        psX = st.palloc(ctx, "a3_pX", [128, 2 * 1024])
        psO = st.palloc(ctx, "a3_pO", [128, 2 * 512])

        stl = {0: None, 1: None}
        evl = []
        for s in range(SEQ_PER_CORE):
            lnl = S.dma("sync",
                        bass.AP(lnsb[:, :].tensor, 0,
                                [[16384, 128], [1024, 16], [1, 1024]]),
                        bass.AP(ln_dump.tensor, ln_dump.offset + s * 2048 * 1024,
                                [[1024, 128], [131072, 16], [1, 1024]]),
                        [("pe", S.v("pe"))], ch="a3ln")
            ych = []
            for kc in range(KC):
                lf1 = S.dma("sync", fsb[:, 0:2048],
                            bass.AP(fre_d.ap().tensor, kc * 128,
                                    [[2176, 128], [278528, 16], [1, 128]]),
                            [("pe", S.v("pe"))], ch="a3c1")
                lf2 = S.dma("sync", fsb[:, 2048:4096],
                            bass.AP(fim_d.ap().tensor, kc * 128,
                                    [[2176, 128], [278528, 16], [1, 128]]),
                            ch="a3c2")
                lp1 = S.dma("sync", phsb[:, 0:1024],
                            phre_d.ap()[kc * 128:(kc + 1) * 128, :],
                            ([("dv", ych[-1])] if ych else []), ch="a3p1")
                lp2 = S.dma("sync", phsb[:, 1024:2048],
                            phim_d.ap()[kc * 128:(kc + 1) * 128, :], ch="a3p2")
                pvs = []
                for comp, lf in ((0, lf1), (1, lf2)):
                    for n in range(2):
                        ps = psX[:, comp * 1024 + n * 512:
                                 comp * 1024 + (n + 1) * 512]
                        w = [lf] + ([lnl] if kc == 0 and comp == 0 and n == 0
                                    else [])
                        if ych:
                            w.append(("dv", ych[-1]))
                        pv = None
                        for tc in range(16):
                            pv = S.op("tensor", lambda t, o=ps,
                                      l=fsb[:, comp * 2048 + tc * 128:
                                            comp * 2048 + (tc + 1) * 128],
                                      r=lnsb[:, tc * 1024 + n * 512:
                                             tc * 1024 + (n + 1) * 512],
                                      st_=(tc == 0), sp=(tc == 15):
                                      t.matmul(o, l, r, start=st_, stop=sp),
                                      waits=(w if tc == 0 else []),
                                      inc=("pe" if tc == 15 else None))
                        pvs.append(pv)
                d1 = S.op("vector", lambda v, o=sc1[:, :],
                          a=psX[:, 0:1024], b=phsb[:, 0:1024]:
                          v.tensor_tensor(o, a, b, ALU.mult),
                          waits=[("pe", pvs[-1]), lp1], inc="dv")
                d2 = S.op("vector", lambda v, o=sc2[:, :],
                          a=psX[:, 1024:2048], b=phsb[:, 1024:2048]:
                          v.tensor_tensor(o, a, b, ALU.mult),
                          waits=[("dv", d1), lp2], inc="dv")
                d3 = S.op("vector", lambda v, o=yre[:, kc * 1024:(kc + 1) * 1024],
                          a=sc1[:, :], b=sc2[:, :]:
                          v.tensor_tensor(o, a, b, ALU.subtract),
                          waits=[("dv", d2)], inc="dv")
                d4 = S.op("vector", lambda v, o=sc1[:, :],
                          a=psX[:, 0:1024], b=phsb[:, 1024:2048]:
                          v.tensor_tensor(o, a, b, ALU.mult),
                          waits=[("dv", d3)], inc="dv")
                d5 = S.op("vector", lambda v, o=sc2[:, :],
                          a=psX[:, 1024:2048], b=phsb[:, 0:1024]:
                          v.tensor_tensor(o, a, b, ALU.mult),
                          waits=[("dv", d4)], inc="dv")
                d6 = S.op("vector", lambda v, o=yim[:, kc * 1024:(kc + 1) * 1024],
                          a=sc1[:, :], b=sc2[:, :]:
                          v.tensor_tensor(o, a, b, ALU.add),
                          waits=[("dv", d5)], inc="dv")
                ych.append(d6)
            for tt in range(16):
                b = tt % 2
                lg1 = S.dma("sync", gsb[:, 0:2176],
                            bass.AP(gre_d.ap().tensor, tt * 128,
                                    [[2048, 128], [262144, 17], [1, 128]]),
                            [("pe", S.v("pe"))], ch="a3g1")
                lg2 = S.dma("sync", gsb[:, 2176:4352],
                            bass.AP(gim_d.ap().tensor, tt * 128,
                                    [[2048, 128], [262144, 17], [1, 128]]),
                            ch="a3g2")
                xev = xe[:, b * 1024:(b + 1) * 1024]
                wv = []
                if stl[b] is not None:
                    wv.append(stl[b])
                lx = S.dma("scalar", xev,
                           x_ap[s * 2048 + tt * 128: s * 2048 + (tt + 1) * 128, :],
                           wv, ch=f"a3xl{b}")
                evs = []
                for n in range(2):
                    ps = psO[:, n * 512:(n + 1) * 512]
                    w = [lg1, lg2]
                    if tt == 0 and n == 0:
                        w.append(("dv", ych[-1]))
                    if evl:
                        w.append(("dv", evl[-1][0]))
                    pv = None
                    for kc in range(KC):
                        pv = S.op("tensor", lambda t, o=ps,
                                  l=gsb[:, kc * 128:(kc + 1) * 128],
                                  r=yre[:, kc * 1024 + n * 512:
                                         kc * 1024 + (n + 1) * 512],
                                  st_=(kc == 0), sp=False:
                                  t.matmul(o, l, r, start=st_, stop=sp),
                                  waits=(w if kc == 0 else []), inc=None)
                        last = (kc == KC - 1)
                        pv = S.op("tensor", lambda t, o=ps,
                                  l=gsb[:, 2176 + kc * 128:
                                        2176 + (kc + 1) * 128],
                                  r=yim[:, kc * 1024 + n * 512:
                                         kc * 1024 + (n + 1) * 512],
                                  st_=False, sp=last:
                                  t.matmul(o, l, r, start=st_, stop=sp),
                                  inc=("pe" if last else None))
                    ev = S.op("vector", lambda v, o=xev[:, n * 512:(n + 1) * 512],
                              p_=ps: v.tensor_tensor(o, o, p_, ALU.add),
                              waits=[("pe", pv), lx], inc="dv")
                    evs.append(ev)
                evl.append((evs[-1], S.v("pe")))
                stl[b] = S.dma("gpsimd",
                               x_ap[s * 2048 + tt * 128: s * 2048 + (tt + 1) * 128, :],
                               xev, [("dv", evs[-1])], ch=f"a3xs{b}")
        S.flush()


def emit_moe_cache(st, srcs, caches, pieces, nexp, ffn, dscale=0.5, tag="wc"):
    S, sb = st.S, st.sb
    NP = len(pieces)
    NCH = max(pieces)
    wg = st.din(srcs[0], (nexp * 1024, ffn))
    wu = st.din(srcs[1], (nexp * 1024, ffn))
    wd = st.din(srcs[2], (nexp * ffn, 1024))
    cg = st.dint(caches[0], (nexp * NP * 128, NCH * 1024), BF16)
    cu = st.dint(caches[1], (nexp * NP * 128, NCH * 1024), BF16)
    cd = st.dint(caches[2], (nexp * NP * 128, NCH * 1024), BF16)
    offs = [sum(pieces[:p]) for p in range(NP)]
    with contextlib.ExitStack() as ctx:
        stg = st.alloc(ctx, tag + "_stage", [128, 2 * 1024], F32)
        img = st.alloc(ctx, tag + "_img", [128, 2 * NCH * 1024], BF16)
        cast_ac, out_gd = [], []
        kidx = 0
        li = 0
        for e in range(nexp):
            for p in range(NP):
                j = e * NP + p
                NF = pieces[p]
                off = offs[p]
                for kind, wsrc, cdst, scale in ((0, wg, cg, 1.0), (1, wu, cu, 1.0),
                                                (2, wd, cd, dscale)):
                    kb = kidx % 2
                    imgv = img[:, kb * NCH * 1024:(kb + 1) * NCH * 1024]
                    ncold = 8 if kind < 2 else NF
                    for c in range(ncold):
                        cb = li % 2
                        li += 1
                        if kind < 2:
                            r0 = e * 1024 + c * 128
                            src_ap = wsrc.ap()[r0:r0 + 128,
                                               off * 128:(off + NF) * 128]
                            dst_ap = imgv[:, c * NF * 128:(c + 1) * NF * 128]
                            ww = NF * 128
                        else:
                            r0 = e * ffn + (off + c) * 128
                            src_ap = wsrc.ap()[r0:r0 + 128, :]
                            dst_ap = imgv[:, c * 1024:(c + 1) * 1024]
                            ww = 1024
                        stgv = stg[:, cb * 1024:cb * 1024 + ww]
                        w = []
                        if len(cast_ac) >= 2:
                            w.append(("ac", cast_ac[-2]))
                        ld = S.dma("sync", stgv, src_ap, w, ch=tag + f"l{cb}")
                        w2 = [ld]
                        if len(out_gd) >= 2:
                            w2.append(out_gd[-2])
                        cast_ac.append(S.op(
                            "scalar", lambda a, i_=stgv, o=dst_ap, sc=scale:
                            a.activation(o, i_, AF.Copy, bias=0.0, scale=sc),
                            waits=w2, inc="ac"))
                    wout = NF * 1024
                    out_gd.append(S.dma("gpsimd",
                                        cdst.ap()[j * 128:(j + 1) * 128, :wout],
                                        imgv[:, :wout], [("ac", cast_ac[-1])],
                                        ch=tag + f"s{kidx % 2}"))
                    kidx += 1
        done = [out_gd[-1], out_gd[-2]]
        S.flush()
    return done


def emit_l2_router(st, comb_sb, ah_sb, rw16):
    S, sb = st.S, st.sb
    xT = sb["xT"]
    with contextlib.ExitStack() as ctx:
        el = st.alloc(ctx, "rt_el", [128, 2 * 8], F32)
        mk = st.alloc(ctx, "rt_mk", [128, 2 * 8], F32)
        vv = st.alloc(ctx, "rt_v", [128, 2 * 8], F32)
        psl = st.palloc(ctx, "rt_ps", [128, 1024])
        done_hist = []
        for i in range(NTILE):
            b = i % 2
            ps = psl[:, b * 512:b * 512 + 8]
            elv = el[:, b * 8:(b + 1) * 8]
            mkv = mk[:, b * 8:(b + 1) * 8]
            v0 = vv[:, b * 8:(b + 1) * 8]
            w0 = [("dv", done_hist[-2])] if len(done_hist) >= 2 else []
            pv = None
            for c in range(DCH):
                pv = S.op("tensor", lambda t, o=ps,
                          l=xT[:, c * 4096 + i * 128: c * 4096 + (i + 1) * 128],
                          r=rw16[:, c * 8:(c + 1) * 8],
                          st_=(c == 0), sp=(c == DCH - 1):
                          t.matmul(o, l, r, start=st_, stop=sp),
                          waits=(w0 if c == 0 else []),
                          inc=("pe" if c == DCH - 1 else None))
            def vop(fn, waits, inc="dv"):
                return S.op("vector", fn, waits=waits, inc=inc)
            d = vop(lambda v, o=v0[:, 0:1], p_=ps:
                    v.reduce_max(o, p_, AX.X, negate=True), [("pe", pv)])
            a1 = S.op("scalar", lambda a, o=elv, p_=ps, b_=v0[:, 0:1],
                      s_=v0[:, 1:2]:
                      a.activation(o, p_, AF.Exp, bias=b_, accum_out=s_),
                      waits=[("dv", d)], inc="ac")
            d = vop(lambda v, o=v0[:, 2:3], i_=v0[:, 1:2]:
                    v.reciprocal(o, i_), [("ac", a1)])
            d = vop(lambda v, o=elv, s_=v0[:, 2:3]:
                    v.tensor_scalar(o, o, s_, None, ALU.mult), [("dv", d)])
            d = vop(lambda v, o=v0[:, 3:4], i_=elv:
                    v.reduce_max(o, i_, AX.X), [("dv", d)])
            d = vop(lambda v, o=mkv, i_=elv, m_=v0[:, 3:4]:
                    v.tensor_scalar(o, i_, m_, None, ALU.is_ge), [("dv", d)])
            d = vop(lambda v, o=mkv, i_=elv:
                    v.scalar_tensor_tensor(o, o, -1e3, i_, ALU.mult, ALU.add),
                    [("dv", d)])
            d = vop(lambda v, o=v0[:, 4:5], i_=mkv:
                    v.reduce_max(o, i_, AX.X), [("dv", d)])
            d = vop(lambda v, o=v0[:, 5:6], a_=v0[:, 3:4], b_=v0[:, 4:5]:
                    v.tensor_tensor(o, a_, b_, ALU.add), [("dv", d)])
            d = vop(lambda v, o=v0[:, 6:7], i_=v0[:, 5:6]:
                    v.reciprocal(o, i_), [("dv", d)])
            d = vop(lambda v, o=mkv, i_=elv, m_=v0[:, 4:5]:
                    v.tensor_scalar(o, i_, m_, None, ALU.is_ge), [("dv", d)])
            d = vop(lambda v, o=mkv, i_=elv:
                    v.tensor_tensor(o, o, i_, ALU.mult), [("dv", d)])
            d = vop(lambda v, o=comb_sb[:, i * 8:(i + 1) * 8], i_=mkv,
                    r_=v0[:, 6:7], a_=ah_sb[:, 0:1]:
                    v.tensor_scalar(o, i_, r_, a_, ALU.mult, ALU.mult),
                    [("dv", d)])
            done_hist.append(d)
        S.flush()


def emit_moe_ffn(st, x_ap, ln_loads, caches, pieces, nexp=8, combine=None,
                 cache_gd=0, tag="f1"):
    S, sb = st.S, st.sb
    xT = sb["xT"]
    cg, cu, cd = (st.dram[n] for n in caches)
    NP = len(pieces)
    NCH = max(pieces)
    with contextlib.ExitStack() as ctx:
        wgs = st.alloc(ctx, tag + "_wg", [128, 2 * NCH * 1024], BF16)
        wus = st.alloc(ctx, tag + "_wu", [128, 2 * NCH * 1024], BF16)
        wds = st.alloc(ctx, tag + "_wd", [128, 2 * NCH * 1024], BF16)
        hsb = st.alloc(ctx, tag + "_h", [128, NCH * 512], BF16)
        sg = st.alloc(ctx, tag + "_sg", [128, 2 * 512], BF16)
        xe = st.alloc(ctx, tag + "_xe", [128, 2 * 1024], F32)
        psg = st.palloc(ctx, tag + "_pg", [128, 2 * 512])
        psu = st.palloc(ctx, tag + "_pu", [128, 2 * 512])
        psy = st.palloc(ctx, tag + "_py", [128, 2 * 512])

        pe_done, gd_out = {}, {}
        last_buf_store = {0: None, 1: None}
        silu_ac, hmul_dv, yevac_dv = [], [], []
        last_gd = None
        for e in range(nexp):
            for p in range(NP):
                j = e * NP + p
                NF = pieces[p]
                jb = j % 2
                wgv = wgs[:, jb * NCH * 1024:jb * NCH * 1024 + NF * 1024]
                wuv = wus[:, jb * NCH * 1024:jb * NCH * 1024 + NF * 1024]
                wdv = wds[:, jb * NCH * 1024:jb * NCH * 1024 + NF * 1024]
                w = list(cache_gd) if (j == 0 and cache_gd) else []
                if j >= 2:
                    w.append(("pe", pe_done[j - 2]))
                lg = S.dma("sync", wgv, cg.ap()[j * 128:(j + 1) * 128, :NF * 1024],
                           w, ch=tag + f"wg{jb}")
                lu = S.dma("sync", wuv, cu.ap()[j * 128:(j + 1) * 128, :NF * 1024],
                           ch=tag + f"wu{jb}")
                ld = S.dma("sync", wdv, cd.ap()[j * 128:(j + 1) * 128, :NF * 1024],
                           ch=tag + f"wd{jb}")
                for s in range(8):
                    for f in range(NF):
                        fb = f % 2
                        gps = psg[:, fb * 512:(fb + 1) * 512]
                        ups = psu[:, fb * 512:(fb + 1) * 512]
                        w_g = [lg] if (s == 0 and f == 0) else []
                        if len(silu_ac) >= 2:
                            w_g.append(("ac", silu_ac[-2]))
                        w_u = [lu] if (s == 0 and f == 0) else []
                        if len(hmul_dv) >= 2:
                            w_u.append(("dv", hmul_dv[-2]))
                        peg = peu = None
                        for c in range(DCH):
                            peg = S.op(
                                "tensor", lambda t, o=gps,
                                l=wgv[:, c * NF * 128 + f * 128: c * NF * 128 + (f + 1) * 128],
                                r=xT[:, c * 4096 + s * 512: c * 4096 + (s + 1) * 512],
                                st_=(c == 0), sp=(c == DCH - 1):
                                t.matmul(o, l, r, start=st_, stop=sp),
                                waits=(w_g if c == 0 else []),
                                inc=("pe" if c == DCH - 1 else None))
                        for c in range(DCH):
                            peu = S.op(
                                "tensor", lambda t, o=ups,
                                l=wuv[:, c * NF * 128 + f * 128: c * NF * 128 + (f + 1) * 128],
                                r=xT[:, c * 4096 + s * 512: c * 4096 + (s + 1) * 512],
                                st_=(c == 0), sp=(c == DCH - 1):
                                t.matmul(o, l, r, start=st_, stop=sp),
                                waits=(w_u if c == 0 else []),
                                inc=("pe" if c == DCH - 1 else None))
                        sgv = sg[:, fb * 512:(fb + 1) * 512]
                        w_s = [("pe", peg)]
                        if len(hmul_dv) >= 2:
                            w_s.append(("dv", hmul_dv[-2]))
                        silu_ac.append(S.op("scalar", lambda a, o=sgv, i_=gps:
                                            a.activation(o, i_, AF.Silu),
                                            waits=w_s, inc="ac"))
                        hmul_dv.append(S.op(
                            "vector", lambda v, o=hsb[:, f * 512:(f + 1) * 512],
                            a=sgv, b=ups: v.tensor_tensor(o, a, b, ALU.mult),
                            waits=[("ac", silu_ac[-1]), ("pe", peu)], inc="dv"))
                    h_dv = S.v("dv")
                    for m in range(4):
                        idx = s * 4 + m
                        rows = slice(idx * 128, (idx + 1) * 128)
                        xb = idx % 2
                        xev = xe[:, xb * 1024:(xb + 1) * 1024]
                        wx = []
                        if (j - 1, idx) in gd_out:
                            wx.append(gd_out[(j - 1, idx)])
                        if last_buf_store[xb] is not None:
                            wx.append(last_buf_store[xb])
                        lx = S.dma("scalar", xev, x_ap[rows, :], wx,
                                   ch=tag + f"xl{idx % 2}")
                        evs = []
                        for n in range(2):
                            yb = (m * 2 + n) % 2
                            yps = psy[:, yb * 512:(yb + 1) * 512]
                            wy = []
                            if m == 0 and n == 0:
                                wy.append(("dv", h_dv))
                                if s == 0:
                                    wy.append(ld)
                            if len(yevac_dv) >= 2:
                                wy.append(("dv", yevac_dv[-2]))
                            pey = None
                            for f in range(NF):
                                pey = S.op(
                                    "tensor", lambda t, o=yps,
                                    l=hsb[:, f * 512 + m * 128: f * 512 + (m + 1) * 128],
                                    r=wdv[:, f * 1024 + n * 512: f * 1024 + n * 512 + 512],
                                    st_=(f == 0), sp=(f == NF - 1):
                                    t.matmul(o, l, r, start=st_, stop=sp),
                                    waits=(wy if f == 0 else []),
                                    inc=("pe" if f == NF - 1 else None))
                            if combine is None:
                                yevac_dv.append(S.op(
                                    "vector",
                                    lambda v, o=xev[:, n * 512:(n + 1) * 512], p_=yps:
                                    v.tensor_tensor(o, o, p_, ALU.add),
                                    waits=[("pe", pey), lx], inc="dv"))
                            else:
                                comb_sb, stride = combine
                                sc = comb_sb[:, idx * stride + e: idx * stride + e + 1]
                                yevac_dv.append(S.op(
                                    "vector",
                                    lambda v, o=xev[:, n * 512:(n + 1) * 512], p_=yps,
                                    s_=sc:
                                    v.scalar_tensor_tensor(o, p_, s_, o,
                                                           ALU.mult, ALU.add),
                                    waits=[("pe", pey), lx], inc="dv"))
                            evs.append(yevac_dv[-1])
                        wout = [("dv", evs[-1])]
                        if j == 0:
                            wout += ln_loads
                        gd_out[(j, idx)] = S.dma("gpsimd", x_ap[rows, :], xev,
                                                 wout, ch=tag + f"xs{idx % 2}")
                        last_buf_store[xb] = gd_out[(j, idx)]
                        last_gd = gd_out[(j, idx)]
                pe_done[j] = S.v("pe")
        S.flush()
    return last_gd


def build_stage(subblocks):
    st = Stage(subblocks)
    S = st.S
    xin = st.din("xin", (NT, D))
    xout = st.dout("xout", (NT, D))
    xres = st.dint("xres", (NT, D))
    with contextlib.ExitStack() as ctx:
        S.make_sems(ctx)
        st.alloc(ctx, "xT", [128, 8 * 4096], BF16)
        st.alloc(ctx, "ident", [128, 128], BF16)
        ide = st.din("c_ident16", (128, 128), BF16)
        S.dma("sync", st.sb["ident"][:, :], ide.ap(), ch="misc")
        st.x_ready = [S.dma("sync", xres.ap(), xin.ap(), ch="xcp")]
        S.flush(drain_dma=True)

        need_l1 = any(s == "L1f" for s in subblocks)
        need_l2 = any(s == "L2f" for s in subblocks)
        need_l0 = any(s == "L0f" for s in subblocks)
        need_l3 = any(s == "L3f" for s in subblocks)
        cache_gd = cache_gd2 = cache_gd0 = cache_gd3 = 0
        acg = {}
        if any(s == "L0a" for s in subblocks):
            acg["a0q"] = emit_plain_cache(st, "w_a0_qkv", "c_a0_qkv",
                                          1024, 3072, "ca0q")
            acg["a0o"] = emit_plain_cache(st, "w_a0_wo", "c_a0_wo",
                                          1024, 1024, "ca0o")
        if any(s == "L1a" for s in subblocks):
            acg["a1q"] = emit_plain_cache(st, "w_a1_qkv", "c_a1_qkv",
                                          1024, 3072, "ca1q")
            acg["a1o"] = emit_plain_cache(st, "w_a1_wo", "c_a1_wo",
                                          1024, 1024, "ca1o")
        if any(s == "L2a" for s in subblocks):
            g1 = emit_plain_cache(st, "w_a2_plT", "c_a2_plT", 1024, 1024, "cpl")
            g2 = emit_plain_cache(st, "w_a2_wlT", "c_a2_wlT", 1024, 3072, "cwl")
            acg["a2lq"] = emit_compose_cache(st, "c_a2_plT", "c_a2_wlT",
                                             "c_a2l_qkv", g1 + g2, "cmpl")
            g3 = emit_plain_cache(st, "w_a2_pgT", "c_a2_pgT", 1024, 1024, "cpg")
            g4 = emit_plain_cache(st, "w_a2_wgT", "c_a2_wgT", 1024, 3072, "cwg")
            acg["a2gq"] = emit_compose_cache(st, "c_a2_pgT", "c_a2_wgT",
                                             "c_a2g_qkv", g3 + g4, "cmpg")
            acg["a2lo"] = emit_plain_cache(st, "w_a2_wlo", "c_a2l_wo",
                                           1024, 1024, "clo")
            acg["a2go"] = emit_plain_cache(st, "w_a2_wgo", "c_a2g_wo",
                                           1024, 1024, "cgo")
        if need_l0:
            cache_gd0 = emit_l0_cache(st)
        if need_l3:
            cache_gd3 = emit_l3_cache(st)
        if need_l1:
            cache_gd = emit_moe_cache(st, ("w_l1_g", "w_l1_u", "w_l1_d"),
                                      ("c_l1_g", "c_l1_u", "c_l1_d"),
                                      [8, 8, 8, 8], 8, 4096, tag="wc1")
        if need_l2:
            cache_gd2 = emit_moe_cache(st, ("w_l2_g", "w_l2_u", "w_l2_d"),
                                       ("c_l2_g", "c_l2_u", "c_l2_d"),
                                       [6, 6, 5, 5], 8, 2816, tag="wc2")

        for name in subblocks:
            if name == "L3a":
                lnd = st.dint("ln16_s", (NT, 1024), BF16)
                emit_ln_to_xT(st, xres.ap(), mode="dump", wb=lnd.ap())
                emit_l3a(st, xres.ap(), lnd.ap())
            elif name == "L2a":
                qkvd = st.dint("qkv_s", (NT, 3072), BF16)
                fused = st.dint("fused_l2", (NT, D))
                with contextlib.ExitStack() as c3:
                    st.alloc(c3, "l2a_w", [128, 2], F32)
                    st.alloc(c3, "l2a_blo", [128, 1024], F32)
                    st.alloc(c3, "l2a_bgo", [128, 1024], F32)
                    emit_l2_fusionw(st)
                    emit_ln_to_xT(st, xres.ap())
                    emit_attn01_qkv(st, "l2", qkvd.ap(), "c_a2l_qkv",
                                    cache_gd=acg["a2lq"],
                                    bias_name="c_l2_blin", tag="a2l")
                    emit_attn2_core(st, qkvd.ap(), 128, fused.ap(),
                                    "c_a2l_wo", True, 0)
                    emit_attn01_qkv(st, "l2", qkvd.ap(), "c_a2g_qkv",
                                    cache_gd=acg["a2gq"],
                                    bias_name="c_l2_bgin", tag="a2g")
                    emit_attn2_core(st, qkvd.ap(), 1024, fused.ap(),
                                    "c_a2g_wo", False, 1)
                    emit_l2a_final(st, xres.ap(), fused.ap())
            elif name in ("L0a", "L1a"):
                var = "l0" if name == "L0a" else "l1"
                qkvd = st.dint("qkv_s", (NT, 3072), BF16)
                emit_ln_to_xT(st, xres.ap())
                emit_attn01_qkv(st, var, qkvd.ap(),
                                "c_a%s_qkv" % var[1],
                                cache_gd=acg.get("a%sq" % var[1], 0))
                emit_attn01_core(st, var, qkvd.ap(), xres.ap(),
                                 "c_a%s_wo" % var[1])
            elif name == "L3f":
                hbuf = st.dint("h_l3", (NT, D))
                emit_ln_to_xT(st, xres.ap(), mode="l3", wb=hbuf.ap())
                emit_l3_ffn(st, xres.ap(), hbuf.ap(), cache_gd=cache_gd3)
            elif name == "L0f":
                ln_loads = emit_ln_to_xT(st, xres.ap(), mode="l0")
                emit_l0_main(st, xres.ap(), ln_loads, cache_gd=cache_gd0)
            elif name == "L1f":
                ln_loads = emit_ln_to_xT(st, xres.ap())
                emit_moe_ffn(st, xres.ap(), ln_loads,
                             ("c_l1_g", "c_l1_u", "c_l1_d"), pieces=[8, 8, 8, 8],
                             cache_gd=cache_gd, tag="f1")
            elif name == "L2f":
                with contextlib.ExitStack() as c2:
                    comb = st.alloc(c2, "l2_comb", [128, NTILE * 8], F32)
                    ah = st.alloc(c2, "l2_ah", [128, 1], F32)
                    rw16 = st.alloc(c2, "l2_rw", [128, 64], BF16)
                    wbc = st.alloc(c2, "w_bcast", [128, 1024], BF16)
                    wbcf = st.alloc(c2, "w_bcastf", [128, 1024], F32)
                    asb = st.alloc(c2, "l2_asb", [1, 1], F32)
                    ones = st.alloc(c2, "ones_f32", [1, 128], F32)
                    rwf = st.alloc(c2, "l2_rwf", [128, 64], F32)
                    pa = st.palloc(c2, "l2_pa", [128, 512])
                    rw_d = st.din("w_l2_router", (128, 64))
                    nw_d = st.din("c_l2_normw", (128, 1024))
                    al_d = st.din("s_l2_alpha", (1, 1))
                    on_d = st.din("c_ones_f32", (1, 128))
                    l1 = S.dma("sync", rwf[:, :], rw_d.ap(), ch="l2c")
                    l2_ = S.dma("sync", wbcf[:, :], nw_d.ap(), ch="l2c2")
                    S.op("vector", lambda v: v.tensor_copy(wbc[:, :], wbcf[:, :]),
                         waits=[l2_], inc="dv")
                    l3 = S.dma("sync", asb[:, :], al_d.ap(), ch="l2c3")
                    l4 = S.dma("sync", ones[:, :], on_d.ap(), ch="l2c4")
                    S.op("vector", lambda v: v.tensor_copy(rw16[:, :], rwf[:, :]),
                         waits=[l1], inc="dv")
                    a_ = S.op("vector", lambda v: v.tensor_scalar(
                        asb[:, :], asb[:, :], -0.5, 2.0, ALU.max, ALU.min),
                        waits=[l3], inc="dv")
                    p_ = S.op("tensor", lambda t: t.matmul(
                        pa[:, 0:1], ones[:, :], asb[:, :]),
                        waits=[("dv", a_), l4], inc="pe")
                    S.op("scalar", lambda a: a.activation(
                        ah[:, :], pa[:, 0:1], AF.Copy, bias=0.0, scale=0.5),
                        waits=[("pe", p_)], inc="ac")
                    S.flush()
                    ln_loads = emit_ln_to_xT(st, xres.ap(), mode="l2",
                                             wb=xres.ap())
                    emit_l2_router(st, comb, ah, rw16)
                    emit_moe_ffn(st, xres.ap(), ln_loads,
                                 ("c_l2_g", "c_l2_u", "c_l2_d"),
                                 pieces=[6, 6, 5, 5], combine=(comb, 8),
                                 cache_gd=cache_gd2, tag="f2")
            else:
                raise NotImplementedError(name)

        allw = [(ch, S.cnt[ch]) for ch in list(S.cnt) if ch.startswith("H_")
                and S.cnt[ch] > 0]
        allw += [(s, S.v(s)) for s in ("pe", "dv", "ac") if S.v(s) > 0]
        last = S.dma("sync", xout.ap(), xres.ap(), allw, ch="xcp")
        S.op("sync", lambda e: e.nop(), waits=[last])
        S.flush()
    return st


ALL = ["L0f", "L0a", "L0f", "L1f", "L1a", "L1f", "L2f", "L2a", "L2f", "L3f", "L3a", "L3f"]
STAGES = [ALL]


def kernel(**inputs):
    need = set(s for stage in STAGES for s in stage)
    shards = shard_x(inputs)
    host = host_prep(inputs, need)
    x_cur = shards
    for subblocks in STAGES:
        st = build_stage(subblocks)
        in_maps = []
        for c in range(NCORES):
            m = {"xin": x_cur[c]}
            for nm in st.ext_inputs:
                if nm != "xin":
                    m[nm] = host[nm]
            in_maps.append(m)
        res = run_bass_kernel_spmd(st.nc, in_maps, core_ids=list(range(NCORES)))
        x_cur = [np.asarray(res.results[c]["xout"]) for c in range(NCORES)]
    out = np.concatenate([np.asarray(x, np.float32).reshape(SEQ_PER_CORE, T, D)
                          for x in x_cur], axis=0)
    return np.ascontiguousarray(out)
